# version 14
# baseline (speedup 1.0000x reference)
import math
import numpy as np
import concourse.bass as bass
import concourse.mybir as mybir
from concourse.bass_utils import run_bass_kernel_spmd

F32 = mybir.dt.float32
BF16 = mybir.dt.bfloat16
AF = mybir.ActivationFunctionType
ALU = mybir.AluOpType
AX = mybir.AxisListType

E = 1024
H = 16
WS = 512
NOWN = 4
NLW = 6
TLOC = NLW * WS
NCH = TLOC // 128
TOWN = NOWN * WS
LN512 = float(math.log(512.0))
C2 = 30.0
HFF = 4096
SPAN = 256
NSPAN = TOWN // SPAN
GROUPS = [(0, 3), (3, 6), (6, 9), (9, 12)]


class Plan:
    def __init__(self, cnt=None):
        self.ops = []
        self.cnt = dict(cnt) if cnt else {"T": 0, "V": 0, "S": 0, "G": 0, "dY": 0, "dG": 0}
        self.cover = {}
        self.spacers = {}

    SCALE = 256

    def op(self, eng, fn):
        self.cnt[eng] += self.SCALE
        tgt = self.cnt[eng]

        def run(e, sems, fn=fn, eng=eng):
            fn(e, sems).then_inc(sems[eng], self.SCALE)
        self.ops.append((eng, run))
        return (eng, tgt)

    def dma(self, eng, fn):
        sem = {"Y": "dY", "G": "dG"}[eng]
        self.cnt[sem] += 16
        tgt = self.cnt[sem]

        def run(e, sems, fn=fn, sem=sem):
            fn(e, sems).then_inc(sems[sem], 16)
        self.ops.append((eng, run))
        return (sem, tgt)

    FLUSH_K = 1

    def _spacer(self, sem):
        self.cnt[sem] += self.SCALE

        def run(e, sems, sem=sem):
            e.drain()
            e.sem_inc(sems[sem], self.SCALE)
        self.ops.append((sem, run))

    def wait(self, eng, dep):
        sem, val = dep
        if val <= 0:
            return
        if sem in getattr(self, "spacers", {}) and sem != eng:
            cu, sv = self.cover.get(sem, (0, 0))
            if val > cu:
                covered = self.cnt[sem]
                for _ in range(self.FLUSH_K):
                    self._spacer(sem)
                self.cover[sem] = (covered, self.cnt[sem])
                cu, sv = self.cover[sem]
            val = sv

        def run(e, sems, sem=sem, val=val):
            e.wait_ge(sems[sem], val)
        self.ops.append((eng, run))

    def waits(self, eng, deps):
        best = {}
        for (sem, val) in deps:
            if val > 0:
                best[sem] = max(best.get(sem, 0), val)
        for sem, val in best.items():
            self.wait(eng, (sem, val))

    def snap(self):
        return dict(self.cnt)

    def barrier_from(self, eng, snap):
        self.waits(eng, [(k, v) for k, v in snap.items() if k != {"Y": "dY", "G": "dG"}.get(eng, eng)])

    def emit(self, blk, sems):
        def mk(name):
            def body(e):
                for eng, fn in self.ops:
                    if eng == name:
                        fn(e, sems)
            return body
        blk.tensor(mk("T"))
        blk.vector(mk("V"))
        blk.scalar(mk("S"))
        blk.gpsimd(mk("G"))
        blk.sync(mk("Y"))


def build_program():
    nc = bass.Bass("TRN2", target_bir_lowering=False)

    xh = nc.dram_tensor("xh", [TLOC, E], F32, kind="ExternalInput").ap()
    x16 = nc.dram_tensor("x16", [16, E], F32, kind="ExternalInput").ap()
    wqT = nc.dram_tensor("wqT", [E, E], F32, kind="ExternalInput").ap()
    wkT = nc.dram_tensor("wkT", [E, E], F32, kind="ExternalInput").ap()
    wvT = nc.dram_tensor("wvT", [E, E], F32, kind="ExternalInput").ap()
    woT = nc.dram_tensor("woT", [E, E], F32, kind="ExternalInput").ap()
    w1T = nc.dram_tensor("w1T", [E, HFF], F32, kind="ExternalInput").ap()
    w2T = nc.dram_tensor("w2T", [HFF, E], F32, kind="ExternalInput").ap()
    bqc = nc.dram_tensor("bqc", [128, 8], F32, kind="ExternalInput").ap()
    bkc = nc.dram_tensor("bkc", [128, 8], F32, kind="ExternalInput").ap()
    bvr = nc.dram_tensor("bvr", [1, E], F32, kind="ExternalInput").ap()
    b1c = nc.dram_tensor("b1c", [128, 32], F32, kind="ExternalInput").ap()
    bor = nc.dram_tensor("bor", [1, E], F32, kind="ExternalInput").ap()
    b2r = nc.dram_tensor("b2r", [1, E], F32, kind="ExternalInput").ap()
    sclc = nc.dram_tensor("sclc", [16, 1], F32, kind="ExternalInput").ap()
    lamt = nc.dram_tensor("lamt", [128, NCH], F32, kind="ExternalInput").ap()
    padsel = nc.dram_tensor("padsel", [16, 16 * 65], F32, kind="ExternalInput").ap()
    lnmu = nc.dram_tensor("lnmu", [16, NOWN], F32, kind="ExternalInput").ap()
    out = nc.dram_tensor("out", [TOWN, E], F32, kind="ExternalOutput").ap()
    xnTs = nc.dram_tensor("xnTs", [E, TOWN], BF16, kind="Internal").ap()
    attnTs = nc.dram_tensor("attnTs", [E, TOWN], BF16, kind="Internal").ap()

    import contextlib
    es0 = contextlib.ExitStack()
    with es0:
        CC0 = es0.enter_context
        semT = CC0(nc.semaphore("semT"))
        semV = CC0(nc.semaphore("semV"))
        semS = CC0(nc.semaphore("semS"))
        semG = CC0(nc.semaphore("semG"))
        semdY = CC0(nc.semaphore("semdY"))
        semdG = CC0(nc.semaphore("semdG"))
        psA = CC0(nc.psum_tensor("psA", [128, 1536], F32))
        psB = CC0(nc.psum_tensor("psB", [128, 1536], F32))
        psC = CC0(nc.psum_tensor("psC", [128, 512], F32))
        psT = CC0(nc.psum_tensor("psT", [128, 1024], BF16))
        ones = CC0(nc.sbuf_tensor("ones", [128, 128], BF16))
        ident = CC0(nc.sbuf_tensor("ident", [128, 128], BF16))
        tlow = CC0(nc.sbuf_tensor("tlow", [128, 128], BF16))
        thig = CC0(nc.sbuf_tensor("thig", [128, 128], BF16))
        iotaf = CC0(nc.sbuf_tensor("iotaf", [128, 128], F32))
        iotap = CC0(nc.sbuf_tensor("iotap", [128, 128], F32))
        xn16 = CC0(nc.sbuf_tensor("xn16", [16, E], BF16))
        eoT_ = CC0(nc.sbuf_tensor("eoT", [128, 8 * 16], BF16))
        E2s = CC0(nc.sbuf_tensor("E2s", [16, E], BF16))
        ODP = CC0(nc.sbuf_tensor("ODP", [128, 32 * 16], BF16))
        sclb = CC0(nc.sbuf_tensor("sclb", [16, 1], F32))
        lamb = CC0(nc.sbuf_tensor("lamb", [128, NCH], F32))
        lnmuS = CC0(nc.sbuf_tensor("lnmuS", [16, NOWN], F32))
        smll = CC0(nc.sbuf_tensor("smll", [128, 64], F32))
        sm2 = CC0(nc.sbuf_tensor("sm2", [128, 64], F32))
        junkS = CC0(nc.sbuf_tensor("junkS", [128, 2], F32))
        junkG = CC0(nc.sbuf_tensor("junkG", [128, 2], F32))
        eoT = [eoT_[:, k * 16:(k + 1) * 16] for k in range(8)]
        sems = dict(T=semT, V=semV, S=semS, G=semG, dY=semdY, dG=semdG)

        def sl(i):
            return smll[0:16, i:i + 1]

        P = Plan()
        P.spacers = {
            "V": lambda e: e.tensor_copy(smll[:, 62:63], smll[:, 63:64]),
            "S": lambda e: e.copy(junkS[:, 0:1], junkS[:, 1:2]),
            "G": lambda e: e.memset(junkG[:, 0:1], 0.0),
        }

        def VG():
            P.op("V", lambda e, s: e.tensor_copy(smll[:, 62:63], smll[:, 63:64]))

        es1 = contextlib.ExitStack()
        with es1:
            CC1 = es1.enter_context
            wq_ = CC1(nc.sbuf_tensor("wq", [128, 8 * E], BF16))
            wk_ = CC1(nc.sbuf_tensor("wk", [128, 8 * E], BF16))
            wv_ = CC1(nc.sbuf_tensor("wv", [128, 8 * E], BF16))
            bqs = CC1(nc.sbuf_tensor("bqs", [128, 8], F32))
            bks = CC1(nc.sbuf_tensor("bks", [128, 8], F32))
            bvs = CC1(nc.sbuf_tensor("bvs", [1, E], BF16))
            pads = CC1(nc.sbuf_tensor("pads", [16, 16 * 65], BF16))
            xt_ = CC1(nc.sbuf_tensor("xt", [128, 2 * E], F32))
            xntm_ = CC1(nc.sbuf_tensor("xntm", [128, 2 * E], BF16))
            sqb_ = CC1(nc.sbuf_tensor("sqb", [128, 8 * WS], BF16))
            xnT_ = CC1(nc.sbuf_tensor("xnT", [128, 8 * 3 * WS], BF16))
            kt_ = CC1(nc.sbuf_tensor("kt", [128, 8 * 4 * WS], BF16))
            vS = CC1(nc.sbuf_tensor("vS", [128, 16 * 1088], BF16))
            qt_ = CC1(nc.sbuf_tensor("qt", [128, 8 * WS], BF16))
            probs_ = CC1(nc.sbuf_tensor("probs", [128, 4 * 1536], BF16))
            rs = CC1(nc.sbuf_tensor("rs", [128, 64], F32))
            rsb = CC1(nc.sbuf_tensor("rsb", [128, 64], BF16))
            atw_ = CC1(nc.sbuf_tensor("atw", [128, 8 * WS], BF16))
            mkr = CC1(nc.sbuf_tensor("mkr", [16, 8], F32))
            scrA = CC1(nc.sbuf_tensor("scrA", [128, 512], F32))
            scrB = CC1(nc.sbuf_tensor("scrB", [128, 512], BF16))
            rbsb = CC1(nc.sbuf_tensor("rbsb", [64, 512], F32))
            Sg = CC1(nc.sbuf_tensor("Sg", [1, 64], F32))
            Sg16 = CC1(nc.sbuf_tensor("Sg16", [1, 16], BF16))
            Sg16f = CC1(nc.sbuf_tensor("Sg16f", [1, 16], F32))
            padp = CC1(nc.sbuf_tensor("padp", [16, 512], BF16))
            nmrow = CC1(nc.sbuf_tensor("nmrow", [1, 16], BF16))
            bcA = CC1(nc.sbuf_tensor("bcA", [128, 16], F32))
            bcB = CC1(nc.sbuf_tensor("bcB", [128, 16], F32))
            bcC = CC1(nc.sbuf_tensor("bcC", [128, 16], F32))
            Jsl = CC1(nc.sbuf_tensor("Jsl", [128, 31], BF16))
            blk0 = CC1(nc.Block())
            wq = [wq_[:, k * E:(k + 1) * E] for k in range(8)]
            wk = [wk_[:, k * E:(k + 1) * E] for k in range(8)]
            wv = [wv_[:, k * E:(k + 1) * E] for k in range(8)]
            xt = [xt_[:, (i % 2) * E:(i % 2 + 1) * E] for i in range(4)]
            x16f = xt_[0:16, 0:E]
            x16sq = xntm_[0:16, 0:E]
            xntm = [xntm_[:, (i % 2) * E:(i % 2 + 1) * E] for i in range(4)]
            xnT = [xnT_[:, k * 3 * WS:(k + 1) * 3 * WS] for k in range(8)]
            kt = [kt_[:, k * 4 * WS:(k + 1) * 4 * WS] for k in range(8)]
            qt = [qt_[:, k * WS:(k + 1) * WS] for k in range(8)]
            sqb = [sqb_[:, k * WS:(k + 1) * WS] for k in range(8)]
            probs = [probs_[:, i * 1536:(i + 1) * 1536] for i in range(4)]
            r1scr = probs_[:, 0:1536]
            atw = [atw_[:, k * WS:(k + 1) * WS] for k in range(8)]

            for k in range(8):
                P.dma("G", lambda e, s, k=k: e.dma_start(out=wq[k], in_=wqT[k * 128:(k + 1) * 128, :]))
                P.dma("G", lambda e, s, k=k: e.dma_start(out=wk[k], in_=wkT[k * 128:(k + 1) * 128, :]))
                P.dma("G", lambda e, s, k=k: e.dma_start(out=wv[k], in_=wvT[k * 128:(k + 1) * 128, :]))
            P.dma("G", lambda e, s: e.dma_start(out=bvs[:, :], in_=bvr))
            P.dma("G", lambda e, s: e.dma_start(out=pads[:, :], in_=padsel))
            P.dma("Y", lambda e, s: e.dma_start(out=bqs[:, :], in_=bqc))
            P.dma("Y", lambda e, s: e.dma_start(out=bks[:, :], in_=bkc))
            P.dma("Y", lambda e, s: e.dma_start(out=sclb[:, :], in_=sclc))
            P.dma("Y", lambda e, s: e.dma_start(out=lamb[:, :], in_=lamt))
            x16ld = P.dma("Y", lambda e, s: e.dma_start(out=x16f, in_=x16))
            P.dma("Y", lambda e, s: e.dma_start(out=lnmuS[:, :], in_=lnmu))

            g1 = P.op("G", lambda e, s: e.iota(iotaf[:, :], [[1, 128]], channel_multiplier=0,
                                               allow_small_or_imprecise_dtypes=True))
            g2 = P.op("G", lambda e, s: e.iota(iotap[:, :], [[0, 128]], channel_multiplier=1,
                                               allow_small_or_imprecise_dtypes=True))
            g3 = P.op("G", lambda e, s: e.memset(Jsl[:, :], 0.0))
            g4 = P.op("G", lambda e, s: e.memset(ODP[:, :], 0.0))
            P.waits("V", [g1, g2, g3])
            c1 = P.op("V", lambda e, s: e.tensor_tensor(ident[:, :], iotaf[:, :], iotap[:, :], ALU.is_equal))
            P.op("V", lambda e, s: e.tensor_tensor(tlow[:, :], iotaf[:, :], iotap[:, :], ALU.is_le))
            P.op("V", lambda e, s: e.tensor_tensor(thig[:, :], iotaf[:, :], iotap[:, :], ALU.is_ge))
            P.op("V", lambda e, s: e.memset(Jsl[0:64, 15:16], 1.0))
            P.op("V", lambda e, s: e.memset(Jsl[64:128, 16:17], 1.0))
            P.op("V", lambda e, s: e.memset(ones[:, :], 1.0))

            P.wait("G", x16ld)
            gx = P.op("G", lambda e, s: e.tensor_mul(x16sq[:, :], x16f[:, :], x16f[:, :]))
            P.wait("V", x16ld)
            P.op("V", lambda e, s: e.reduce_sum(sm2[0:16, 0:1], x16f[:, :], axis=AX.X))
            P.wait("V", gx)
            P.op("V", lambda e, s: e.reduce_sum(sm2[0:16, 1:2], x16sq[:, :], axis=AX.X))
            P.op("V", lambda e, s: e.tensor_scalar_mul(sm2[0:16, 2:3], sm2[0:16, 0:1], 1.0 / E))
            P.op("V", lambda e, s: e.tensor_scalar_mul(sm2[0:16, 3:4], sm2[0:16, 1:2], 1.0 / E))
            P.op("V", lambda e, s: e.tensor_mul(sm2[0:16, 4:5], sm2[0:16, 2:3], sm2[0:16, 2:3]))
            VG()
            P.op("V", lambda e, s: e.tensor_sub(sm2[0:16, 5:6], sm2[0:16, 3:4], sm2[0:16, 4:5]))
            VG()
            v7 = P.op("V", lambda e, s: e.tensor_scalar_add(sm2[0:16, 6:7], sm2[0:16, 5:6], 1e-5))
            P.wait("S", v7)
            s1 = P.op("S", lambda e, s: e.sqrt(sm2[0:16, 7:8], sm2[0:16, 6:7]))
            P.wait("V", s1)
            P.op("V", lambda e, s: e.reciprocal(sm2[0:16, 8:9], sm2[0:16, 7:8]))
            VG()
            P.op("V", lambda e, s: e.tensor_mul(sm2[0:16, 9:10], sm2[0:16, 2:3], sm2[0:16, 8:9]))
            VG()
            v10 = P.op("V", lambda e, s: e.tensor_scalar_mul(sm2[0:16, 10:11], sm2[0:16, 9:10], -1.0))
            P.wait("S", v10)
            sx = P.op("S", lambda e, s: e.activation(xn16[:, :], x16f[:, :], AF.Identity,
                                                     bias=sm2[0:16, 10:11], scale=sm2[0:16, 8:9]))
            P.waits("T", [sx, c1])
            trl = []
            for k in range(8):
                trl.append(P.op("T", lambda e, s, k=k: e.matmul(
                    psT[:, k * 128:k * 128 + 16], xn16[:, k * 128:(k + 1) * 128], ident[0:16, 0:16],
                    start=(k == 0), stop=(k == 7), is_transpose=True)))
            P.waits("V", trl[-1:])
            ev = []
            for k in range(8):
                ev.append(P.op("V", lambda e, s, k=k: e.tensor_copy(eoT[k][:, :], psT[:, k * 128:k * 128 + 16])))
            ez = P.op("V", lambda e, s: e.memset(E2s[:, :], 0.0))
            P.waits("Y", [sx, ez])
            for m in range(16):
                i, jpp = m // 4, m % 4
                P.dma("Y", lambda e, s, i=i, jpp=jpp, m=m: e.dma_start(
                    out=E2s[m:m + 1, i * 256:(i + 1) * 256],
                    in_=xn16[8 + 2 * i:9 + 2 * i, jpp:jpp + E - 4 + 1:4],
                    allow_slow_non_contiguous=True))
            P.waits("V", [g4] + ev)
            for hk in range(32):
                k, jpp = hk % 8, hk // 8
                P.op("V", lambda e, s, hk=hk, k=k, jpp=jpp: e.tensor_copy(
                    bass.AP(ODP, hk * 16 + jpp, [[32 * 16, 128], [4, 4], [1, 1]]),
                    eoT[k][:, 9:16:2]))

            snap = P.snap()
            for eng in ("T", "V", "S", "G", "Y"):
                P.barrier_from(eng, snap)

            stage_done = {}
            sqred = {}
            sqredv = {}
            xtnrm = {}
            prod_done = {}

            def produce_gen(lw):
                slot2 = lw % 3
                slot3 = lw % 4
                if lw - 1 in prod_done:
                    for eng in ("T", "V", "S", "Y"):
                        P.barrier_from(eng, prod_done[lw - 1])
                def emit_load(tc):
                    key = (lw, tc - 2) if tc >= 2 else (lw - 1, tc + 2)
                    if key in sqred:
                        P.waits("Y", [sqred[key], sqredv[key]])
                    if key in xtnrm:
                        P.waits("Y", [xtnrm[key]])
                    return P.dma("Y", lambda e, s, lw=lw, tc=tc: e.dma_start(
                        out=xt[tc], in_=xh[lw * 512 + tc * 128: lw * 512 + (tc + 1) * 128, :]))

                xload = {}
                xload[0] = emit_load(0)
                xload[1] = emit_load(1)
                yield
                evacs = []
                trl_done = {}
                for half in range(2):
                    c0 = 2 * half
                    for tc in (c0, c0 + 1):
                        xb = xt[tc]
                        P.wait("S", xload[tc])
                        if tc >= 2:
                            P.wait("S", trl_done[tc - 2])
                        sqred[(lw, tc)] = P.op("S", lambda e, s, xb=xb, tc=tc: e.activation(
                            xntm[tc], xb, AF.Square, bias=0.0, scale=1.0,
                            accum_out=sm2[:, 20 + tc:21 + tc]))
                        P.wait("V", xload[tc])
                        sqredv[(lw, tc)] = P.op("V", lambda e, s, xb=xb, tc=tc: e.reduce_sum(
                            sm2[:, 16 + tc:17 + tc], xb, axis=AX.X))
                        yield
                    P.waits("V", [sqred[(lw, c0 + 1)]])
                    P.op("V", lambda e, s, c0=c0: e.tensor_scalar_mul(sm2[:, 24 + c0:26 + c0], sm2[:, 16 + c0:18 + c0], 1.0 / E))
                    P.op("V", lambda e, s, c0=c0: e.tensor_scalar_mul(sm2[:, 28 + c0:30 + c0], sm2[:, 20 + c0:22 + c0], 1.0 / E))
                    P.op("V", lambda e, s, c0=c0: e.tensor_mul(sm2[:, 32 + c0:34 + c0], sm2[:, 24 + c0:26 + c0], sm2[:, 24 + c0:26 + c0]))
                    VG()
                    P.op("V", lambda e, s, c0=c0: e.tensor_sub(sm2[:, 36 + c0:38 + c0], sm2[:, 28 + c0:30 + c0], sm2[:, 32 + c0:34 + c0]))
                    VG()
                    a7 = P.op("V", lambda e, s, c0=c0: e.tensor_scalar_add(sm2[:, 40 + c0:42 + c0], sm2[:, 36 + c0:38 + c0], 1e-5))
                    P.wait("S", a7)
                    b1_ = P.op("S", lambda e, s, c0=c0: e.sqrt(sm2[:, 44 + c0:46 + c0], sm2[:, 40 + c0:42 + c0]))
                    P.wait("V", b1_)
                    P.op("V", lambda e, s, c0=c0: e.reciprocal(sm2[:, 48 + c0:50 + c0], sm2[:, 44 + c0:46 + c0]))
                    VG()
                    P.op("V", lambda e, s, c0=c0: e.tensor_mul(sm2[:, 52 + c0:54 + c0], sm2[:, 24 + c0:26 + c0], sm2[:, 48 + c0:50 + c0]))
                    VG()
                    b4 = P.op("V", lambda e, s, c0=c0: e.tensor_scalar_mul(sm2[:, 56 + c0:58 + c0], sm2[:, 52 + c0:54 + c0], -1.0))
                    yield
                    for tc in (c0, c0 + 1):
                        xb = xt[tc]
                        if tc == c0:
                            P.wait("V", b4)
                        if tc >= 2:
                            P.waits("V", [trl_done[tc - 2]])
                        VG()
                        nrm = P.op("V", lambda e, s, xb=xb, tc=tc: e.tensor_scalar(
                            xntm[tc], xb, sm2[:, 56 + tc:57 + tc], sm2[:, 48 + tc:49 + tc],
                            ALU.add, ALU.mult))
                        xtnrm[(lw, tc)] = nrm
                        P.wait("T", nrm)
                        if evacs:
                            P.waits("T", evacs[-1:])
                        trl2 = []
                        for k in range(8):
                            trl2.append(P.op("T", lambda e, s, tc=tc, k=k: e.matmul(
                                psT[:, k * 128:(k + 1) * 128], xntm[tc][:, k * 128:(k + 1) * 128], ident[:, :],
                                start=(k == 0), stop=(k == 7), is_transpose=True)))
                        trl_done[tc] = trl2[-1]
                        P.waits("V", trl2[-1:])
                        for k in range(8):
                            col = slot2 * 512 + tc * 128
                            evacs.append(P.op("V", lambda e, s, k=k, col=col: e.tensor_copy(
                                xnT[k][:, col:col + 128], psT[:, k * 128:(k + 1) * 128])))
                        if tc + 2 <= 3:
                            xload[tc + 2] = emit_load(tc + 2)
                        yield
                if 1 <= lw <= 4:
                    P.waits("Y", evacs[-8:])
                    for k in range(8):
                        P.dma("Y", lambda e, s, k=k, lw=lw, slot2=slot2: e.dma_start(
                            out=xnTs[k * 128:(k + 1) * 128, (lw - 1) * 512:lw * 512],
                            in_=xnT[k][:, slot2 * 512:slot2 * 512 + 512]))
                    yield
                P.waits("T", evacs)
                pbank = [psB[:, 0:512], psB[:, 512:1024], psB[:, 1024:1536]]
                kdone = []
                for oc in range(8):
                    if oc >= 3:
                        P.wait("T", kdone[oc - 3])
                    half = pbank[oc % 3]
                    mm = []
                    for k in range(8):
                        mm.append(P.op("T", lambda e, s, oc=oc, k=k, half=half, slot2=slot2: e.matmul(
                            half, wk[k][:, oc * 128:(oc + 1) * 128],
                            xnT[k][:, slot2 * 512:slot2 * 512 + 512],
                            start=(k == 0), stop=(k == 7))))
                    P.waits("V", mm[-1:])
                    kdone.append(P.op("V", lambda e, s, oc=oc, half=half, slot3=slot3: e.tensor_scalar(
                        kt[oc][:, slot3 * 512:slot3 * 512 + 512], half,
                        bks[:, oc:oc + 1], None, ALU.add)))
                    if oc % 2 == 1:
                        yield
                vprev = []
                for tc in range(4):
                    kc = lw * 4 + tc
                    slot16 = kc % 16
                    for oc2 in range(2):
                        idx = tc * 2 + oc2
                        if idx >= 3:
                            P.wait("T", vprev[idx - 3])
                        else:
                            P.wait("T", kdone[[6, 7, 5][idx]])
                        half = pbank[idx % 3]
                        hh, ho = psB, (idx % 3) * 512
                        mm = []
                        for k in range(8):
                            mm.append(P.op("T", lambda e, s, tc=tc, oc2=oc2, k=k, half=half, slot2=slot2: e.matmul(
                                half,
                                xnT[k][:, slot2 * 512 + tc * 128: slot2 * 512 + (tc + 1) * 128],
                                wv[k][:, oc2 * 512:(oc2 + 1) * 512],
                                start=(k == 0), stop=False)))
                        mm.append(P.op("T", lambda e, s, oc2=oc2, half=half: e.matmul(
                            half, ones[0:1, 0:128], bvs[0:1, oc2 * 512:(oc2 + 1) * 512],
                            start=False, stop=True)))
                        P.waits("V", mm[-1:])
                        base = slot16 * 1088 + oc2 * 8 * 68
                        vprev.append(P.op("V", lambda e, s, base=base, kc=kc, hh=hh, ho=ho: e.tensor_scalar(
                            bass.AP(vS, base, [[16 * 1088, 128], [68, 8], [1, 64]]),
                            bass.AP(hh, ho, [[1536, 128], [64, 8], [1, 64]]),
                            lamb[:, kc:kc + 1], None, ALU.mult)))
                    P.op("V", lambda e, s, slot16=slot16, kc=kc: e.tensor_scalar(
                        bass.AP(vS, slot16 * 1088 + 64, [[16 * 1088, 128], [68, 16], [1, 1]]),
                        bass.AP(ones, 0, [[128, 128], [1, 16], [1, 1]]),
                        lamb[:, kc:kc + 1], None, ALU.mult))
                    yield
                P.wait("G", ("T", P.cnt["T"]))
                P.waits("G", kdone + vprev[-1:])
                sqd = []
                for k in range(8):
                    sqd.append(P.op("G", lambda e, s, k=k, slot3=slot3: e.tensor_mul(
                        sqb[k][:, :], kt[k][:, slot3 * 512:slot3 * 512 + 512],
                        kt[k][:, slot3 * 512:slot3 * 512 + 512])))
                P.waits("T", sqd + vprev)
                nm = []
                for k in range(8):
                    nm.append(P.op("T", lambda e, s, k=k: e.matmul(
                        psB[0:16, 0:512], Jsl[:, 15 - 2 * k:31 - 2 * k], sqb[k][:, :],
                        start=(k == 0), stop=(k == 7))))
                P.waits("V", nm[-1:])
                P.op("V", lambda e, s, lw=lw: e.reduce_max(mkr[:, lw:lw + 1], psB[0:16, 0:512], axis=AX.X))
                prod_done[lw] = P.snap()

            def run_gen(g):
                for _ in g:
                    pass

            def attend(w, feeder=lambda: None):
                lw = w + 1
                slot2 = lw % 3
                if lw + 1 in prod_done:
                    for eng in ("T", "V", "S", "G"):
                        P.barrier_from(eng, prod_done[lw + 1])
                qdone = []
                banks3 = [psA[:, 0:512], psA[:, 512:1024], psA[:, 1024:1536]]
                for oc in range(8):
                    if oc >= 3:
                        P.wait("T", qdone[oc - 3])
                    half = banks3[oc % 3]
                    mm = []
                    for k in range(8):
                        mm.append(P.op("T", lambda e, s, oc=oc, k=k, half=half, slot2=slot2: e.matmul(
                            half, wq[k][:, oc * 128:(oc + 1) * 128],
                            xnT[k][:, slot2 * 512:slot2 * 512 + 512],
                            start=(k == 0), stop=(k == 7))))
                    P.waits("V", mm[-1:])
                    qdone.append(P.op("V", lambda e, s, oc=oc, half=half: e.tensor_scalar(
                        qt[oc][:, :], half, bqs[:, oc:oc + 1], None, ALU.add)))
                P.waits("G", qdone)
                sqd = []
                for k in range(8):
                    sqd.append(P.op("G", lambda e, s, k=k: e.tensor_mul(sqb[k][:, :], qt[k][:, :], qt[k][:, :])))
                P.waits("T", sqd + qdone)
                nm = []
                for k in range(8):
                    nm.append(P.op("T", lambda e, s, k=k: e.matmul(
                        psC[0:16, 0:512], Jsl[:, 15 - 2 * k:31 - 2 * k], sqb[k][:, :],
                        start=(k == 0), stop=(k == 7))))
                P.waits("V", nm[-1:])
                vq = P.op("V", lambda e, s: e.reduce_max(sl(0), psC[0:16, 0:512], axis=AX.X))
                P.wait("T", qdone[6])
                sg = []
                for k in range(8):
                    sg.append(P.op("T", lambda e, s, k=k: e.matmul(
                        psA[0:16, 0:512], Jsl[:, 15 - 2 * k:31 - 2 * k], qt[k][:, :],
                        start=(k == 0), stop=(k == 7))))
                P.waits("V", sg[-1:])
                sgv = P.op("V", lambda e, s: e.tensor_copy(scrA[0:16, :], psA[0:16, 0:512]))
                P.wait("V", vq)
                P.op("V", lambda e, s, lw=lw: e.tensor_max(sl(1), mkr[:, lw - 1:lw], mkr[:, lw:lw + 1]))
                VG()
                P.op("V", lambda e, s, lw=lw: e.tensor_max(sl(2), sl(1), mkr[:, lw + 1:lw + 2]))
                VG()
                P.op("V", lambda e, s: e.tensor_scalar_max(sl(3), sl(2), 64.0))
                VG()
                vb = P.op("V", lambda e, s: e.tensor_mul(sl(4), sl(0), sl(3)))
                P.wait("S", vb)
                sb = P.op("S", lambda e, s: e.sqrt(sl(5), sl(4)))
                P.wait("V", sb)
                P.op("V", lambda e, s: e.reciprocal(sl(6), sl(5)))
                VG()
                P.op("V", lambda e, s: e.tensor_scalar_mul(sl(7), sl(6), 40.0))
                VG()
                P.op("V", lambda e, s: e.tensor_scalar_min(sl(8), sl(7), 1.0))
                VG()
                P.op("V", lambda e, s: e.tensor_copy(scrB[0:16, 0:1], sl(8)))
                VG()
                P.op("V", lambda e, s: e.tensor_copy(sl(9), scrB[0:16, 0:1]))
                VG()
                P.op("V", lambda e, s: e.reciprocal(sl(10), sl(9)))
                P.op("V", lambda e, s: e.tensor_scalar_mul(sl(22), sl(9), -1.0))
                P.op("V", lambda e, s: e.tensor_mul(sl(11), sl(5), sl(9)))
                VG()
                P.op("V", lambda e, s: e.tensor_scalar_mul(sl(11), sl(11), -1.0))
                P.op("V", lambda e, s: e.tensor_scalar(sl(12), sl(5), -1.0, LN512, ALU.mult, ALU.add))
                VG()
                P.op("V", lambda e, s: e.tensor_mul(sl(12), sl(12), sl(9)))
                VG()
                r11 = P.op("V", lambda e, s: e.tensor_copy(scrB[0:16, 1:2], sl(11)))
                P.waits("T", [r11])
                t1 = P.op("T", lambda e, s: e.matmul(psT[0:1, 0:16], scrB[0:16, 0:1], ident[0:16, 0:16],
                                                     start=True, stop=True, is_transpose=True))
                P.waits("V", [t1])
                u1 = P.op("V", lambda e, s: e.tensor_copy(nmrow[0:1, 0:16], psT[0:1, 0:16]))
                P.waits("T", [u1])
                t3 = P.op("T", lambda e, s: e.matmul(psC[:, 0:16], ones[0:1, 0:128], nmrow[0:1, 0:16],
                                                     start=True, stop=True))
                P.waits("V", [t3])
                u3 = P.op("V", lambda e, s: e.tensor_copy(bcB[:, :], psC[:, 0:16]))
                P.waits("T", [u3])
                t2 = P.op("T", lambda e, s: e.matmul(psT[0:1, 0:16], scrB[0:16, 1:2], ident[0:16, 0:16],
                                                     start=True, stop=True, is_transpose=True))
                P.waits("V", [t2])
                u2 = P.op("V", lambda e, s: e.tensor_copy(nmrow[0:1, 0:16], psT[0:1, 0:16]))
                P.waits("T", [u2])
                t4 = P.op("T", lambda e, s: e.matmul(psC[:, 0:16], ones[0:1, 0:128], nmrow[0:1, 0:16],
                                                     start=True, stop=True))
                P.waits("V", [t4])
                u4 = P.op("V", lambda e, s: e.tensor_copy(bcC[:, :], psC[:, 0:16]))
                snap0 = P.snap()
                for eng in ("T", "S", "V"):
                    P.barrier_from(eng, snap0)

                pe = P.op("S", lambda e, s: e.activation(
                    padp[:, :], scrA[0:16, 0:512], AF.Exp, bias=sl(12), scale=sl(22),
                    accum_out=sl(13)))

                lastex = [None]
                for h in range(H):
                    kh = h // 2
                    ro = (h % 2) * 64
                    for g, (j0, j1) in enumerate(GROUPS):
                        ps = psA
                        if lastex[0] is not None:
                            P.wait("T", lastex[0])
                        mm = []
                        for j in range(j0, j1):
                            kc = w * 4 + j
                            col = ((kc // 4) % 4) * 512 + (kc % 4) * 128
                            mm.append(P.op("T", lambda e, s, kh=kh, ro=ro, col=col, j=j, j0=j0, ps=ps: e.matmul(
                                ps[:, (j - j0) * 512:(j - j0 + 1) * 512],
                                kt[kh][ro:ro + 64, col:col + 128],
                                qt[kh][ro:ro + 64, :],
                                start=True, stop=True)))
                        P.waits("S", mm)
                        lastex[0] = P.op("S", lambda e, s, ps=ps, h=h, g=g: e.activation(
                            r1scr[:, :], ps[:, :], AF.Exp, bias=bcC[:, h:h + 1], scale=bcB[:, h:h + 1],
                            accum_out=rs[:, 4 * h + g:4 * h + g + 1]))
                    feeder()
                snap1 = P.snap()
                for eng in ("T", "V"):
                    P.barrier_from(eng, snap1)

                rc = P.op("V", lambda e, s: e.tensor_copy(rsb[:, :], rs[:, :]))
                P.waits("T", [rc])
                smm = []
                for h in range(H):
                    smm.append(P.op("T", lambda e, s, h=h: e.matmul(
                        psC[0:1, 4 * h:4 * h + 4], ones[:, 0:1], rsb[:, 4 * h:4 * h + 4],
                        start=(h == 0), stop=(h == 15))))
                P.waits("V", smm[-1:])
                P.op("V", lambda e, s: e.tensor_copy(Sg[0:1, :], psC[0:1, 0:64]))
                VG()
                P.op("V", lambda e, s: e.reduce_sum(
                    Sg16f[0:1, 0:16], bass.AP(Sg, 0, [[64, 1], [4, 16], [1, 4]]), axis=AX.X))
                VG()
                w2_ = P.op("V", lambda e, s: e.tensor_copy(Sg16[0:1, 0:16], Sg16f[0:1, 0:16]))
                P.waits("T", [w2_])
                w3_ = P.op("T", lambda e, s: e.matmul(psC[0:16, 64:65], Sg16[0:1, 0:16], ones[0:1, 0:1],
                                                      start=True, stop=True))
                P.waits("V", [w3_, pe])
                P.op("V", lambda e, s: e.tensor_copy(sl(14), psC[0:16, 64:65]))
                VG()
                w5_ = P.op("V", lambda e, s: e.tensor_add(sl(15), sl(14), sl(13)))
                P.wait("S", w5_)
                w6_ = P.op("S", lambda e, s: e.activation(sl(16), sl(15), AF.Ln, bias=0.0, scale=1.0))
                P.wait("V", w6_)
                P.op("V", lambda e, s: e.tensor_mul(sl(17), sl(16), sl(10)))
                VG()
                P.op("V", lambda e, s: e.tensor_add(sl(18), sl(17), sl(5)))
                VG()
                P.op("V", lambda e, s: e.tensor_scalar(sl(19), sl(18), -1.0, C2, ALU.mult, ALU.add))
                VG()
                P.op("V", lambda e, s: e.tensor_scalar_min(sl(19), sl(19), 80.0))
                VG()
                wa_ = P.op("V", lambda e, s: e.tensor_copy(scrB[0:16, 2:3], sl(19)))
                VG()
                P.op("V", lambda e, s: e.tensor_copy(sl(20), scrB[0:16, 2:3]))
                VG()
                P.op("V", lambda e, s: e.tensor_scalar_add(sl(21), sl(20), LN512))
                VG()
                wc_ = P.op("V", lambda e, s, w=w: e.tensor_add(sl(23), sl(21), lnmuS[0:16, w:w + 1]))
                P.waits("T", [wa_])
                y1_ = P.op("T", lambda e, s: e.matmul(psT[0:1, 0:16], scrB[0:16, 2:3], ident[0:16, 0:16],
                                                      start=True, stop=True, is_transpose=True))
                P.waits("V", [y1_])
                y2_ = P.op("V", lambda e, s: e.tensor_copy(nmrow[0:1, 0:16], psT[0:1, 0:16]))
                P.waits("T", [y2_])
                y3_ = P.op("T", lambda e, s: e.matmul(psC[:, 0:16], ones[0:1, 0:128], nmrow[0:1, 0:16],
                                                      start=True, stop=True))
                P.waits("V", [y3_])
                P.op("V", lambda e, s: e.tensor_copy(bcA[:, :], psC[:, 0:16]))
                P.waits("S", [wc_])
                P.op("S", lambda e, s: e.activation(padp[:, :], scrA[0:16, 0:512], AF.Exp,
                                                    bias=sl(23), scale=-1.0))
                snap2 = P.snap()
                for eng in ("T", "S", "V"):
                    P.barrier_from(eng, snap2)

                lastex = [None]
                prev_n5 = None
                for h in range(H):
                    kh = h // 2
                    ro = (h % 2) * 64
                    groupinfo = []
                    exps = []
                    for g, (j0, j1) in enumerate(GROUPS):
                        ps = psA
                        if lastex[0] is not None:
                            P.wait("T", lastex[0])
                        for j in range(j0, j1):
                            kc = w * 4 + j
                            col = ((kc // 4) % 4) * 512 + (kc % 4) * 128
                            if j <= 3:
                                qlo, qhi = 0, (j + 1) * 128
                            elif j >= 8:
                                qlo, qhi = (j - 8) * 128, 512
                            else:
                                qlo, qhi = 0, 512
                            P.op("T", lambda e, s, kh=kh, ro=ro, col=col, j=j, j0=j0, ps=ps,
                                 qlo=qlo, qhi=qhi: e.matmul(
                                ps[:, (j - j0) * 512 + qlo:(j - j0) * 512 + qhi],
                                kt[kh][ro:ro + 64, col:col + 128],
                                qt[kh][ro:ro + 64, qlo:qhi],
                                start=True, stop=True))
                        cntT = ("T", P.cnt["T"])
                        P.wait("S", cntT)
                        pb = probs[g]
                        ex = P.op("S", lambda e, s, ps=ps, h=h, pb=pb: e.activation(
                            pb[:, :], ps[:, :], AF.Exp, bias=bcA[:, h:h + 1], scale=1.0))
                        exps.append(ex)
                        lastex[0] = ex
                        groupinfo.append((g, j0, j1, pb))
                    P.waits("V", exps[-1:])
                    mk = []
                    for (g, j0, j1, pb) in groupinfo:
                        for j in range(j0, j1):
                            loc = (j - j0) * 512
                            if j <= 3:
                                b0 = j * 128
                                mk.append(P.op("V", lambda e, s, pb=pb, loc=loc, b0=b0: e.tensor_mul(
                                    pb[:, loc + b0:loc + b0 + 128], pb[:, loc + b0:loc + b0 + 128], tlow[:, :])))
                            elif j >= 8:
                                b0 = (j - 8) * 128
                                mk.append(P.op("V", lambda e, s, pb=pb, loc=loc, b0=b0: e.tensor_mul(
                                    pb[:, loc + b0:loc + b0 + 128], pb[:, loc + b0:loc + b0 + 128], thig[:, :])))
                    P.waits("T", mk[-1:] + exps[-1:])
                    if prev_n5 is not None:
                        P.wait("T", prev_n5)
                    first = True
                    for (g, j0, j1, pb) in groupinfo:
                        for j in range(j0, j1):
                            kc = w * 4 + j
                            slot16 = kc % 16
                            loc = (j - j0) * 512
                            if j <= 3:
                                qlo, qhi = 0, (j + 1) * 128
                            elif j >= 8:
                                qlo, qhi = (j - 8) * 128, 512
                            else:
                                qlo, qhi = 0, 512
                            P.op("T", lambda e, s, slot16=slot16, h=h, pb=pb, loc=loc,
                                 qlo=qlo, qhi=qhi, first=first: e.matmul(
                                psC[0:65, qlo:qhi],
                                vS[:, slot16 * 1088 + h * 68:slot16 * 1088 + h * 68 + 65],
                                pb[:, loc + qlo:loc + qhi],
                                start=first, stop=False))
                            first = False
                    pvlast = P.op("T", lambda e, s, h=h, w=w: e.matmul(
                        psC[0:65, 0:512],
                        pads[:, h * 65:h * 65 + 65],
                        padp[:, :],
                        start=False, stop=True))
                    P.waits("V", [pvlast])
                    n1 = P.op("V", lambda e, s: e.tensor_copy(scrA[64:65, :], psC[64:65, 0:512]))
                    VG()
                    P.op("V", lambda e, s: e.reciprocal(scrA[96:97, :], scrA[64:65, :]))
                    VG()
                    n2 = P.op("V", lambda e, s: e.tensor_copy(scrB[64:65, 0:512], scrA[96:97, :]))
                    psn = psA
                    P.waits("T", [n2])
                    n3 = P.op("T", lambda e, s, psn=psn: e.matmul(
                        psn[0:64, 0:512], ones[64:65, 0:64], scrB[64:65, 0:512], start=True, stop=True))
                    P.waits("V", [n3])
                    n4 = P.op("V", lambda e, s, psn=psn: e.tensor_copy(rbsb[:, :], psn[0:64, 0:512]))
                    VG()
                    n5 = P.op("V", lambda e, s, kh=kh, ro=ro: e.scalar_tensor_tensor(
                        atw[kh][ro:ro + 64, :], psC[0:64, 0:512], 1.0, rbsb[:, :], ALU.mult, ALU.mult))
                    prev_n5 = ("V", P.cnt["V"])
                    lastex[0] = n4
                    feeder()
                snap3 = P.snap()
                P.barrier_from("Y", snap3)
                for k in range(8):
                    P.dma("Y", lambda e, s, k=k, w=w: e.dma_start(
                        out=attnTs[k * 128:(k + 1) * 128, w * 512:(w + 1) * 512], in_=atw[k][:, :]))
                stage_done[w] = P.snap()
                for eng in ("T", "V", "S", "G"):
                    P.barrier_from(eng, stage_done[w])

            run_gen(produce_gen(0))
            run_gen(produce_gen(1))
            run_gen(produce_gen(2))
            for _w in range(4):
                if _w + 3 <= 5:
                    _gen = produce_gen(_w + 3)

                    def _feeder(_gen=_gen):
                        next(_gen, None)
                else:
                    def _feeder():
                        pass
                attend(_w, _feeder)
                if _w + 3 <= 5:
                    run_gen(_gen)

            P.emit(blk0, sems)

        P2 = Plan(P.cnt)
        P2.spacers = P.spacers
        es2 = contextlib.ExitStack()
        with es2:
            CC2 = es2.enter_context
            W1_ = CC2(nc.sbuf_tensor("W1", [128, 8 * HFF], BF16))
            W2_ = CC2(nc.sbuf_tensor("W2", [128, 32 * E], BF16))
            WO_ = CC2(nc.sbuf_tensor("WO", [128, 8 * E], BF16))
            b1s = CC2(nc.sbuf_tensor("b1s", [128, 32], F32))
            bsum = CC2(nc.sbuf_tensor("bsum", [1, E], BF16))
            hT_ = CC2(nc.sbuf_tensor("hT", [128, 32 * SPAN], BF16))
            xnsp_ = CC2(nc.sbuf_tensor("xnsp", [128, 8 * SPAN], BF16))
            atsp_ = CC2(nc.sbuf_tensor("atsp", [128, 8 * SPAN], BF16))
            xsp_ = CC2(nc.sbuf_tensor("xsp", [128, 2 * E], F32))
            ost_ = CC2(nc.sbuf_tensor("ost", [128, 2 * 512], F32))
            aT4 = CC2(nc.sbuf_tensor("aT4", [4, SPAN], BF16))
            aTr = CC2(nc.sbuf_tensor("aTr", [1, 4 * SPAN], BF16))
            EVN = CC2(nc.sbuf_tensor("EVN", [1, 4 * E], BF16))
            cS = CC2(nc.sbuf_tensor("cS", [16, SPAN], BF16))
            blk1 = CC2(nc.Block())
            W1 = [W1_[:, k * HFF:(k + 1) * HFF] for k in range(8)]
            W2 = [W2_[:, k * E:(k + 1) * E] for k in range(32)]
            WO = [WO_[:, k * E:(k + 1) * E] for k in range(8)]
            hT = [hT_[:, k * SPAN:(k + 1) * SPAN] for k in range(32)]
            xnsp = [xnsp_[:, k * SPAN:(k + 1) * SPAN] for k in range(8)]
            atsp = [atsp_[:, k * SPAN:(k + 1) * SPAN] for k in range(8)]
            xsp = [xsp_[:, i * E:(i + 1) * E] for i in range(2)]
            bosr = xsp_[0:1, 0:E]
            b2sr = xsp_[0:1, E:2 * E]
            ost = [ost_[:, i * 512:(i + 1) * 512] for i in range(2)]

            pb0 = P2.snap()
            for eng in ("T", "V", "S", "G", "Y"):
                P2.barrier_from(eng, pb0)

            for k in range(8):
                P2.dma("G", lambda e, s, k=k: e.dma_start(out=W1[k], in_=w1T[k * 128:(k + 1) * 128, :]))
            for k in range(32):
                P2.dma("G", lambda e, s, k=k: e.dma_start(out=W2[k], in_=w2T[k * 128:(k + 1) * 128, :]))
            for k in range(8):
                P2.dma("G", lambda e, s, k=k: e.dma_start(out=WO[k], in_=woT[k * 128:(k + 1) * 128, :]))
            for i in range(4):
                P2.dma("Y", lambda e, s, i=i: e.dma_start(out=EVN[0:1, i * E:(i + 1) * E], in_=xn16[2 * i:2 * i + 1, :]))
            l1 = P2.dma("Y", lambda e, s: e.dma_start(out=b1s[:, :], in_=b1c))
            l2 = P2.dma("Y", lambda e, s: e.dma_start(out=bosr, in_=bor))
            l3 = P2.dma("Y", lambda e, s: e.dma_start(out=b2sr, in_=b2r))
            P2.waits("V", [l2, l3])
            P2.op("V", lambda e, s: e.tensor_add(bsum[0:1, :], bosr[0:1, :], b2sr[0:1, :]))
            wsnap = P2.snap()
            for eng in ("T", "S", "V", "Y"):
                P2.barrier_from(eng, wsnap)

            for sp in range(NSPAN):
                sb2 = 0
                lds = []
                for k in range(8):
                    lds.append(P2.dma("Y", lambda e, s, k=k, sp=sp, sb2=sb2: e.dma_start(
                        out=xnsp[k][:, :],
                        in_=xnTs[k * 128:(k + 1) * 128, sp * SPAN:(sp + 1) * SPAN])))
                    lds.append(P2.dma("Y", lambda e, s, k=k, sp=sp, sb2=sb2: e.dma_start(
                        out=atsp[k][:, :],
                        in_=attnTs[k * 128:(k + 1) * 128, sp * SPAN:(sp + 1) * SPAN])))
                for tt in range(2):
                    lds.append(P2.dma("Y", lambda e, s, tt=tt, sp=sp, sb2=sb2: e.dma_start(
                        out=xsp[tt][:, :],
                        in_=xh[512 + sp * SPAN + tt * 128: 512 + sp * SPAN + (tt + 1) * 128, :])))
                P2.waits("T", lds)
                am = []
                for k in range(8):
                    am.append(P2.op("T", lambda e, s, k=k, sb2=sb2: e.matmul(
                        psC[0:4, 0:SPAN], eoT[k][:, 1:8:2], xnsp[k][:, :],
                        start=(k == 0), stop=(k == 7))))
                P2.waits("S", am[-1:])
                ae = P2.op("S", lambda e, s: e.activation(aT4[:, :], psC[0:4, 0:SPAN], AF.Identity,
                                                          bias=0.0, scale=sclb[0:4, 0:1]))
                P2.wait("Y", ae)
                acp = []
                for i in range(4):
                    acp.append(P2.dma("Y", lambda e, s, i=i: e.dma_start(
                        out=aTr[0:1, i * SPAN:(i + 1) * SPAN], in_=aT4[i:i + 1, :])))
                P2.waits("T", acp)
                hdone = []
                hbanks = [psA[:, 0:SPAN], psA[:, 512:512 + SPAN], psA[:, 1024:1024 + SPAN],
                          psB[:, 0:SPAN], psB[:, 512:512 + SPAN]]
                for hc in range(32):
                    ph = hbanks[hc % 5]
                    if hc >= 5 and hc % 5 == 0:
                        P2.wait("T", hdone[hc - 1])
                    mm = []
                    for k in range(8):
                        mm.append(P2.op("T", lambda e, s, hc=hc, k=k, ph=ph: e.matmul(
                            ph, W1[k][:, hc * 128:(hc + 1) * 128],
                            xnsp[k][:, :],
                            start=(k == 0), stop=False)))
                    i = hc // 8
                    mm.append(P2.op("T", lambda e, s, hc=hc, i=i, ph=ph: e.matmul(
                        ph, EVN[0:1, i * E + (hc % 8) * 128: i * E + (hc % 8 + 1) * 128],
                        aTr[0:1, i * SPAN:(i + 1) * SPAN],
                        start=False, stop=True)))
                    P2.waits("S", mm[-1:])
                    hdone.append(P2.op("S", lambda e, s, hc=hc, ph=ph: e.activation(
                        hT[hc][:, :], ph, AF.Gelu, bias=b1s[:, hc:hc + 1], scale=1.0)))
                P2.waits("T", hdone[-1:])
                cm = []
                for hk in range(32):
                    cm.append(P2.op("T", lambda e, s, hk=hk: e.matmul(
                        psC[0:16, 0:SPAN], ODP[:, hk * 16:(hk + 1) * 16], hT[hk][:, :],
                        start=(hk == 0), stop=(hk == 31))))
                P2.waits("S", cm[-1:])
                ce = P2.op("S", lambda e, s: e.activation(cS[:, :], psC[0:16, 0:SPAN], AF.Identity,
                                                          bias=0.0, scale=sclb[0:16, 0:1]))
                P2.wait("T", ce)
                for tt in range(2):
                    for oc2 in range(2):
                        po = psB[:, 1024:1536] if (tt * 2 + oc2) % 2 == 0 else psC[:, 0:512]
                        ob = ost[(tt * 2 + oc2) % 2]
                        mm = []
                        for hk in range(32):
                            mm.append(P2.op("T", lambda e, s, hk=hk, tt=tt, oc2=oc2, po=po: e.matmul(
                                po, hT[hk][:, tt * 128:(tt + 1) * 128],
                                W2[hk][:, oc2 * 512:(oc2 + 1) * 512],
                                start=(hk == 0), stop=False)))
                        for k in range(8):
                            mm.append(P2.op("T", lambda e, s, k=k, tt=tt, oc2=oc2, po=po, sb2=sb2: e.matmul(
                                po, atsp[k][:, tt * 128:(tt + 1) * 128],
                                WO[k][:, oc2 * 512:(oc2 + 1) * 512],
                                start=False, stop=False)))
                        mm.append(P2.op("T", lambda e, s, tt=tt, oc2=oc2, po=po: e.matmul(
                            po, cS[:, tt * 128:(tt + 1) * 128],
                            E2s[:, oc2 * 512:(oc2 + 1) * 512],
                            start=False, stop=False)))
                        mm.append(P2.op("T", lambda e, s, oc2=oc2, po=po: e.matmul(
                            po, ones[0:1, 0:128], bsum[0:1, oc2 * 512:(oc2 + 1) * 512],
                            start=False, stop=True)))
                        P2.waits("V", mm[-1:])
                        vo = P2.op("V", lambda e, s, tt=tt, oc2=oc2, po=po, ob=ob, sb2=sb2: e.scalar_tensor_tensor(
                            ob[:, :], po, 1.0,
                            xsp[tt][:, oc2 * 512:(oc2 + 1) * 512],
                            ALU.mult, ALU.add))
                        P2.wait("Y", vo)
                        st = P2.dma("Y", lambda e, s, tt=tt, oc2=oc2, sp=sp, ob=ob: e.dma_start(
                            out=out[sp * SPAN + tt * 128: sp * SPAN + (tt + 1) * 128,
                                    oc2 * 512:(oc2 + 1) * 512],
                            in_=ob[:, :]))
                        P2.wait("T", vo)
                        P2.wait("V", st)
                spsnap = P2.snap()
                for eng in ("T", "V", "S"):
                    P2.barrier_from(eng, spsnap)

            final = P2.snap()

            def fin(e, s, final=final):
                e.wait_ge(s["dY"], final["dY"])
            P2.ops.append(("Y", fin))

            P2.emit(blk1, sems)

    return nc


def kernel(L0_attn_x, L0_attn_wq, L0_attn_bq, L0_attn_wk, L0_attn_bk,
           L0_attn_wv, L0_attn_bv, L0_attn_wo, L0_attn_bo,
           L0_ffn_W1, L0_ffn_b1, L0_ffn_W2, L0_ffn_b2, L0_ffn_scale):
    x = np.ascontiguousarray(np.asarray(L0_attn_x, dtype=np.float32))
    wq = np.asarray(L0_attn_wq, np.float32)
    wk = np.asarray(L0_attn_wk, np.float32)
    wv = np.asarray(L0_attn_wv, np.float32)
    wo = np.asarray(L0_attn_wo, np.float32)
    W1 = np.asarray(L0_ffn_W1, np.float32)
    W2 = np.asarray(L0_ffn_W2, np.float32)
    bq = np.asarray(L0_attn_bq, np.float32)
    bk = np.asarray(L0_attn_bk, np.float32)
    bv = np.asarray(L0_attn_bv, np.float32)
    bo = np.asarray(L0_attn_bo, np.float32)
    b1 = np.asarray(L0_ffn_b1, np.float32)
    b2 = np.asarray(L0_ffn_b2, np.float32)
    scale = np.float32(np.asarray(L0_ffn_scale))

    shared = {
        "wqT": np.ascontiguousarray(wq.T), "wkT": np.ascontiguousarray(wk.T),
        "wvT": np.ascontiguousarray(wv.T), "woT": np.ascontiguousarray(wo.T),
        "w1T": np.ascontiguousarray(W1.T), "w2T": np.ascontiguousarray(W2.T),
        "bqc": np.ascontiguousarray(bq.reshape(8, 128).T),
        "bkc": np.ascontiguousarray(bk.reshape(8, 128).T),
        "bvr": bv.reshape(1, E).copy(), "bor": bo.reshape(1, E).copy(),
        "b2r": b2.reshape(1, E).copy(),
        "b1c": np.ascontiguousarray(b1.reshape(32, 128).T),
        "sclc": np.broadcast_to(scale, (16, 1)).copy(),
    }
    in_maps = []
    for core in range(8):
        b, wi = core // 4, core % 4
        t0 = wi * TOWN
        xhc = np.zeros((TLOC, E), np.float32)
        if wi > 0:
            xhc[0:512] = x[b, t0 - 512:t0]
        xhc[512:512 + TOWN] = x[b, t0:t0 + TOWN]
        if wi < 3:
            xhc[512 + TOWN:TLOC] = x[b, t0 + TOWN:t0 + TOWN + 512]
        lam = np.ones(NCH, np.float32)
        if wi == 0:
            lam[0:4] = 0
        if wi == 3:
            lam[20:24] = 0
        mu = np.zeros(NOWN, np.float32)
        if wi == 0:
            mu[0] = 1
        if wi == 3:
            mu[3] = 1
        psel = np.zeros((16, 16 * 65), np.float32)
        for h in range(H):
            psel[h, h * 65:h * 65 + 64] = -1.0
            psel[h, h * 65 + 64] = 1.0
        lnm = np.full((16, NOWN), -1e30, np.float32)
        for w in range(NOWN):
            if mu[w] > 0:
                lnm[:, w] = 0.0
        in_maps.append(dict(shared,
                            xh=xhc,
                            x16=np.ascontiguousarray(x[b, 1:17]),
                            lamt=np.ascontiguousarray(np.broadcast_to(lam, (128, NCH))),
                            padsel=psel, lnmu=lnm))

    nc = build_program()
    res = run_bass_kernel_spmd(nc, in_maps, core_ids=list(range(8)))
    outp = np.zeros((2, 8192, E), np.float32)
    for core in range(8):
        b, wi = core // 4, core % 4
        outp[b, wi * TOWN:(wi + 1) * TOWN] = res.results[core]["out"]
    return outp


# version 18 (final)
# speedup vs baseline: 1.0815x; 1.0815x over previous
import math
import numpy as np
import concourse.bass as bass
import concourse.mybir as mybir
from concourse.bass_utils import run_bass_kernel_spmd

F32 = mybir.dt.float32
BF16 = mybir.dt.bfloat16
AF = mybir.ActivationFunctionType
ALU = mybir.AluOpType
AX = mybir.AxisListType

E = 1024
H = 16
WS = 512
NOWN = 4
NLW = 6
TLOC = NLW * WS
NCH = TLOC // 128
TOWN = NOWN * WS
LN512 = float(math.log(512.0))
C2 = 30.0
HFF = 4096
SPAN = 256
NSPAN = TOWN // SPAN
GROUPS = [(0, 3), (3, 6), (6, 9), (9, 12)]


class Plan:
    def __init__(self, cnt=None):
        self.ops = []
        self.cnt = dict(cnt) if cnt else {"T": 0, "V": 0, "S": 0, "G": 0, "dY": 0, "dG": 0}
        self.base = dict(self.cnt)
        self.cover = {}
        self.spacers = {}
        self.qstream = {"Y": [], "G": []}
        self.defer = []

    SCALE = 256

    def op(self, eng, fn):
        self.cnt[eng] += self.SCALE
        tgt = self.cnt[eng]

        def run(e, sems, fn=fn, eng=eng):
            fn(e, sems).then_inc(sems[eng], self.SCALE)
        self.ops.append((eng, run))
        return (eng, tgt)

    def dma(self, eng, fn):
        sem = {"Y": "dY", "G": "dG"}[eng]
        self.cnt[sem] += 16
        tgt = self.cnt[sem]

        def run(e, sems, fn=fn, sem=sem):
            fn(e, sems).then_inc(sems[sem], 16)
        self.qstream[eng].append(16)
        self.ops.append((eng, run))
        return (sem, tgt)

    FLUSH_K = 1

    def _spacer(self, sem):
        self.cnt[sem] += self.SCALE

        def run(e, sems, sem=sem):
            e.drain()
            e.sem_inc(sems[sem], self.SCALE)
        self.ops.append((sem, run))

    def wait(self, eng, dep):
        sem, val = dep
        if val <= 0:
            return
        if sem in getattr(self, "spacers", {}) and sem != eng:
            cu, sv = self.cover.get(sem, (0, 0))
            if val > cu:
                covered = self.cnt[sem]
                for _ in range(self.FLUSH_K):
                    self._spacer(sem)
                self.cover[sem] = (covered, self.cnt[sem])
                cu, sv = self.cover[sem]
            val = sv

        if sem in ("dY", "dG"):
            cell = [val]
            self.defer.append((sem, cell))

            def run(e, sems, sem=sem, cell=cell):
                e.wait_ge(sems[sem], cell[0])
        else:
            def run(e, sems, sem=sem, val=val):
                e.wait_ge(sems[sem], val)
        if eng in ("Y", "G"):
            self.qstream[eng].append("w")
        self.ops.append((eng, run))

    def waits(self, eng, deps):
        best = {}
        for (sem, val) in deps:
            if val > 0:
                best[sem] = max(best.get(sem, 0), val)
        for sem, val in best.items():
            self.wait(eng, (sem, val))

    def snap(self):
        return dict(self.cnt)

    def barrier_from(self, eng, snap):
        self.waits(eng, [(k, v) for k, v in snap.items() if k != {"Y": "dY", "G": "dG"}.get(eng, eng)])

    def finalize(self):
        for q, sem in (("Y", "dY"), ("G", "dG")):
            cum = self.base[sem]
            bs = [cum]
            for it in self.qstream[q]:
                if it == "w":
                    bs.append(cum)
                else:
                    cum += it
            bs.append(cum)
            bs = sorted(set(bs))
            for s2, cell in self.defer:
                if s2 != sem:
                    continue
                v = cell[0]
                cell[0] = next((b for b in bs if b >= v), bs[-1])
        self.defer = []

    def emit(self, blk, sems):
        self.finalize()

        def mk(name):
            def body(e):
                for eng, fn in self.ops:
                    if eng == name:
                        fn(e, sems)
            return body
        blk.tensor(mk("T"))
        blk.vector(mk("V"))
        blk.scalar(mk("S"))
        blk.gpsimd(mk("G"))
        blk.sync(mk("Y"))


def build_program():
    nc = bass.Bass("TRN2", target_bir_lowering=False)

    xh = nc.dram_tensor("xh", [TLOC, E], F32, kind="ExternalInput").ap()
    x16 = nc.dram_tensor("x16", [16, E], F32, kind="ExternalInput").ap()
    wqT = nc.dram_tensor("wqT", [E, E], F32, kind="ExternalInput").ap()
    wkT = nc.dram_tensor("wkT", [E, E], F32, kind="ExternalInput").ap()
    wvT = nc.dram_tensor("wvT", [E, E], F32, kind="ExternalInput").ap()
    woT = nc.dram_tensor("woT", [E, E], F32, kind="ExternalInput").ap()
    w1T = nc.dram_tensor("w1T", [E, HFF], F32, kind="ExternalInput").ap()
    w2T = nc.dram_tensor("w2T", [HFF, E], F32, kind="ExternalInput").ap()
    bqc = nc.dram_tensor("bqc", [128, 8], F32, kind="ExternalInput").ap()
    bkc = nc.dram_tensor("bkc", [128, 8], F32, kind="ExternalInput").ap()
    bvr = nc.dram_tensor("bvr", [1, E], F32, kind="ExternalInput").ap()
    b1c = nc.dram_tensor("b1c", [128, 32], F32, kind="ExternalInput").ap()
    bor = nc.dram_tensor("bor", [1, E], F32, kind="ExternalInput").ap()
    b2r = nc.dram_tensor("b2r", [1, E], F32, kind="ExternalInput").ap()
    sclc = nc.dram_tensor("sclc", [16, 1], F32, kind="ExternalInput").ap()
    lamt = nc.dram_tensor("lamt", [128, NCH], F32, kind="ExternalInput").ap()
    padsel = nc.dram_tensor("padsel", [16, 16 * 65], F32, kind="ExternalInput").ap()
    lnmu = nc.dram_tensor("lnmu", [16, NOWN], F32, kind="ExternalInput").ap()
    out = nc.dram_tensor("out", [TOWN, E], F32, kind="ExternalOutput").ap()
    xnTs = nc.dram_tensor("xnTs", [E, TOWN], BF16, kind="Internal").ap()
    attnTs = nc.dram_tensor("attnTs", [E, TOWN], BF16, kind="Internal").ap()

    import contextlib
    es0 = contextlib.ExitStack()
    with es0:
        CC0 = es0.enter_context
        semT = CC0(nc.semaphore("semT"))
        semV = CC0(nc.semaphore("semV"))
        semS = CC0(nc.semaphore("semS"))
        semG = CC0(nc.semaphore("semG"))
        semdY = CC0(nc.semaphore("semdY"))
        semdG = CC0(nc.semaphore("semdG"))
        psA = CC0(nc.psum_tensor("psA", [128, 1536], F32))
        psB = CC0(nc.psum_tensor("psB", [128, 1536], F32))
        psC = CC0(nc.psum_tensor("psC", [128, 512], F32))
        psT = CC0(nc.psum_tensor("psT", [128, 1024], BF16))
        ones = CC0(nc.sbuf_tensor("ones", [128, 128], BF16))
        ident = CC0(nc.sbuf_tensor("ident", [128, 128], BF16))
        tlow = CC0(nc.sbuf_tensor("tlow", [128, 128], BF16))
        thig = CC0(nc.sbuf_tensor("thig", [128, 128], BF16))
        iotaf = CC0(nc.sbuf_tensor("iotaf", [128, 128], F32))
        iotap = CC0(nc.sbuf_tensor("iotap", [128, 128], F32))
        xn16 = CC0(nc.sbuf_tensor("xn16", [16, E], BF16))
        eoT_ = CC0(nc.sbuf_tensor("eoT", [128, 8 * 16], BF16))
        E2s = CC0(nc.sbuf_tensor("E2s", [16, E], BF16))
        EVN = CC0(nc.sbuf_tensor("EVN", [1, 4 * E], BF16))
        ODP = CC0(nc.sbuf_tensor("ODP", [128, 32 * 16], BF16))
        sclb = CC0(nc.sbuf_tensor("sclb", [16, 1], F32))
        lamb = CC0(nc.sbuf_tensor("lamb", [128, NCH], F32))
        lnmuS = CC0(nc.sbuf_tensor("lnmuS", [16, NOWN], F32))
        smll = CC0(nc.sbuf_tensor("smll", [128, 64], F32))
        sm2 = CC0(nc.sbuf_tensor("sm2", [128, 64], F32))
        junkS = CC0(nc.sbuf_tensor("junkS", [128, 2], F32))
        junkG = CC0(nc.sbuf_tensor("junkG", [128, 2], F32))
        eoT = [eoT_[:, k * 16:(k + 1) * 16] for k in range(8)]
        sems = dict(T=semT, V=semV, S=semS, G=semG, dY=semdY, dG=semdG)

        def sl(i):
            return smll[0:16, i:i + 1]

        P = Plan()
        P.spacers = {
            "V": lambda e: e.tensor_copy(smll[:, 62:63], smll[:, 63:64]),
            "S": lambda e: e.copy(junkS[:, 0:1], junkS[:, 1:2]),
            "G": lambda e: e.memset(junkG[:, 0:1], 0.0),
        }

        def VG():
            P.op("V", lambda e, s: e.tensor_copy(smll[:, 62:63], smll[:, 63:64]))

        es1 = contextlib.ExitStack()
        with es1:
            CC1 = es1.enter_context
            wq_ = CC1(nc.sbuf_tensor("wq", [128, 8 * E], BF16))
            wk_ = CC1(nc.sbuf_tensor("wk", [128, 8 * E], BF16))
            wv_ = CC1(nc.sbuf_tensor("wv", [128, 8 * E], BF16))
            bqs = CC1(nc.sbuf_tensor("bqs", [128, 8], F32))
            bks = CC1(nc.sbuf_tensor("bks", [128, 8], F32))
            bvs = CC1(nc.sbuf_tensor("bvs", [1, E], BF16))
            pads = CC1(nc.sbuf_tensor("pads", [16, 16 * 65], BF16))
            xt_ = CC1(nc.sbuf_tensor("xt", [128, 4 * E], F32))
            xsq_ = CC1(nc.sbuf_tensor("xsq", [128, 2 * E], F32))
            xntm_ = CC1(nc.sbuf_tensor("xntm", [128, 4 * E], BF16))
            xnT_ = CC1(nc.sbuf_tensor("xnT", [128, 8 * 2 * WS], BF16))
            kt_ = CC1(nc.sbuf_tensor("kt", [128, 8 * 3 * WS], BF16))
            vS = CC1(nc.sbuf_tensor("vS", [128, 12 * 1088], BF16))
            qt_ = CC1(nc.sbuf_tensor("qt", [128, 8 * WS], BF16))
            probs_ = CC1(nc.sbuf_tensor("probs", [128, 4 * 1536], BF16))
            rs = CC1(nc.sbuf_tensor("rs", [128, 64], F32))
            rsb = CC1(nc.sbuf_tensor("rsb", [128, 64], BF16))
            atw_ = CC1(nc.sbuf_tensor("atw", [128, 8 * WS], BF16))
            mkr = CC1(nc.sbuf_tensor("mkr", [16, 8], F32))
            scrA = CC1(nc.sbuf_tensor("scrA", [128, 512], F32))
            scrB = CC1(nc.sbuf_tensor("scrB", [128, 512], BF16))
            rbsb = CC1(nc.sbuf_tensor("rbsb", [64, 512], F32))
            Sg = CC1(nc.sbuf_tensor("Sg", [1, 64], F32))
            Sg16 = CC1(nc.sbuf_tensor("Sg16", [1, 16], BF16))
            Sg16f = CC1(nc.sbuf_tensor("Sg16f", [1, 16], F32))
            padp = CC1(nc.sbuf_tensor("padp", [16, 512], BF16))
            nmrow = CC1(nc.sbuf_tensor("nmrow", [1, 16], BF16))
            bcA = CC1(nc.sbuf_tensor("bcA", [128, 16], F32))
            bcB = CC1(nc.sbuf_tensor("bcB", [128, 16], F32))
            bcC = CC1(nc.sbuf_tensor("bcC", [128, 16], F32))
            Jsl = CC1(nc.sbuf_tensor("Jsl", [128, 31], BF16))
            blk0 = CC1(nc.Block())
            wq = [wq_[:, k * E:(k + 1) * E] for k in range(8)]
            wk = [wk_[:, k * E:(k + 1) * E] for k in range(8)]
            wv = [wv_[:, k * E:(k + 1) * E] for k in range(8)]
            xt = [xt_[:, i * E:(i + 1) * E] for i in range(4)]
            x16f = xt_[0:16, 0:E]
            x16sq = xsq_[0:16, 0:E]
            xsq = [xsq_[:, i * E:(i + 1) * E] for i in range(2)]
            xntm = [xntm_[:, i * E:(i + 1) * E] for i in range(4)]
            xnT = [xnT_[:, k * 2 * WS:(k + 1) * 2 * WS] for k in range(8)]
            kt = [kt_[:, k * 3 * WS:(k + 1) * 3 * WS] for k in range(8)]
            qt = [qt_[:, k * WS:(k + 1) * WS] for k in range(8)]
            sqb = [xntm_[:, k * WS:(k + 1) * WS] for k in range(8)]
            probs = [probs_[:, i * 1536:(i + 1) * 1536] for i in range(4)]
            r1scr = probs_[:, 0:1536]
            atw = [atw_[:, k * WS:(k + 1) * WS] for k in range(8)]

            for k in range(8):
                P.dma("G", lambda e, s, k=k: e.dma_start(out=wq[k], in_=wqT[k * 128:(k + 1) * 128, :]))
                P.dma("G", lambda e, s, k=k: e.dma_start(out=wk[k], in_=wkT[k * 128:(k + 1) * 128, :]))
                P.dma("G", lambda e, s, k=k: e.dma_start(out=wv[k], in_=wvT[k * 128:(k + 1) * 128, :]))
            P.dma("G", lambda e, s: e.dma_start(out=bvs[:, :], in_=bvr))
            P.dma("G", lambda e, s: e.dma_start(out=pads[:, :], in_=padsel))
            P.dma("Y", lambda e, s: e.dma_start(out=bqs[:, :], in_=bqc))
            P.dma("Y", lambda e, s: e.dma_start(out=bks[:, :], in_=bkc))
            P.dma("Y", lambda e, s: e.dma_start(out=sclb[:, :], in_=sclc))
            P.dma("Y", lambda e, s: e.dma_start(out=lamb[:, :], in_=lamt))
            x16ld = P.dma("Y", lambda e, s: e.dma_start(out=x16f, in_=x16))
            P.dma("Y", lambda e, s: e.dma_start(out=lnmuS[:, :], in_=lnmu))

            g1 = P.op("G", lambda e, s: e.iota(iotaf[:, :], [[1, 128]], channel_multiplier=0,
                                               allow_small_or_imprecise_dtypes=True))
            g2 = P.op("G", lambda e, s: e.iota(iotap[:, :], [[0, 128]], channel_multiplier=1,
                                               allow_small_or_imprecise_dtypes=True))
            g3 = P.op("G", lambda e, s: e.memset(Jsl[:, :], 0.0))
            g4 = P.op("G", lambda e, s: e.memset(ODP[:, :], 0.0))
            P.waits("V", [g1, g2, g3])
            c1 = P.op("V", lambda e, s: e.tensor_tensor(ident[:, :], iotaf[:, :], iotap[:, :], ALU.is_equal))
            P.op("V", lambda e, s: e.tensor_tensor(tlow[:, :], iotaf[:, :], iotap[:, :], ALU.is_le))
            P.op("V", lambda e, s: e.tensor_tensor(thig[:, :], iotaf[:, :], iotap[:, :], ALU.is_ge))
            P.op("V", lambda e, s: e.memset(Jsl[0:64, 15:16], 1.0))
            P.op("V", lambda e, s: e.memset(Jsl[64:128, 16:17], 1.0))
            P.op("V", lambda e, s: e.memset(ones[:, :], 1.0))

            P.wait("G", x16ld)
            gx = P.op("G", lambda e, s: e.tensor_mul(x16sq[:, :], x16f[:, :], x16f[:, :]))
            P.wait("V", x16ld)
            P.op("V", lambda e, s: e.reduce_sum(sm2[0:16, 0:1], x16f[:, :], axis=AX.X))
            P.wait("V", gx)
            P.op("V", lambda e, s: e.reduce_sum(sm2[0:16, 1:2], x16sq[:, :], axis=AX.X))
            P.op("V", lambda e, s: e.tensor_scalar_mul(sm2[0:16, 2:3], sm2[0:16, 0:1], 1.0 / E))
            P.op("V", lambda e, s: e.tensor_scalar_mul(sm2[0:16, 3:4], sm2[0:16, 1:2], 1.0 / E))
            P.op("V", lambda e, s: e.tensor_mul(sm2[0:16, 4:5], sm2[0:16, 2:3], sm2[0:16, 2:3]))
            VG()
            P.op("V", lambda e, s: e.tensor_sub(sm2[0:16, 5:6], sm2[0:16, 3:4], sm2[0:16, 4:5]))
            VG()
            v7 = P.op("V", lambda e, s: e.tensor_scalar_add(sm2[0:16, 6:7], sm2[0:16, 5:6], 1e-5))
            P.wait("S", v7)
            s1 = P.op("S", lambda e, s: e.sqrt(sm2[0:16, 7:8], sm2[0:16, 6:7]))
            P.wait("V", s1)
            P.op("V", lambda e, s: e.reciprocal(sm2[0:16, 8:9], sm2[0:16, 7:8]))
            VG()
            P.op("V", lambda e, s: e.tensor_mul(sm2[0:16, 9:10], sm2[0:16, 2:3], sm2[0:16, 8:9]))
            VG()
            v10 = P.op("V", lambda e, s: e.tensor_scalar_mul(sm2[0:16, 10:11], sm2[0:16, 9:10], -1.0))
            P.wait("S", v10)
            sx = P.op("S", lambda e, s: e.activation(xn16[:, :], x16f[:, :], AF.Identity,
                                                     bias=sm2[0:16, 10:11], scale=sm2[0:16, 8:9]))
            P.waits("T", [sx, c1])
            trl = []
            for k in range(8):
                trl.append(P.op("T", lambda e, s, k=k: e.matmul(
                    psT[:, k * 128:k * 128 + 16], xn16[:, k * 128:(k + 1) * 128], ident[0:16, 0:16],
                    start=(k == 0), stop=(k == 7), is_transpose=True)))
            P.waits("V", trl[-1:])
            ev = []
            for k in range(8):
                ev.append(P.op("V", lambda e, s, k=k: e.tensor_copy(eoT[k][:, :], psT[:, k * 128:k * 128 + 16])))
            ez = P.op("V", lambda e, s: e.memset(E2s[:, :], 0.0))
            P.waits("Y", [sx, ez])
            for i in range(4):
                P.dma("Y", lambda e, s, i=i: e.dma_start(out=EVN[0:1, i * E:(i + 1) * E], in_=xn16[2 * i:2 * i + 1, :]))
            for m in range(16):
                i, jpp = m // 4, m % 4
                P.dma("Y", lambda e, s, i=i, jpp=jpp, m=m: e.dma_start(
                    out=E2s[m:m + 1, i * 256:(i + 1) * 256],
                    in_=xn16[8 + 2 * i:9 + 2 * i, jpp:jpp + E - 4 + 1:4],
                    allow_slow_non_contiguous=True))
            P.waits("V", [g4] + ev)
            for hk in range(32):
                k, jpp = hk % 8, hk // 8
                P.op("V", lambda e, s, hk=hk, k=k, jpp=jpp: e.tensor_copy(
                    bass.AP(ODP, hk * 16 + jpp, [[32 * 16, 128], [4, 4], [1, 1]]),
                    eoT[k][:, 9:16:2]))

            snap = P.snap()
            for eng in ("T", "V", "S", "G", "Y"):
                P.barrier_from(eng, snap)

            stage_done = {}
            sqred = {}
            prod_done = {}

            def produce(lw):
                slot2 = lw % 2
                slot3 = lw % 3
                if lw - 1 in prod_done:
                    P.barrier_from("Y", prod_done[lw - 1])
                xload = []
                for tc in range(4):
                    xload.append(P.dma("Y", lambda e, s, lw=lw, tc=tc: e.dma_start(
                        out=xt[tc], in_=xh[lw * 512 + tc * 128: lw * 512 + (tc + 1) * 128, :])))
                for tc in range(4):
                    xb = xt[tc]
                    sq = xsq[tc % 2]
                    P.wait("G", xload[tc])
                    if tc >= 2:
                        P.wait("G", sqred[tc - 2])
                    gq = P.op("G", lambda e, s, xb=xb, sq=sq: e.tensor_mul(sq, xb, xb))
                    P.wait("V", xload[tc])
                    P.op("V", lambda e, s, xb=xb, tc=tc: e.reduce_sum(sm2[:, 16 + tc:17 + tc], xb, axis=AX.X))
                    P.wait("V", gq)
                    sqred[tc] = P.op("V", lambda e, s, sq=sq, tc=tc: e.reduce_sum(sm2[:, 20 + tc:21 + tc], sq, axis=AX.X))
                P.op("V", lambda e, s: e.tensor_scalar_mul(sm2[:, 24:28], sm2[:, 16:20], 1.0 / E))
                P.op("V", lambda e, s: e.tensor_scalar_mul(sm2[:, 28:32], sm2[:, 20:24], 1.0 / E))
                P.op("V", lambda e, s: e.tensor_mul(sm2[:, 32:36], sm2[:, 24:28], sm2[:, 24:28]))
                VG()
                P.op("V", lambda e, s: e.tensor_sub(sm2[:, 36:40], sm2[:, 28:32], sm2[:, 32:36]))
                VG()
                a7 = P.op("V", lambda e, s: e.tensor_scalar_add(sm2[:, 40:44], sm2[:, 36:40], 1e-5))
                P.wait("S", a7)
                b1_ = P.op("S", lambda e, s: e.sqrt(sm2[:, 44:48], sm2[:, 40:44]))
                P.wait("V", b1_)
                P.op("V", lambda e, s: e.reciprocal(sm2[:, 48:52], sm2[:, 44:48]))
                VG()
                P.op("V", lambda e, s: e.tensor_mul(sm2[:, 52:56], sm2[:, 24:28], sm2[:, 48:52]))
                VG()
                P.op("V", lambda e, s: e.tensor_scalar_mul(sm2[:, 56:60], sm2[:, 52:56], -1.0))
                evacs = []
                for tc in range(4):
                    xb = xt[tc]
                    VG()
                    nrm = P.op("V", lambda e, s, xb=xb, tc=tc: e.tensor_scalar(
                        xntm[tc], xb, sm2[:, 56 + tc:57 + tc], sm2[:, 48 + tc:49 + tc],
                        ALU.add, ALU.mult))
                    P.wait("T", nrm)
                    trl2 = []
                    for k in range(8):
                        trl2.append(P.op("T", lambda e, s, tc=tc, k=k: e.matmul(
                            psT[:, k * 128:(k + 1) * 128], xntm[tc][:, k * 128:(k + 1) * 128], ident[:, :],
                            start=(k == 0), stop=(k == 7), is_transpose=True)))
                    P.waits("V", trl2[-1:])
                    for k in range(8):
                        col = slot2 * 512 + tc * 128
                        evacs.append(P.op("V", lambda e, s, k=k, col=col: e.tensor_copy(
                            xnT[k][:, col:col + 128], psT[:, k * 128:(k + 1) * 128])))
                if 1 <= lw <= 4:
                    P.waits("Y", evacs[-8:])
                    for k in range(8):
                        P.dma("Y", lambda e, s, k=k, lw=lw, slot2=slot2: e.dma_start(
                            out=xnTs[k * 128:(k + 1) * 128, (lw - 1) * 512:lw * 512],
                            in_=xnT[k][:, slot2 * 512:slot2 * 512 + 512]))
                P.waits("T", evacs)
                banks6m = [(psA, 0), (psA, 512), (psA, 1024), (psB, 0), (psB, 512), (psB, 1024)]
                banks6 = [h_[:, o_:o_ + 512] for (h_, o_) in banks6m]
                kdone = []
                for oc in range(8):
                    if oc >= 6:
                        P.wait("T", kdone[oc - 6])
                    half = banks6[oc % 6]
                    mm = []
                    for k in range(8):
                        mm.append(P.op("T", lambda e, s, oc=oc, k=k, half=half, slot2=slot2: e.matmul(
                            half, wk[k][:, oc * 128:(oc + 1) * 128],
                            xnT[k][:, slot2 * 512:slot2 * 512 + 512],
                            start=(k == 0), stop=(k == 7))))
                    P.waits("V", mm[-1:])
                    kdone.append(P.op("V", lambda e, s, oc=oc, half=half, slot3=slot3: e.tensor_scalar(
                        kt[oc][:, slot3 * 512:slot3 * 512 + 512], half,
                        bks[:, oc:oc + 1], None, ALU.add)))
                vprev = []
                for tc in range(4):
                    kc = lw * 4 + tc
                    slot12 = kc % 12
                    for oc2 in range(2):
                        idx = tc * 2 + oc2
                        if idx >= 6:
                            P.wait("T", vprev[idx - 6])
                        else:
                            P.wait("T", kdone[idx + 6 if idx < 2 else idx])
                        half = banks6[idx % 6]
                        hh, ho = banks6m[idx % 6]
                        mm = []
                        for k in range(8):
                            mm.append(P.op("T", lambda e, s, tc=tc, oc2=oc2, k=k, half=half, slot2=slot2: e.matmul(
                                half,
                                xnT[k][:, slot2 * 512 + tc * 128: slot2 * 512 + (tc + 1) * 128],
                                wv[k][:, oc2 * 512:(oc2 + 1) * 512],
                                start=(k == 0), stop=False)))
                        mm.append(P.op("T", lambda e, s, oc2=oc2, half=half: e.matmul(
                            half, ones[0:1, 0:128], bvs[0:1, oc2 * 512:(oc2 + 1) * 512],
                            start=False, stop=True)))
                        P.waits("V", mm[-1:])
                        base = slot12 * 1088 + oc2 * 8 * 68
                        vprev.append(P.op("V", lambda e, s, base=base, kc=kc, hh=hh, ho=ho: e.tensor_scalar(
                            bass.AP(vS, base, [[12 * 1088, 128], [68, 8], [1, 64]]),
                            bass.AP(hh, ho, [[1536, 128], [64, 8], [1, 64]]),
                            lamb[:, kc:kc + 1], None, ALU.mult)))
                    P.op("V", lambda e, s, slot12=slot12, kc=kc: e.tensor_scalar(
                        bass.AP(vS, slot12 * 1088 + 64, [[12 * 1088, 128], [68, 16], [1, 1]]),
                        bass.AP(ones, 0, [[128, 128], [1, 16], [1, 1]]),
                        lamb[:, kc:kc + 1], None, ALU.mult))
                P.waits("G", kdone + vprev[-1:])
                sqd = []
                for k in range(8):
                    sqd.append(P.op("G", lambda e, s, k=k, slot3=slot3: e.tensor_mul(
                        sqb[k][:, :], kt[k][:, slot3 * 512:slot3 * 512 + 512],
                        kt[k][:, slot3 * 512:slot3 * 512 + 512])))
                P.waits("T", sqd + vprev)
                nm = []
                for k in range(8):
                    nm.append(P.op("T", lambda e, s, k=k: e.matmul(
                        psC[0:16, 0:512], Jsl[:, 15 - 2 * k:31 - 2 * k], sqb[k][:, :],
                        start=(k == 0), stop=(k == 7))))
                P.waits("V", nm[-1:])
                P.op("V", lambda e, s, lw=lw: e.reduce_max(mkr[:, lw:lw + 1], psC[0:16, 0:512], axis=AX.X))
                prod_done[lw] = P.snap()
                for eng in ("T", "V", "S", "G"):
                    P.barrier_from(eng, prod_done[lw])

            def attend(w):
                lw = w + 1
                slot2 = lw % 2
                qdone = []
                banks6 = [psA[:, 0:512], psA[:, 512:1024], psA[:, 1024:1536],
                          psB[:, 0:512], psB[:, 512:1024], psB[:, 1024:1536]]
                for oc in range(8):
                    if oc >= 6:
                        P.wait("T", qdone[oc - 6])
                    half = banks6[oc % 6]
                    mm = []
                    for k in range(8):
                        mm.append(P.op("T", lambda e, s, oc=oc, k=k, half=half, slot2=slot2: e.matmul(
                            half, wq[k][:, oc * 128:(oc + 1) * 128],
                            xnT[k][:, slot2 * 512:slot2 * 512 + 512],
                            start=(k == 0), stop=(k == 7))))
                    P.waits("V", mm[-1:])
                    qdone.append(P.op("V", lambda e, s, oc=oc, half=half: e.tensor_scalar(
                        qt[oc][:, :], half, bqs[:, oc:oc + 1], None, ALU.add)))
                P.waits("G", qdone)
                sqd = []
                for k in range(8):
                    sqd.append(P.op("G", lambda e, s, k=k: e.tensor_mul(sqb[k][:, :], qt[k][:, :], qt[k][:, :])))
                P.waits("T", sqd + qdone)
                nm = []
                for k in range(8):
                    nm.append(P.op("T", lambda e, s, k=k: e.matmul(
                        psC[0:16, 0:512], Jsl[:, 15 - 2 * k:31 - 2 * k], sqb[k][:, :],
                        start=(k == 0), stop=(k == 7))))
                P.waits("V", nm[-1:])
                vq = P.op("V", lambda e, s: e.reduce_max(sl(0), psC[0:16, 0:512], axis=AX.X))
                sg = []
                for k in range(8):
                    sg.append(P.op("T", lambda e, s, k=k: e.matmul(
                        psB[0:16, 0:512], Jsl[:, 15 - 2 * k:31 - 2 * k], qt[k][:, :],
                        start=(k == 0), stop=(k == 7))))
                P.waits("V", sg[-1:])
                sgv = P.op("V", lambda e, s: e.tensor_copy(scrA[0:16, :], psB[0:16, 0:512]))
                P.wait("V", vq)
                P.op("V", lambda e, s, lw=lw: e.tensor_max(sl(1), mkr[:, lw - 1:lw], mkr[:, lw:lw + 1]))
                VG()
                P.op("V", lambda e, s, lw=lw: e.tensor_max(sl(2), sl(1), mkr[:, lw + 1:lw + 2]))
                VG()
                P.op("V", lambda e, s: e.tensor_scalar_max(sl(3), sl(2), 64.0))
                VG()
                vb = P.op("V", lambda e, s: e.tensor_mul(sl(4), sl(0), sl(3)))
                P.wait("S", vb)
                sb = P.op("S", lambda e, s: e.sqrt(sl(5), sl(4)))
                P.wait("V", sb)
                P.op("V", lambda e, s: e.reciprocal(sl(6), sl(5)))
                VG()
                P.op("V", lambda e, s: e.tensor_scalar_mul(sl(7), sl(6), 40.0))
                VG()
                P.op("V", lambda e, s: e.tensor_scalar_min(sl(8), sl(7), 1.0))
                VG()
                P.op("V", lambda e, s: e.tensor_copy(scrB[0:16, 0:1], sl(8)))
                VG()
                P.op("V", lambda e, s: e.tensor_copy(sl(9), scrB[0:16, 0:1]))
                VG()
                P.op("V", lambda e, s: e.reciprocal(sl(10), sl(9)))
                P.op("V", lambda e, s: e.tensor_scalar_mul(sl(22), sl(9), -1.0))
                P.op("V", lambda e, s: e.tensor_mul(sl(11), sl(5), sl(9)))
                VG()
                P.op("V", lambda e, s: e.tensor_scalar_mul(sl(11), sl(11), -1.0))
                P.op("V", lambda e, s: e.tensor_scalar(sl(12), sl(5), -1.0, LN512, ALU.mult, ALU.add))
                VG()
                P.op("V", lambda e, s: e.tensor_mul(sl(12), sl(12), sl(9)))
                VG()
                r11 = P.op("V", lambda e, s: e.tensor_copy(scrB[0:16, 1:2], sl(11)))
                P.waits("T", [r11])
                t1 = P.op("T", lambda e, s: e.matmul(psT[0:1, 0:16], scrB[0:16, 0:1], ident[0:16, 0:16],
                                                     start=True, stop=True, is_transpose=True))
                P.waits("V", [t1])
                u1 = P.op("V", lambda e, s: e.tensor_copy(nmrow[0:1, 0:16], psT[0:1, 0:16]))
                P.waits("T", [u1])
                t3 = P.op("T", lambda e, s: e.matmul(psC[:, 0:16], ones[0:1, 0:128], nmrow[0:1, 0:16],
                                                     start=True, stop=True))
                P.waits("V", [t3])
                u3 = P.op("V", lambda e, s: e.tensor_copy(bcB[:, :], psC[:, 0:16]))
                P.waits("T", [u3])
                t2 = P.op("T", lambda e, s: e.matmul(psT[0:1, 0:16], scrB[0:16, 1:2], ident[0:16, 0:16],
                                                     start=True, stop=True, is_transpose=True))
                P.waits("V", [t2])
                u2 = P.op("V", lambda e, s: e.tensor_copy(nmrow[0:1, 0:16], psT[0:1, 0:16]))
                P.waits("T", [u2])
                t4 = P.op("T", lambda e, s: e.matmul(psC[:, 0:16], ones[0:1, 0:128], nmrow[0:1, 0:16],
                                                     start=True, stop=True))
                P.waits("V", [t4])
                u4 = P.op("V", lambda e, s: e.tensor_copy(bcC[:, :], psC[:, 0:16]))
                snap0 = P.snap()
                for eng in ("T", "S", "V"):
                    P.barrier_from(eng, snap0)

                pe = P.op("S", lambda e, s: e.activation(
                    padp[:, :], scrA[0:16, 0:512], AF.Exp, bias=sl(12), scale=sl(22),
                    accum_out=sl(13)))

                lastex = {0: None, 1: None}
                pg = 0
                for h in range(H):
                    kh = h // 2
                    ro = (h % 2) * 64
                    for g, (j0, j1) in enumerate(GROUPS):
                        ps = psA if pg == 0 else psB
                        if lastex[pg] is not None:
                            P.wait("T", lastex[pg])
                        mm = []
                        for j in range(j0, j1):
                            kc = w * 4 + j
                            col = ((kc // 4) % 3) * 512 + (kc % 4) * 128
                            mm.append(P.op("T", lambda e, s, kh=kh, ro=ro, col=col, j=j, j0=j0, ps=ps: e.matmul(
                                ps[:, (j - j0) * 512:(j - j0 + 1) * 512],
                                kt[kh][ro:ro + 64, col:col + 128],
                                qt[kh][ro:ro + 64, :],
                                start=True, stop=True)))
                        P.waits("S", mm)
                        lastex[pg] = P.op("S", lambda e, s, ps=ps, h=h, g=g: e.activation(
                            r1scr[:, :], ps[:, :], AF.Exp, bias=bcC[:, h:h + 1], scale=bcB[:, h:h + 1],
                            accum_out=rs[:, 4 * h + g:4 * h + g + 1]))
                        pg ^= 1
                snap1 = P.snap()
                for eng in ("T", "V"):
                    P.barrier_from(eng, snap1)

                rc = P.op("V", lambda e, s: e.tensor_copy(rsb[:, :], rs[:, :]))
                P.waits("T", [rc])
                smm = []
                for h in range(H):
                    smm.append(P.op("T", lambda e, s, h=h: e.matmul(
                        psC[0:1, 4 * h:4 * h + 4], ones[:, 0:1], rsb[:, 4 * h:4 * h + 4],
                        start=(h == 0), stop=(h == 15))))
                P.waits("V", smm[-1:])
                P.op("V", lambda e, s: e.tensor_copy(Sg[0:1, :], psC[0:1, 0:64]))
                VG()
                P.op("V", lambda e, s: e.reduce_sum(
                    Sg16f[0:1, 0:16], bass.AP(Sg, 0, [[64, 1], [4, 16], [1, 4]]), axis=AX.X))
                VG()
                w2_ = P.op("V", lambda e, s: e.tensor_copy(Sg16[0:1, 0:16], Sg16f[0:1, 0:16]))
                P.waits("T", [w2_])
                w3_ = P.op("T", lambda e, s: e.matmul(psC[0:16, 64:65], Sg16[0:1, 0:16], ones[0:1, 0:1],
                                                      start=True, stop=True))
                P.waits("V", [w3_, pe])
                P.op("V", lambda e, s: e.tensor_copy(sl(14), psC[0:16, 64:65]))
                VG()
                w5_ = P.op("V", lambda e, s: e.tensor_add(sl(15), sl(14), sl(13)))
                P.wait("S", w5_)
                w6_ = P.op("S", lambda e, s: e.activation(sl(16), sl(15), AF.Ln, bias=0.0, scale=1.0))
                P.wait("V", w6_)
                P.op("V", lambda e, s: e.tensor_mul(sl(17), sl(16), sl(10)))
                VG()
                P.op("V", lambda e, s: e.tensor_add(sl(18), sl(17), sl(5)))
                VG()
                P.op("V", lambda e, s: e.tensor_scalar(sl(19), sl(18), -1.0, C2, ALU.mult, ALU.add))
                VG()
                P.op("V", lambda e, s: e.tensor_scalar_min(sl(19), sl(19), 80.0))
                VG()
                wa_ = P.op("V", lambda e, s: e.tensor_copy(scrB[0:16, 2:3], sl(19)))
                VG()
                P.op("V", lambda e, s: e.tensor_copy(sl(20), scrB[0:16, 2:3]))
                VG()
                P.op("V", lambda e, s: e.tensor_scalar_add(sl(21), sl(20), LN512))
                VG()
                wc_ = P.op("V", lambda e, s, w=w: e.tensor_add(sl(23), sl(21), lnmuS[0:16, w:w + 1]))
                P.waits("T", [wa_])
                y1_ = P.op("T", lambda e, s: e.matmul(psT[0:1, 0:16], scrB[0:16, 2:3], ident[0:16, 0:16],
                                                      start=True, stop=True, is_transpose=True))
                P.waits("V", [y1_])
                y2_ = P.op("V", lambda e, s: e.tensor_copy(nmrow[0:1, 0:16], psT[0:1, 0:16]))
                P.waits("T", [y2_])
                y3_ = P.op("T", lambda e, s: e.matmul(psC[:, 0:16], ones[0:1, 0:128], nmrow[0:1, 0:16],
                                                      start=True, stop=True))
                P.waits("V", [y3_])
                P.op("V", lambda e, s: e.tensor_copy(bcA[:, :], psC[:, 0:16]))
                P.waits("S", [wc_])
                P.op("S", lambda e, s: e.activation(padp[:, :], scrA[0:16, 0:512], AF.Exp,
                                                    bias=sl(23), scale=-1.0))
                snap2 = P.snap()
                for eng in ("T", "S", "V"):
                    P.barrier_from(eng, snap2)

                lastex = {0: None, 1: None}
                pg = 0
                prev_n5 = None
                for h in range(H):
                    kh = h // 2
                    ro = (h % 2) * 64
                    groupinfo = []
                    exps = []
                    for g, (j0, j1) in enumerate(GROUPS):
                        ps = psA if pg == 0 else psB
                        if lastex[pg] is not None:
                            P.wait("T", lastex[pg])
                        for j in range(j0, j1):
                            kc = w * 4 + j
                            col = ((kc // 4) % 3) * 512 + (kc % 4) * 128
                            if j <= 3:
                                qlo, qhi = 0, (j + 1) * 128
                            elif j >= 8:
                                qlo, qhi = (j - 8) * 128, 512
                            else:
                                qlo, qhi = 0, 512
                            P.op("T", lambda e, s, kh=kh, ro=ro, col=col, j=j, j0=j0, ps=ps,
                                 qlo=qlo, qhi=qhi: e.matmul(
                                ps[:, (j - j0) * 512 + qlo:(j - j0) * 512 + qhi],
                                kt[kh][ro:ro + 64, col:col + 128],
                                qt[kh][ro:ro + 64, qlo:qhi],
                                start=True, stop=True))
                        cntT = ("T", P.cnt["T"])
                        P.wait("S", cntT)
                        pb = probs[g]
                        ex = P.op("S", lambda e, s, ps=ps, h=h, pb=pb: e.activation(
                            pb[:, :], ps[:, :], AF.Exp, bias=bcA[:, h:h + 1], scale=1.0))
                        exps.append(ex)
                        lastex[pg] = ex
                        groupinfo.append((g, j0, j1, pb))
                        pg ^= 1
                    P.waits("V", exps[-1:])
                    mk = []
                    for (g, j0, j1, pb) in groupinfo:
                        for j in range(j0, j1):
                            loc = (j - j0) * 512
                            if j <= 3:
                                b0 = j * 128
                                mk.append(P.op("V", lambda e, s, pb=pb, loc=loc, b0=b0: e.tensor_mul(
                                    pb[:, loc + b0:loc + b0 + 128], pb[:, loc + b0:loc + b0 + 128], tlow[:, :])))
                            elif j >= 8:
                                b0 = (j - 8) * 128
                                mk.append(P.op("V", lambda e, s, pb=pb, loc=loc, b0=b0: e.tensor_mul(
                                    pb[:, loc + b0:loc + b0 + 128], pb[:, loc + b0:loc + b0 + 128], thig[:, :])))
                    P.waits("T", mk[-1:] + exps[-1:])
                    if prev_n5 is not None:
                        P.wait("T", prev_n5)
                    first = True
                    for (g, j0, j1, pb) in groupinfo:
                        for j in range(j0, j1):
                            kc = w * 4 + j
                            slot12 = kc % 12
                            loc = (j - j0) * 512
                            if j <= 3:
                                qlo, qhi = 0, (j + 1) * 128
                            elif j >= 8:
                                qlo, qhi = (j - 8) * 128, 512
                            else:
                                qlo, qhi = 0, 512
                            P.op("T", lambda e, s, slot12=slot12, h=h, pb=pb, loc=loc,
                                 qlo=qlo, qhi=qhi, first=first: e.matmul(
                                psC[0:65, qlo:qhi],
                                vS[:, slot12 * 1088 + h * 68:slot12 * 1088 + h * 68 + 65],
                                pb[:, loc + qlo:loc + qhi],
                                start=first, stop=False))
                            first = False
                    pvlast = P.op("T", lambda e, s, h=h, w=w: e.matmul(
                        psC[0:65, 0:512],
                        pads[:, h * 65:h * 65 + 65],
                        padp[:, :],
                        start=False, stop=True))
                    P.waits("V", [pvlast])
                    n1 = P.op("V", lambda e, s: e.tensor_copy(scrA[64:65, :], psC[64:65, 0:512]))
                    VG()
                    P.op("V", lambda e, s: e.reciprocal(scrA[96:97, :], scrA[64:65, :]))
                    VG()
                    n2 = P.op("V", lambda e, s: e.tensor_copy(scrB[64:65, 0:512], scrA[96:97, :]))
                    psn = psA if pg == 0 else psB
                    P.waits("T", [n2])
                    n3 = P.op("T", lambda e, s, psn=psn: e.matmul(
                        psn[0:64, 0:512], ones[64:65, 0:64], scrB[64:65, 0:512], start=True, stop=True))
                    P.waits("V", [n3])
                    n4 = P.op("V", lambda e, s, psn=psn: e.tensor_copy(rbsb[:, :], psn[0:64, 0:512]))
                    VG()
                    n5 = P.op("V", lambda e, s, kh=kh, ro=ro: e.scalar_tensor_tensor(
                        atw[kh][ro:ro + 64, :], psC[0:64, 0:512], 1.0, rbsb[:, :], ALU.mult, ALU.mult))
                    prev_n5 = ("V", P.cnt["V"])
                    lastex[pg] = n4
                snap3 = P.snap()
                P.barrier_from("Y", snap3)
                for k in range(8):
                    P.dma("Y", lambda e, s, k=k, w=w: e.dma_start(
                        out=attnTs[k * 128:(k + 1) * 128, w * 512:(w + 1) * 512], in_=atw[k][:, :]))
                stage_done[w] = P.snap()
                for eng in ("T", "V", "S", "G"):
                    P.barrier_from(eng, stage_done[w])

            produce(0)
            produce(1)
            produce(2)
            attend(0)
            produce(3)
            attend(1)
            produce(4)
            attend(2)
            produce(5)
            attend(3)

            fP = P.snap()

            def endy(e, s, fP=fP):
                e.wait_ge(s["dY"], fP["dY"])
            P.ops.append(("Y", endy))
            P.qstream["Y"].append("w")

            def endg(e, s, fP=fP):
                e.wait_ge(s["dG"], fP["dG"])
            P.ops.append(("G", endg))
            P.qstream["G"].append("w")

            P.emit(blk0, sems)

        P2 = Plan(P.cnt)
        P2.spacers = P.spacers
        es2 = contextlib.ExitStack()
        with es2:
            CC2 = es2.enter_context
            W1_ = CC2(nc.sbuf_tensor("W1", [128, 8 * HFF], BF16))
            W2_ = CC2(nc.sbuf_tensor("W2", [128, 32 * E], BF16))
            WO_ = CC2(nc.sbuf_tensor("WO", [128, 8 * E], BF16))
            b1s = CC2(nc.sbuf_tensor("b1s", [128, 32], F32))
            bsum = CC2(nc.sbuf_tensor("bsum", [1, E], BF16))
            hT_ = CC2(nc.sbuf_tensor("hT", [128, 32 * SPAN], BF16))
            xnsp_ = CC2(nc.sbuf_tensor("xnsp", [128, 8 * SPAN], BF16))
            atsp_ = CC2(nc.sbuf_tensor("atsp", [128, 8 * SPAN], BF16))
            xsp_ = CC2(nc.sbuf_tensor("xsp", [128, 2 * E], F32))
            ost_ = CC2(nc.sbuf_tensor("ost", [128, 2 * 512], F32))
            aT4 = CC2(nc.sbuf_tensor("aT4", [4, SPAN], BF16))
            aTr = CC2(nc.sbuf_tensor("aTr", [1, 4 * SPAN], BF16))
            cS = CC2(nc.sbuf_tensor("cS", [16, SPAN], BF16))
            blk1 = CC2(nc.Block())
            W1 = [W1_[:, k * HFF:(k + 1) * HFF] for k in range(8)]
            W2 = [W2_[:, k * E:(k + 1) * E] for k in range(32)]
            WO = [WO_[:, k * E:(k + 1) * E] for k in range(8)]
            hT = [hT_[:, k * SPAN:(k + 1) * SPAN] for k in range(32)]
            xnsp = [xnsp_[:, k * SPAN:(k + 1) * SPAN] for k in range(8)]
            atsp = [atsp_[:, k * SPAN:(k + 1) * SPAN] for k in range(8)]
            xsp = [xsp_[:, i * E:(i + 1) * E] for i in range(2)]
            bosr = xsp_[0:1, 0:E]
            b2sr = xsp_[0:1, E:2 * E]
            ost = [ost_[:, i * 512:(i + 1) * 512] for i in range(2)]

            pb0 = P2.snap()
            for eng in ("T", "V", "S", "G", "Y"):
                P2.barrier_from(eng, pb0)

            for k in range(8):
                P2.dma("G", lambda e, s, k=k: e.dma_start(out=W1[k], in_=w1T[k * 128:(k + 1) * 128, :]))
            for k in range(32):
                P2.dma("G", lambda e, s, k=k: e.dma_start(out=W2[k], in_=w2T[k * 128:(k + 1) * 128, :]))
            for k in range(8):
                P2.dma("G", lambda e, s, k=k: e.dma_start(out=WO[k], in_=woT[k * 128:(k + 1) * 128, :]))
            l1 = P2.dma("Y", lambda e, s: e.dma_start(out=b1s[:, :], in_=b1c))
            l2 = P2.dma("Y", lambda e, s: e.dma_start(out=bosr, in_=bor))
            l3 = P2.dma("Y", lambda e, s: e.dma_start(out=b2sr, in_=b2r))
            P2.waits("V", [l2, l3])
            P2.op("V", lambda e, s: e.tensor_add(bsum[0:1, :], bosr[0:1, :], b2sr[0:1, :]))
            wsnap = P2.snap()
            for eng in ("T", "S", "V", "Y"):
                P2.barrier_from(eng, wsnap)

            for sp in range(NSPAN):
                sb2 = 0
                lds = []
                for k in range(8):
                    lds.append(P2.dma("Y", lambda e, s, k=k, sp=sp, sb2=sb2: e.dma_start(
                        out=xnsp[k][:, :],
                        in_=xnTs[k * 128:(k + 1) * 128, sp * SPAN:(sp + 1) * SPAN])))
                    lds.append(P2.dma("Y", lambda e, s, k=k, sp=sp, sb2=sb2: e.dma_start(
                        out=atsp[k][:, :],
                        in_=attnTs[k * 128:(k + 1) * 128, sp * SPAN:(sp + 1) * SPAN])))
                for tt in range(2):
                    lds.append(P2.dma("Y", lambda e, s, tt=tt, sp=sp, sb2=sb2: e.dma_start(
                        out=xsp[tt][:, :],
                        in_=xh[512 + sp * SPAN + tt * 128: 512 + sp * SPAN + (tt + 1) * 128, :])))
                P2.waits("T", lds)
                am = []
                for k in range(8):
                    am.append(P2.op("T", lambda e, s, k=k, sb2=sb2: e.matmul(
                        psC[0:4, 0:SPAN], eoT[k][:, 1:8:2], xnsp[k][:, :],
                        start=(k == 0), stop=(k == 7))))
                P2.waits("S", am[-1:])
                ae = P2.op("S", lambda e, s: e.activation(aT4[:, :], psC[0:4, 0:SPAN], AF.Identity,
                                                          bias=0.0, scale=sclb[0:4, 0:1]))
                P2.wait("Y", ae)
                acp = []
                for i in range(4):
                    acp.append(P2.dma("Y", lambda e, s, i=i: e.dma_start(
                        out=aTr[0:1, i * SPAN:(i + 1) * SPAN], in_=aT4[i:i + 1, :])))
                P2.waits("T", acp)
                hdone = []
                hbanks = [psA[:, 0:SPAN], psA[:, 512:512 + SPAN], psA[:, 1024:1024 + SPAN],
                          psB[:, 0:SPAN], psB[:, 512:512 + SPAN]]
                for hc in range(32):
                    ph = hbanks[hc % 5]
                    if hc >= 5 and hc % 5 == 0:
                        P2.wait("T", hdone[hc - 1])
                    mm = []
                    for k in range(8):
                        mm.append(P2.op("T", lambda e, s, hc=hc, k=k, ph=ph: e.matmul(
                            ph, W1[k][:, hc * 128:(hc + 1) * 128],
                            xnsp[k][:, :],
                            start=(k == 0), stop=False)))
                    i = hc // 8
                    mm.append(P2.op("T", lambda e, s, hc=hc, i=i, ph=ph: e.matmul(
                        ph, EVN[0:1, i * E + (hc % 8) * 128: i * E + (hc % 8 + 1) * 128],
                        aTr[0:1, i * SPAN:(i + 1) * SPAN],
                        start=False, stop=True)))
                    P2.waits("S", mm[-1:])
                    hdone.append(P2.op("S", lambda e, s, hc=hc, ph=ph: e.activation(
                        hT[hc][:, :], ph, AF.Gelu, bias=b1s[:, hc:hc + 1], scale=1.0)))
                P2.waits("T", hdone[-1:])
                cm = []
                for hk in range(32):
                    cm.append(P2.op("T", lambda e, s, hk=hk: e.matmul(
                        psC[0:16, 0:SPAN], ODP[:, hk * 16:(hk + 1) * 16], hT[hk][:, :],
                        start=(hk == 0), stop=(hk == 31))))
                P2.waits("S", cm[-1:])
                ce = P2.op("S", lambda e, s: e.activation(cS[:, :], psC[0:16, 0:SPAN], AF.Identity,
                                                          bias=0.0, scale=sclb[0:16, 0:1]))
                P2.wait("T", ce)
                for tt in range(2):
                    for oc2 in range(2):
                        po = psB[:, 1024:1536] if (tt * 2 + oc2) % 2 == 0 else psC[:, 0:512]
                        ob = ost[(tt * 2 + oc2) % 2]
                        mm = []
                        for hk in range(32):
                            mm.append(P2.op("T", lambda e, s, hk=hk, tt=tt, oc2=oc2, po=po: e.matmul(
                                po, hT[hk][:, tt * 128:(tt + 1) * 128],
                                W2[hk][:, oc2 * 512:(oc2 + 1) * 512],
                                start=(hk == 0), stop=False)))
                        for k in range(8):
                            mm.append(P2.op("T", lambda e, s, k=k, tt=tt, oc2=oc2, po=po, sb2=sb2: e.matmul(
                                po, atsp[k][:, tt * 128:(tt + 1) * 128],
                                WO[k][:, oc2 * 512:(oc2 + 1) * 512],
                                start=False, stop=False)))
                        mm.append(P2.op("T", lambda e, s, tt=tt, oc2=oc2, po=po: e.matmul(
                            po, cS[:, tt * 128:(tt + 1) * 128],
                            E2s[:, oc2 * 512:(oc2 + 1) * 512],
                            start=False, stop=False)))
                        mm.append(P2.op("T", lambda e, s, oc2=oc2, po=po: e.matmul(
                            po, ones[0:1, 0:128], bsum[0:1, oc2 * 512:(oc2 + 1) * 512],
                            start=False, stop=True)))
                        P2.waits("V", mm[-1:])
                        vo = P2.op("V", lambda e, s, tt=tt, oc2=oc2, po=po, ob=ob, sb2=sb2: e.scalar_tensor_tensor(
                            ob[:, :], po, 1.0,
                            xsp[tt][:, oc2 * 512:(oc2 + 1) * 512],
                            ALU.mult, ALU.add))
                        P2.wait("Y", vo)
                        st = P2.dma("Y", lambda e, s, tt=tt, oc2=oc2, sp=sp, ob=ob: e.dma_start(
                            out=out[sp * SPAN + tt * 128: sp * SPAN + (tt + 1) * 128,
                                    oc2 * 512:(oc2 + 1) * 512],
                            in_=ob[:, :]))
                        P2.wait("T", vo)
                        P2.wait("V", st)
                spsnap = P2.snap()
                for eng in ("T", "V", "S"):
                    P2.barrier_from(eng, spsnap)

            final = P2.snap()

            def fin(e, s, final=final):
                e.wait_ge(s["dY"], final["dY"])
            P2.ops.append(("Y", fin))

            P2.emit(blk1, sems)

    return nc


def kernel(L0_attn_x, L0_attn_wq, L0_attn_bq, L0_attn_wk, L0_attn_bk,
           L0_attn_wv, L0_attn_bv, L0_attn_wo, L0_attn_bo,
           L0_ffn_W1, L0_ffn_b1, L0_ffn_W2, L0_ffn_b2, L0_ffn_scale):
    x = np.ascontiguousarray(np.asarray(L0_attn_x, dtype=np.float32))
    wq = np.asarray(L0_attn_wq, np.float32)
    wk = np.asarray(L0_attn_wk, np.float32)
    wv = np.asarray(L0_attn_wv, np.float32)
    wo = np.asarray(L0_attn_wo, np.float32)
    W1 = np.asarray(L0_ffn_W1, np.float32)
    W2 = np.asarray(L0_ffn_W2, np.float32)
    bq = np.asarray(L0_attn_bq, np.float32)
    bk = np.asarray(L0_attn_bk, np.float32)
    bv = np.asarray(L0_attn_bv, np.float32)
    bo = np.asarray(L0_attn_bo, np.float32)
    b1 = np.asarray(L0_ffn_b1, np.float32)
    b2 = np.asarray(L0_ffn_b2, np.float32)
    scale = np.float32(np.asarray(L0_ffn_scale))

    shared = {
        "wqT": np.ascontiguousarray(wq.T), "wkT": np.ascontiguousarray(wk.T),
        "wvT": np.ascontiguousarray(wv.T), "woT": np.ascontiguousarray(wo.T),
        "w1T": np.ascontiguousarray(W1.T), "w2T": np.ascontiguousarray(W2.T),
        "bqc": np.ascontiguousarray(bq.reshape(8, 128).T),
        "bkc": np.ascontiguousarray(bk.reshape(8, 128).T),
        "bvr": bv.reshape(1, E).copy(), "bor": bo.reshape(1, E).copy(),
        "b2r": b2.reshape(1, E).copy(),
        "b1c": np.ascontiguousarray(b1.reshape(32, 128).T),
        "sclc": np.broadcast_to(scale, (16, 1)).copy(),
    }
    in_maps = []
    for core in range(8):
        b, wi = core // 4, core % 4
        t0 = wi * TOWN
        xhc = np.zeros((TLOC, E), np.float32)
        if wi > 0:
            xhc[0:512] = x[b, t0 - 512:t0]
        xhc[512:512 + TOWN] = x[b, t0:t0 + TOWN]
        if wi < 3:
            xhc[512 + TOWN:TLOC] = x[b, t0 + TOWN:t0 + TOWN + 512]
        lam = np.ones(NCH, np.float32)
        if wi == 0:
            lam[0:4] = 0
        if wi == 3:
            lam[20:24] = 0
        mu = np.zeros(NOWN, np.float32)
        if wi == 0:
            mu[0] = 1
        if wi == 3:
            mu[3] = 1
        psel = np.zeros((16, 16 * 65), np.float32)
        for h in range(H):
            psel[h, h * 65:h * 65 + 64] = -1.0
            psel[h, h * 65 + 64] = 1.0
        lnm = np.full((16, NOWN), -1e30, np.float32)
        for w in range(NOWN):
            if mu[w] > 0:
                lnm[:, w] = 0.0
        in_maps.append(dict(shared,
                            xh=xhc,
                            x16=np.ascontiguousarray(x[b, 1:17]),
                            lamt=np.ascontiguousarray(np.broadcast_to(lam, (128, NCH))),
                            padsel=psel, lnmu=lnm))

    nc = build_program()
    res = run_bass_kernel_spmd(nc, in_maps, core_ids=list(range(8)))
    outp = np.zeros((2, 8192, E), np.float32)
    for core in range(8):
        b, wi = core // 4, core % 4
        outp[b, wi * TOWN:(wi + 1) * TOWN] = res.results[core]["out"]
    return outp


# version 20
# speedup vs baseline: 1.0818x; 1.0003x over previous
import math
import numpy as np
import concourse.bass as bass
import concourse.mybir as mybir
from concourse.bass_utils import run_bass_kernel_spmd

F32 = mybir.dt.float32
BF16 = mybir.dt.bfloat16
AF = mybir.ActivationFunctionType
ALU = mybir.AluOpType
AX = mybir.AxisListType

E = 1024
H = 16
WS = 512
NOWN = 4
NLW = 6
TLOC = NLW * WS
NCH = TLOC // 128
TOWN = NOWN * WS
LN512 = float(math.log(512.0))
C2 = 30.0
HFF = 4096
SPAN = 256
NSPAN = TOWN // SPAN
GROUPS = [(0, 3), (3, 6), (6, 9), (9, 12)]


class Plan:
    def __init__(self, cnt=None):
        self.ops = []
        self.cnt = dict(cnt) if cnt else {"T": 0, "V": 0, "S": 0, "G": 0, "dY": 0, "dG": 0}
        self.base = dict(self.cnt)
        self.cover = {}
        self.spacers = {}
        self.qstream = {"Y": [], "G": []}
        self.defer = []

    SCALE = 256

    def op(self, eng, fn):
        self.cnt[eng] += self.SCALE
        tgt = self.cnt[eng]

        def run(e, sems, fn=fn, eng=eng):
            fn(e, sems).then_inc(sems[eng], self.SCALE)
        self.ops.append((eng, run))
        return (eng, tgt)

    def dma(self, eng, fn):
        sem = {"Y": "dY", "G": "dG"}[eng]
        self.cnt[sem] += 16
        tgt = self.cnt[sem]

        def run(e, sems, fn=fn, sem=sem):
            fn(e, sems).then_inc(sems[sem], 16)
        self.qstream[eng].append(16)
        self.ops.append((eng, run))
        return (sem, tgt)

    FLUSH_K = 1

    def _spacer(self, sem):
        self.cnt[sem] += self.SCALE

        def run(e, sems, sem=sem):
            e.drain()
            e.sem_inc(sems[sem], self.SCALE)
        self.ops.append((sem, run))

    def wait(self, eng, dep):
        sem, val = dep
        if val <= 0:
            return
        if sem in getattr(self, "spacers", {}) and sem != eng:
            cu, sv = self.cover.get(sem, (0, 0))
            if val > cu:
                covered = self.cnt[sem]
                for _ in range(self.FLUSH_K):
                    self._spacer(sem)
                self.cover[sem] = (covered, self.cnt[sem])
                cu, sv = self.cover[sem]
            val = sv

        if sem in ("dY", "dG"):
            cell = [val]
            self.defer.append((sem, cell))

            def run(e, sems, sem=sem, cell=cell):
                e.wait_ge(sems[sem], cell[0])
        else:
            def run(e, sems, sem=sem, val=val):
                e.wait_ge(sems[sem], val)
        if eng in ("Y", "G"):
            self.qstream[eng].append("w")
        self.ops.append((eng, run))

    def waits(self, eng, deps):
        best = {}
        for (sem, val) in deps:
            if val > 0:
                best[sem] = max(best.get(sem, 0), val)
        for sem, val in best.items():
            self.wait(eng, (sem, val))

    def snap(self):
        return dict(self.cnt)

    def barrier_from(self, eng, snap):
        self.waits(eng, [(k, v) for k, v in snap.items() if k != {"Y": "dY", "G": "dG"}.get(eng, eng)])

    def finalize(self):
        for q, sem in (("Y", "dY"), ("G", "dG")):
            cum = self.base[sem]
            bs = [cum]
            for it in self.qstream[q]:
                if it == "w":
                    bs.append(cum)
                else:
                    cum += it
            bs.append(cum)
            bs = sorted(set(bs))
            for s2, cell in self.defer:
                if s2 != sem:
                    continue
                v = cell[0]
                cell[0] = next((b for b in bs if b >= v), bs[-1])
        self.defer = []

    def emit(self, blk, sems):
        self.finalize()

        def mk(name):
            def body(e):
                for eng, fn in self.ops:
                    if eng == name:
                        fn(e, sems)
            return body
        blk.tensor(mk("T"))
        blk.vector(mk("V"))
        blk.scalar(mk("S"))
        blk.gpsimd(mk("G"))
        blk.sync(mk("Y"))


def build_program():
    nc = bass.Bass("TRN2", target_bir_lowering=False)

    xh = nc.dram_tensor("xh", [TLOC, E], F32, kind="ExternalInput").ap()
    x16 = nc.dram_tensor("x16", [16, E], F32, kind="ExternalInput").ap()
    wqT = nc.dram_tensor("wqT", [E, E], F32, kind="ExternalInput").ap()
    wkT = nc.dram_tensor("wkT", [E, E], F32, kind="ExternalInput").ap()
    wvT = nc.dram_tensor("wvT", [E, E], F32, kind="ExternalInput").ap()
    woT = nc.dram_tensor("woT", [E, E], F32, kind="ExternalInput").ap()
    w1T = nc.dram_tensor("w1T", [E, HFF], F32, kind="ExternalInput").ap()
    w2T = nc.dram_tensor("w2T", [HFF, E], F32, kind="ExternalInput").ap()
    bqc = nc.dram_tensor("bqc", [128, 8], F32, kind="ExternalInput").ap()
    bkc = nc.dram_tensor("bkc", [128, 8], F32, kind="ExternalInput").ap()
    bvr = nc.dram_tensor("bvr", [1, E], F32, kind="ExternalInput").ap()
    b1c = nc.dram_tensor("b1c", [128, 32], F32, kind="ExternalInput").ap()
    bor = nc.dram_tensor("bor", [1, E], F32, kind="ExternalInput").ap()
    b2r = nc.dram_tensor("b2r", [1, E], F32, kind="ExternalInput").ap()
    sclc = nc.dram_tensor("sclc", [16, 1], F32, kind="ExternalInput").ap()
    lamt = nc.dram_tensor("lamt", [128, NCH], F32, kind="ExternalInput").ap()
    padsel = nc.dram_tensor("padsel", [16, 16 * 65], F32, kind="ExternalInput").ap()
    lnmu = nc.dram_tensor("lnmu", [16, NOWN], F32, kind="ExternalInput").ap()
    out = nc.dram_tensor("out", [TOWN, E], F32, kind="ExternalOutput").ap()
    xnTs = nc.dram_tensor("xnTs", [E, TOWN], BF16, kind="Internal").ap()
    attnTs = nc.dram_tensor("attnTs", [E, TOWN], BF16, kind="Internal").ap()

    import contextlib
    es0 = contextlib.ExitStack()
    with es0:
        CC0 = es0.enter_context
        semT = CC0(nc.semaphore("semT"))
        semV = CC0(nc.semaphore("semV"))
        semS = CC0(nc.semaphore("semS"))
        semG = CC0(nc.semaphore("semG"))
        semdY = CC0(nc.semaphore("semdY"))
        semdG = CC0(nc.semaphore("semdG"))
        psA = CC0(nc.psum_tensor("psA", [128, 1536], F32))
        psB = CC0(nc.psum_tensor("psB", [128, 1536], F32))
        psC = CC0(nc.psum_tensor("psC", [128, 512], F32))
        psT = CC0(nc.psum_tensor("psT", [128, 1024], BF16))
        ones = CC0(nc.sbuf_tensor("ones", [128, 128], BF16))
        ident = CC0(nc.sbuf_tensor("ident", [128, 128], BF16))
        tlow = CC0(nc.sbuf_tensor("tlow", [128, 128], BF16))
        thig = CC0(nc.sbuf_tensor("thig", [128, 128], BF16))
        iotaf = CC0(nc.sbuf_tensor("iotaf", [128, 128], F32))
        iotap = CC0(nc.sbuf_tensor("iotap", [128, 128], F32))
        xn16 = CC0(nc.sbuf_tensor("xn16", [16, E], BF16))
        eoT_ = CC0(nc.sbuf_tensor("eoT", [128, 8 * 16], BF16))
        E2s = CC0(nc.sbuf_tensor("E2s", [16, E], BF16))
        EVN = CC0(nc.sbuf_tensor("EVN", [1, 4 * E], BF16))
        ODP = CC0(nc.sbuf_tensor("ODP", [128, 32 * 16], BF16))
        sclb = CC0(nc.sbuf_tensor("sclb", [16, 1], F32))
        lamb = CC0(nc.sbuf_tensor("lamb", [128, NCH], F32))
        lnmuS = CC0(nc.sbuf_tensor("lnmuS", [16, NOWN], F32))
        smll = CC0(nc.sbuf_tensor("smll", [128, 64], F32))
        sm2 = CC0(nc.sbuf_tensor("sm2", [128, 64], F32))
        junkS = CC0(nc.sbuf_tensor("junkS", [128, 2], F32))
        junkG = CC0(nc.sbuf_tensor("junkG", [128, 2], F32))
        eoT = [eoT_[:, k * 16:(k + 1) * 16] for k in range(8)]
        sems = dict(T=semT, V=semV, S=semS, G=semG, dY=semdY, dG=semdG)

        def sl(i):
            return smll[0:16, i:i + 1]

        P = Plan()
        P.spacers = {
            "V": lambda e: e.tensor_copy(smll[:, 62:63], smll[:, 63:64]),
            "S": lambda e: e.copy(junkS[:, 0:1], junkS[:, 1:2]),
            "G": lambda e: e.memset(junkG[:, 0:1], 0.0),
        }

        def VG():
            P.op("V", lambda e, s: e.tensor_copy(smll[:, 62:63], smll[:, 63:64]))

        es1 = contextlib.ExitStack()
        with es1:
            CC1 = es1.enter_context
            wq_ = CC1(nc.sbuf_tensor("wq", [128, 8 * E], BF16))
            wk_ = CC1(nc.sbuf_tensor("wk", [128, 8 * E], BF16))
            wv_ = CC1(nc.sbuf_tensor("wv", [128, 8 * E], BF16))
            bqs = CC1(nc.sbuf_tensor("bqs", [128, 8], F32))
            bks = CC1(nc.sbuf_tensor("bks", [128, 8], F32))
            bvs = CC1(nc.sbuf_tensor("bvs", [1, E], BF16))
            pads = CC1(nc.sbuf_tensor("pads", [16, 16 * 65], BF16))
            xt_ = CC1(nc.sbuf_tensor("xt", [128, 4 * E], F32))
            xsq_ = CC1(nc.sbuf_tensor("xsq", [128, 2 * E], F32))
            xntm_ = CC1(nc.sbuf_tensor("xntm", [128, 4 * E], BF16))
            xnT_ = CC1(nc.sbuf_tensor("xnT", [128, 8 * 2 * WS], BF16))
            kt_ = CC1(nc.sbuf_tensor("kt", [128, 8 * 3 * WS], BF16))
            vS = CC1(nc.sbuf_tensor("vS", [128, 12 * 1088], BF16))
            qt_ = CC1(nc.sbuf_tensor("qt", [128, 8 * WS], BF16))
            probs_ = CC1(nc.sbuf_tensor("probs", [128, 4 * 1536], BF16))
            rs = CC1(nc.sbuf_tensor("rs", [128, 64], F32))
            rsb = CC1(nc.sbuf_tensor("rsb", [128, 64], BF16))
            atw_ = CC1(nc.sbuf_tensor("atw", [128, 8 * WS], BF16))
            mkr = CC1(nc.sbuf_tensor("mkr", [16, 8], F32))
            scrA = CC1(nc.sbuf_tensor("scrA", [128, 512], F32))
            scrB = CC1(nc.sbuf_tensor("scrB", [128, 512], BF16))
            rbsb = CC1(nc.sbuf_tensor("rbsb", [64, 512], F32))
            Sg = CC1(nc.sbuf_tensor("Sg", [1, 64], F32))
            Sg16 = CC1(nc.sbuf_tensor("Sg16", [1, 16], BF16))
            Sg16f = CC1(nc.sbuf_tensor("Sg16f", [1, 16], F32))
            padp = CC1(nc.sbuf_tensor("padp", [16, 512], BF16))
            nmrow = CC1(nc.sbuf_tensor("nmrow", [1, 16], BF16))
            bcA = CC1(nc.sbuf_tensor("bcA", [128, 16], F32))
            bcB = CC1(nc.sbuf_tensor("bcB", [128, 16], F32))
            bcC = CC1(nc.sbuf_tensor("bcC", [128, 16], F32))
            Jsl = CC1(nc.sbuf_tensor("Jsl", [128, 31], BF16))
            blk0 = CC1(nc.Block())
            wq = [wq_[:, k * E:(k + 1) * E] for k in range(8)]
            wk = [wk_[:, k * E:(k + 1) * E] for k in range(8)]
            wv = [wv_[:, k * E:(k + 1) * E] for k in range(8)]
            xt = [xt_[:, i * E:(i + 1) * E] for i in range(4)]
            x16f = xt_[0:16, 0:E]
            x16sq = xsq_[0:16, 0:E]
            xsq = [xsq_[:, i * E:(i + 1) * E] for i in range(2)]
            xntm = [xntm_[:, i * E:(i + 1) * E] for i in range(4)]
            xnT = [xnT_[:, k * 2 * WS:(k + 1) * 2 * WS] for k in range(8)]
            kt = [kt_[:, k * 3 * WS:(k + 1) * 3 * WS] for k in range(8)]
            qt = [qt_[:, k * WS:(k + 1) * WS] for k in range(8)]
            sqb = [xntm_[:, k * WS:(k + 1) * WS] for k in range(8)]
            probs = [probs_[:, i * 1536:(i + 1) * 1536] for i in range(4)]
            r1scr = probs_[:, 0:1536]
            atw = [atw_[:, k * WS:(k + 1) * WS] for k in range(8)]

            for k in range(8):
                P.dma("G", lambda e, s, k=k: e.dma_start(out=wq[k], in_=wqT[k * 128:(k + 1) * 128, :]))
                P.dma("G", lambda e, s, k=k: e.dma_start(out=wk[k], in_=wkT[k * 128:(k + 1) * 128, :]))
                P.dma("G", lambda e, s, k=k: e.dma_start(out=wv[k], in_=wvT[k * 128:(k + 1) * 128, :]))
            P.dma("G", lambda e, s: e.dma_start(out=bvs[:, :], in_=bvr))
            P.dma("G", lambda e, s: e.dma_start(out=pads[:, :], in_=padsel))
            P.dma("Y", lambda e, s: e.dma_start(out=bqs[:, :], in_=bqc))
            P.dma("Y", lambda e, s: e.dma_start(out=bks[:, :], in_=bkc))
            P.dma("Y", lambda e, s: e.dma_start(out=sclb[:, :], in_=sclc))
            P.dma("Y", lambda e, s: e.dma_start(out=lamb[:, :], in_=lamt))
            x16ld = P.dma("Y", lambda e, s: e.dma_start(out=x16f, in_=x16))
            P.dma("Y", lambda e, s: e.dma_start(out=lnmuS[:, :], in_=lnmu))

            g1 = P.op("G", lambda e, s: e.iota(iotaf[:, :], [[1, 128]], channel_multiplier=0,
                                               allow_small_or_imprecise_dtypes=True))
            g2 = P.op("G", lambda e, s: e.iota(iotap[:, :], [[0, 128]], channel_multiplier=1,
                                               allow_small_or_imprecise_dtypes=True))
            g3 = P.op("G", lambda e, s: e.memset(Jsl[:, :], 0.0))
            g4 = P.op("G", lambda e, s: e.memset(ODP[:, :], 0.0))
            P.waits("V", [g1, g2, g3])
            c1 = P.op("V", lambda e, s: e.tensor_tensor(ident[:, :], iotaf[:, :], iotap[:, :], ALU.is_equal))
            P.op("V", lambda e, s: e.tensor_tensor(tlow[:, :], iotaf[:, :], iotap[:, :], ALU.is_le))
            P.op("V", lambda e, s: e.tensor_tensor(thig[:, :], iotaf[:, :], iotap[:, :], ALU.is_ge))
            P.op("V", lambda e, s: e.memset(Jsl[0:64, 15:16], 1.0))
            P.op("V", lambda e, s: e.memset(Jsl[64:128, 16:17], 1.0))
            P.op("V", lambda e, s: e.memset(ones[:, :], 1.0))

            P.wait("G", x16ld)
            gx = P.op("G", lambda e, s: e.tensor_mul(x16sq[:, :], x16f[:, :], x16f[:, :]))
            P.wait("V", x16ld)
            P.op("V", lambda e, s: e.reduce_sum(sm2[0:16, 0:1], x16f[:, :], axis=AX.X))
            P.wait("V", gx)
            P.op("V", lambda e, s: e.reduce_sum(sm2[0:16, 1:2], x16sq[:, :], axis=AX.X))
            P.op("V", lambda e, s: e.tensor_scalar_mul(sm2[0:16, 2:3], sm2[0:16, 0:1], 1.0 / E))
            P.op("V", lambda e, s: e.tensor_scalar_mul(sm2[0:16, 3:4], sm2[0:16, 1:2], 1.0 / E))
            P.op("V", lambda e, s: e.tensor_mul(sm2[0:16, 4:5], sm2[0:16, 2:3], sm2[0:16, 2:3]))
            VG()
            P.op("V", lambda e, s: e.tensor_sub(sm2[0:16, 5:6], sm2[0:16, 3:4], sm2[0:16, 4:5]))
            VG()
            v7 = P.op("V", lambda e, s: e.tensor_scalar_add(sm2[0:16, 6:7], sm2[0:16, 5:6], 1e-5))
            P.wait("S", v7)
            s1 = P.op("S", lambda e, s: e.sqrt(sm2[0:16, 7:8], sm2[0:16, 6:7]))
            P.wait("V", s1)
            P.op("V", lambda e, s: e.reciprocal(sm2[0:16, 8:9], sm2[0:16, 7:8]))
            VG()
            P.op("V", lambda e, s: e.tensor_mul(sm2[0:16, 9:10], sm2[0:16, 2:3], sm2[0:16, 8:9]))
            VG()
            v10 = P.op("V", lambda e, s: e.tensor_scalar_mul(sm2[0:16, 10:11], sm2[0:16, 9:10], -1.0))
            P.wait("S", v10)
            sx = P.op("S", lambda e, s: e.activation(xn16[:, :], x16f[:, :], AF.Identity,
                                                     bias=sm2[0:16, 10:11], scale=sm2[0:16, 8:9]))
            P.waits("T", [sx, c1])
            trl = []
            for k in range(8):
                trl.append(P.op("T", lambda e, s, k=k: e.matmul(
                    psT[:, k * 128:k * 128 + 16], xn16[:, k * 128:(k + 1) * 128], ident[0:16, 0:16],
                    start=(k == 0), stop=(k == 7), is_transpose=True)))
            P.waits("V", trl[-1:])
            ev = []
            for k in range(8):
                ev.append(P.op("V", lambda e, s, k=k: e.tensor_copy(eoT[k][:, :], psT[:, k * 128:k * 128 + 16])))
            ez = P.op("V", lambda e, s: e.memset(E2s[:, :], 0.0))
            P.waits("Y", [sx, ez])
            for i in range(4):
                P.dma("Y", lambda e, s, i=i: e.dma_start(out=EVN[0:1, i * E:(i + 1) * E], in_=xn16[2 * i:2 * i + 1, :]))
            for m in range(16):
                i, jpp = m // 4, m % 4
                P.dma("Y", lambda e, s, i=i, jpp=jpp, m=m: e.dma_start(
                    out=E2s[m:m + 1, i * 256:(i + 1) * 256],
                    in_=xn16[8 + 2 * i:9 + 2 * i, jpp:jpp + E - 4 + 1:4],
                    allow_slow_non_contiguous=True))
            P.waits("V", [g4] + ev)
            for hk in range(32):
                k, jpp = hk % 8, hk // 8
                P.op("V", lambda e, s, hk=hk, k=k, jpp=jpp: e.tensor_copy(
                    bass.AP(ODP, hk * 16 + jpp, [[32 * 16, 128], [4, 4], [1, 1]]),
                    eoT[k][:, 9:16:2]))

            snap = P.snap()
            for eng in ("T", "V", "S", "G", "Y"):
                P.barrier_from(eng, snap)

            stage_done = {}
            sqred = {}
            prod_done = {}

            def produce(lw):
                slot2 = lw % 2
                slot3 = lw % 3
                if lw - 1 in prod_done:
                    P.barrier_from("Y", prod_done[lw - 1])
                xload = []
                for tc in range(4):
                    xload.append(P.dma("Y", lambda e, s, lw=lw, tc=tc: e.dma_start(
                        out=xt[tc], in_=xh[lw * 512 + tc * 128: lw * 512 + (tc + 1) * 128, :])))
                for tc in range(4):
                    xb = xt[tc]
                    sq = xsq[tc % 2]
                    P.wait("G", xload[tc])
                    if tc >= 2:
                        P.wait("G", sqred[tc - 2])
                    gq = P.op("G", lambda e, s, xb=xb, sq=sq: e.tensor_mul(sq, xb, xb))
                    P.wait("V", xload[tc])
                    P.op("V", lambda e, s, xb=xb, tc=tc: e.reduce_sum(sm2[:, 16 + tc:17 + tc], xb, axis=AX.X))
                    P.wait("V", gq)
                    sqred[tc] = P.op("V", lambda e, s, sq=sq, tc=tc: e.reduce_sum(sm2[:, 20 + tc:21 + tc], sq, axis=AX.X))
                P.op("V", lambda e, s: e.tensor_scalar_mul(sm2[:, 24:28], sm2[:, 16:20], 1.0 / E))
                P.op("V", lambda e, s: e.tensor_scalar_mul(sm2[:, 28:32], sm2[:, 20:24], 1.0 / E))
                P.op("V", lambda e, s: e.tensor_mul(sm2[:, 32:36], sm2[:, 24:28], sm2[:, 24:28]))
                VG()
                P.op("V", lambda e, s: e.tensor_sub(sm2[:, 36:40], sm2[:, 28:32], sm2[:, 32:36]))
                VG()
                a7 = P.op("V", lambda e, s: e.tensor_scalar_add(sm2[:, 40:44], sm2[:, 36:40], 1e-5))
                P.wait("S", a7)
                b1_ = P.op("S", lambda e, s: e.sqrt(sm2[:, 44:48], sm2[:, 40:44]))
                P.wait("V", b1_)
                P.op("V", lambda e, s: e.reciprocal(sm2[:, 48:52], sm2[:, 44:48]))
                VG()
                P.op("V", lambda e, s: e.tensor_mul(sm2[:, 52:56], sm2[:, 24:28], sm2[:, 48:52]))
                VG()
                P.op("V", lambda e, s: e.tensor_scalar_mul(sm2[:, 56:60], sm2[:, 52:56], -1.0))
                evacs = []
                for tc in range(4):
                    xb = xt[tc]
                    VG()
                    nrm = P.op("V", lambda e, s, xb=xb, tc=tc: e.tensor_scalar(
                        xntm[tc], xb, sm2[:, 56 + tc:57 + tc], sm2[:, 48 + tc:49 + tc],
                        ALU.add, ALU.mult))
                    P.wait("T", nrm)
                    trl2 = []
                    for k in range(8):
                        trl2.append(P.op("T", lambda e, s, tc=tc, k=k: e.matmul(
                            psT[:, k * 128:(k + 1) * 128], xntm[tc][:, k * 128:(k + 1) * 128], ident[:, :],
                            start=(k == 0), stop=(k == 7), is_transpose=True)))
                    P.waits("V", trl2[-1:])
                    for k in range(8):
                        col = slot2 * 512 + tc * 128
                        evacs.append(P.op("V", lambda e, s, k=k, col=col: e.tensor_copy(
                            xnT[k][:, col:col + 128], psT[:, k * 128:(k + 1) * 128])))
                if 1 <= lw <= 4:
                    P.waits("Y", evacs[-8:])
                    for k in range(8):
                        P.dma("Y", lambda e, s, k=k, lw=lw, slot2=slot2: e.dma_start(
                            out=xnTs[k * 128:(k + 1) * 128, (lw - 1) * 512:lw * 512],
                            in_=xnT[k][:, slot2 * 512:slot2 * 512 + 512]))
                P.waits("T", evacs)
                banks6m = [(psA, 0), (psA, 512), (psA, 1024), (psB, 0), (psB, 512), (psB, 1024)]
                banks6 = [h_[:, o_:o_ + 512] for (h_, o_) in banks6m]
                kdone = []
                for oc in range(8):
                    if oc >= 6:
                        P.wait("T", kdone[oc - 6])
                    half = banks6[oc % 6]
                    mm = []
                    for k in range(8):
                        mm.append(P.op("T", lambda e, s, oc=oc, k=k, half=half, slot2=slot2: e.matmul(
                            half, wk[k][:, oc * 128:(oc + 1) * 128],
                            xnT[k][:, slot2 * 512:slot2 * 512 + 512],
                            start=(k == 0), stop=(k == 7))))
                    P.waits("V", mm[-1:])
                    kdone.append(P.op("V", lambda e, s, oc=oc, half=half, slot3=slot3: e.tensor_scalar(
                        kt[oc][:, slot3 * 512:slot3 * 512 + 512], half,
                        bks[:, oc:oc + 1], None, ALU.add)))
                vprev = []
                for tc in range(4):
                    kc = lw * 4 + tc
                    slot12 = kc % 12
                    for oc2 in range(2):
                        idx = tc * 2 + oc2
                        if idx >= 6:
                            P.wait("T", vprev[idx - 6])
                        else:
                            P.wait("T", kdone[idx + 6 if idx < 2 else idx])
                        half = banks6[idx % 6]
                        hh, ho = banks6m[idx % 6]
                        mm = []
                        for k in range(8):
                            mm.append(P.op("T", lambda e, s, tc=tc, oc2=oc2, k=k, half=half, slot2=slot2: e.matmul(
                                half,
                                xnT[k][:, slot2 * 512 + tc * 128: slot2 * 512 + (tc + 1) * 128],
                                wv[k][:, oc2 * 512:(oc2 + 1) * 512],
                                start=(k == 0), stop=False)))
                        mm.append(P.op("T", lambda e, s, oc2=oc2, half=half: e.matmul(
                            half, ones[0:1, 0:128], bvs[0:1, oc2 * 512:(oc2 + 1) * 512],
                            start=False, stop=True)))
                        P.waits("V", mm[-1:])
                        base = slot12 * 1088 + oc2 * 8 * 68
                        vprev.append(P.op("V", lambda e, s, base=base, kc=kc, hh=hh, ho=ho: e.tensor_scalar(
                            bass.AP(vS, base, [[12 * 1088, 128], [68, 8], [1, 64]]),
                            bass.AP(hh, ho, [[1536, 128], [64, 8], [1, 64]]),
                            lamb[:, kc:kc + 1], None, ALU.mult)))
                    P.op("V", lambda e, s, slot12=slot12, kc=kc: e.tensor_scalar(
                        bass.AP(vS, slot12 * 1088 + 64, [[12 * 1088, 128], [68, 16], [1, 1]]),
                        bass.AP(ones, 0, [[128, 128], [1, 16], [1, 1]]),
                        lamb[:, kc:kc + 1], None, ALU.mult))
                P.waits("G", kdone + vprev[-1:])
                sqd = []
                for k in range(8):
                    sqd.append(P.op("G", lambda e, s, k=k, slot3=slot3: e.tensor_mul(
                        sqb[k][:, :], kt[k][:, slot3 * 512:slot3 * 512 + 512],
                        kt[k][:, slot3 * 512:slot3 * 512 + 512])))
                P.waits("T", sqd + vprev)
                nm = []
                for k in range(8):
                    nm.append(P.op("T", lambda e, s, k=k: e.matmul(
                        psC[0:16, 0:512], Jsl[:, 15 - 2 * k:31 - 2 * k], sqb[k][:, :],
                        start=(k == 0), stop=(k == 7))))
                P.waits("V", nm[-1:])
                P.op("V", lambda e, s, lw=lw: e.reduce_max(mkr[:, lw:lw + 1], psC[0:16, 0:512], axis=AX.X))
                prod_done[lw] = P.snap()
                for eng in ("T", "V", "S", "G"):
                    P.barrier_from(eng, prod_done[lw])

            def attend(w):
                lw = w + 1
                slot2 = lw % 2
                qdone = []
                banks6 = [psA[:, 0:512], psA[:, 512:1024], psA[:, 1024:1536],
                          psB[:, 0:512], psB[:, 512:1024], psB[:, 1024:1536]]
                for oc in range(8):
                    if oc >= 6:
                        P.wait("T", qdone[oc - 6])
                    half = banks6[oc % 6]
                    mm = []
                    for k in range(8):
                        mm.append(P.op("T", lambda e, s, oc=oc, k=k, half=half, slot2=slot2: e.matmul(
                            half, wq[k][:, oc * 128:(oc + 1) * 128],
                            xnT[k][:, slot2 * 512:slot2 * 512 + 512],
                            start=(k == 0), stop=(k == 7))))
                    P.waits("V", mm[-1:])
                    qdone.append(P.op("V", lambda e, s, oc=oc, half=half: e.tensor_scalar(
                        qt[oc][:, :], half, bqs[:, oc:oc + 1], None, ALU.add)))
                P.waits("G", qdone)
                sqd = []
                for k in range(8):
                    sqd.append(P.op("G", lambda e, s, k=k: e.tensor_mul(sqb[k][:, :], qt[k][:, :], qt[k][:, :])))
                P.waits("T", sqd + qdone)
                nm = []
                for k in range(8):
                    nm.append(P.op("T", lambda e, s, k=k: e.matmul(
                        psC[0:16, 0:512], Jsl[:, 15 - 2 * k:31 - 2 * k], sqb[k][:, :],
                        start=(k == 0), stop=(k == 7))))
                P.waits("V", nm[-1:])
                vq = P.op("V", lambda e, s: e.reduce_max(sl(0), psC[0:16, 0:512], axis=AX.X))
                sg = []
                for k in range(8):
                    sg.append(P.op("T", lambda e, s, k=k: e.matmul(
                        psB[0:16, 0:512], Jsl[:, 15 - 2 * k:31 - 2 * k], qt[k][:, :],
                        start=(k == 0), stop=(k == 7))))
                P.waits("V", sg[-1:])
                sgv = P.op("V", lambda e, s: e.tensor_copy(scrA[0:16, :], psB[0:16, 0:512]))
                P.wait("V", vq)
                P.op("V", lambda e, s, lw=lw: e.tensor_max(sl(1), mkr[:, lw - 1:lw], mkr[:, lw:lw + 1]))
                VG()
                P.op("V", lambda e, s, lw=lw: e.tensor_max(sl(2), sl(1), mkr[:, lw + 1:lw + 2]))
                VG()
                P.op("V", lambda e, s: e.tensor_scalar_max(sl(3), sl(2), 64.0))
                VG()
                vb = P.op("V", lambda e, s: e.tensor_mul(sl(4), sl(0), sl(3)))
                P.wait("S", vb)
                sb = P.op("S", lambda e, s: e.sqrt(sl(5), sl(4)))
                P.wait("V", sb)
                P.op("V", lambda e, s: e.reciprocal(sl(6), sl(5)))
                VG()
                P.op("V", lambda e, s: e.tensor_scalar_mul(sl(7), sl(6), 40.0))
                VG()
                P.op("V", lambda e, s: e.tensor_scalar_min(sl(8), sl(7), 1.0))
                VG()
                P.op("V", lambda e, s: e.tensor_copy(scrB[0:16, 0:1], sl(8)))
                VG()
                P.op("V", lambda e, s: e.tensor_copy(sl(9), scrB[0:16, 0:1]))
                VG()
                P.op("V", lambda e, s: e.reciprocal(sl(10), sl(9)))
                P.op("V", lambda e, s: e.tensor_scalar_mul(sl(22), sl(9), -1.0))
                P.op("V", lambda e, s: e.tensor_mul(sl(11), sl(5), sl(9)))
                VG()
                P.op("V", lambda e, s: e.tensor_scalar_mul(sl(11), sl(11), -1.0))
                P.op("V", lambda e, s: e.tensor_scalar(sl(12), sl(5), -1.0, LN512, ALU.mult, ALU.add))
                VG()
                P.op("V", lambda e, s: e.tensor_mul(sl(12), sl(12), sl(9)))
                VG()
                r11 = P.op("V", lambda e, s: e.tensor_copy(scrB[0:16, 1:2], sl(11)))
                P.waits("T", [r11])
                t1 = P.op("T", lambda e, s: e.matmul(psT[0:1, 0:16], scrB[0:16, 0:1], ident[0:16, 0:16],
                                                     start=True, stop=True, is_transpose=True))
                P.waits("V", [t1])
                u1 = P.op("V", lambda e, s: e.tensor_copy(nmrow[0:1, 0:16], psT[0:1, 0:16]))
                P.waits("T", [u1])
                t3 = P.op("T", lambda e, s: e.matmul(psC[:, 0:16], ones[0:1, 0:128], nmrow[0:1, 0:16],
                                                     start=True, stop=True))
                P.waits("V", [t3])
                u3 = P.op("V", lambda e, s: e.tensor_copy(bcB[:, :], psC[:, 0:16]))
                P.waits("T", [u3])
                t2 = P.op("T", lambda e, s: e.matmul(psT[0:1, 0:16], scrB[0:16, 1:2], ident[0:16, 0:16],
                                                     start=True, stop=True, is_transpose=True))
                P.waits("V", [t2])
                u2 = P.op("V", lambda e, s: e.tensor_copy(nmrow[0:1, 0:16], psT[0:1, 0:16]))
                P.waits("T", [u2])
                t4 = P.op("T", lambda e, s: e.matmul(psC[:, 0:16], ones[0:1, 0:128], nmrow[0:1, 0:16],
                                                     start=True, stop=True))
                P.waits("V", [t4])
                u4 = P.op("V", lambda e, s: e.tensor_copy(bcC[:, :], psC[:, 0:16]))
                snap0 = P.snap()
                for eng in ("T", "S", "V"):
                    P.barrier_from(eng, snap0)

                pe = P.op("S", lambda e, s: e.activation(
                    padp[:, :], scrA[0:16, 0:512], AF.Exp, bias=sl(12), scale=sl(22),
                    accum_out=sl(13)))

                lastex = {0: None, 1: None}
                pg = 0
                for h in range(H):
                    kh = h // 2
                    ro = (h % 2) * 64
                    for g, (j0, j1) in enumerate(GROUPS):
                        ps = psA if pg == 0 else psB
                        if lastex[pg] is not None:
                            P.wait("T", lastex[pg])
                        mm = []
                        for j in range(j0, j1):
                            kc = w * 4 + j
                            col = ((kc // 4) % 3) * 512 + (kc % 4) * 128
                            mm.append(P.op("T", lambda e, s, kh=kh, ro=ro, col=col, j=j, j0=j0, ps=ps: e.matmul(
                                ps[:, (j - j0) * 512:(j - j0 + 1) * 512],
                                kt[kh][ro:ro + 64, col:col + 128],
                                qt[kh][ro:ro + 64, :],
                                start=True, stop=True)))
                        P.waits("S", mm)
                        lastex[pg] = P.op("S", lambda e, s, ps=ps, h=h, g=g: e.activation(
                            r1scr[:, :], ps[:, :], AF.Exp, bias=bcC[:, h:h + 1], scale=bcB[:, h:h + 1],
                            accum_out=rs[:, 4 * h + g:4 * h + g + 1]))
                        pg ^= 1
                snap1 = P.snap()
                for eng in ("T", "V"):
                    P.barrier_from(eng, snap1)

                rc = P.op("V", lambda e, s: e.tensor_copy(rsb[:, :], rs[:, :]))
                P.waits("T", [rc])
                smm = []
                for h in range(H):
                    smm.append(P.op("T", lambda e, s, h=h: e.matmul(
                        psC[0:1, 4 * h:4 * h + 4], ones[:, 0:1], rsb[:, 4 * h:4 * h + 4],
                        start=(h == 0), stop=(h == 15))))
                P.waits("V", smm[-1:])
                P.op("V", lambda e, s: e.tensor_copy(Sg[0:1, :], psC[0:1, 0:64]))
                VG()
                P.op("V", lambda e, s: e.reduce_sum(
                    Sg16f[0:1, 0:16], bass.AP(Sg, 0, [[64, 1], [4, 16], [1, 4]]), axis=AX.X))
                VG()
                w2_ = P.op("V", lambda e, s: e.tensor_copy(Sg16[0:1, 0:16], Sg16f[0:1, 0:16]))
                P.waits("T", [w2_])
                w3_ = P.op("T", lambda e, s: e.matmul(psC[0:16, 64:65], Sg16[0:1, 0:16], ones[0:1, 0:1],
                                                      start=True, stop=True))
                P.waits("V", [w3_, pe])
                P.op("V", lambda e, s: e.tensor_copy(sl(14), psC[0:16, 64:65]))
                VG()
                w5_ = P.op("V", lambda e, s: e.tensor_add(sl(15), sl(14), sl(13)))
                P.wait("S", w5_)
                w6_ = P.op("S", lambda e, s: e.activation(sl(16), sl(15), AF.Ln, bias=0.0, scale=1.0))
                P.wait("V", w6_)
                P.op("V", lambda e, s: e.tensor_mul(sl(17), sl(16), sl(10)))
                VG()
                P.op("V", lambda e, s: e.tensor_add(sl(18), sl(17), sl(5)))
                VG()
                P.op("V", lambda e, s: e.tensor_scalar(sl(19), sl(18), -1.0, C2, ALU.mult, ALU.add))
                VG()
                P.op("V", lambda e, s: e.tensor_scalar_min(sl(19), sl(19), 80.0))
                VG()
                wa_ = P.op("V", lambda e, s: e.tensor_copy(scrB[0:16, 2:3], sl(19)))
                VG()
                P.op("V", lambda e, s: e.tensor_copy(sl(20), scrB[0:16, 2:3]))
                VG()
                P.op("V", lambda e, s: e.tensor_scalar_add(sl(21), sl(20), LN512))
                VG()
                wc_ = P.op("V", lambda e, s, w=w: e.tensor_add(sl(23), sl(21), lnmuS[0:16, w:w + 1]))
                P.waits("T", [wa_])
                y1_ = P.op("T", lambda e, s: e.matmul(psT[0:1, 0:16], scrB[0:16, 2:3], ident[0:16, 0:16],
                                                      start=True, stop=True, is_transpose=True))
                P.waits("V", [y1_])
                y2_ = P.op("V", lambda e, s: e.tensor_copy(nmrow[0:1, 0:16], psT[0:1, 0:16]))
                P.waits("T", [y2_])
                y3_ = P.op("T", lambda e, s: e.matmul(psC[:, 0:16], ones[0:1, 0:128], nmrow[0:1, 0:16],
                                                      start=True, stop=True))
                P.waits("V", [y3_])
                P.op("V", lambda e, s: e.tensor_copy(bcA[:, :], psC[:, 0:16]))
                P.waits("S", [wc_])
                P.op("S", lambda e, s: e.activation(padp[:, :], scrA[0:16, 0:512], AF.Exp,
                                                    bias=sl(23), scale=-1.0))
                snap2 = P.snap()
                for eng in ("T", "S", "V"):
                    P.barrier_from(eng, snap2)

                lastex = {0: None, 1: None}
                pg = 0
                prev_n5 = None
                for h in range(H):
                    kh = h // 2
                    ro = (h % 2) * 64
                    groupinfo = []
                    exps = []
                    for g, (j0, j1) in enumerate(GROUPS):
                        ps = psA if pg == 0 else psB
                        if lastex[pg] is not None:
                            P.wait("T", lastex[pg])
                        for j in range(j0, j1):
                            kc = w * 4 + j
                            col = ((kc // 4) % 3) * 512 + (kc % 4) * 128
                            if j <= 3:
                                qlo, qhi = 0, (j + 1) * 128
                            elif j >= 8:
                                qlo, qhi = (j - 8) * 128, 512
                            else:
                                qlo, qhi = 0, 512
                            P.op("T", lambda e, s, kh=kh, ro=ro, col=col, j=j, j0=j0, ps=ps,
                                 qlo=qlo, qhi=qhi: e.matmul(
                                ps[:, (j - j0) * 512 + qlo:(j - j0) * 512 + qhi],
                                kt[kh][ro:ro + 64, col:col + 128],
                                qt[kh][ro:ro + 64, qlo:qhi],
                                start=True, stop=True))
                        cntT = ("T", P.cnt["T"])
                        P.wait("S", cntT)
                        pb = probs[g]
                        ex = P.op("S", lambda e, s, ps=ps, h=h, pb=pb: e.activation(
                            pb[:, :], ps[:, :], AF.Exp, bias=bcA[:, h:h + 1], scale=1.0))
                        exps.append(ex)
                        lastex[pg] = ex
                        groupinfo.append((g, j0, j1, pb))
                        pg ^= 1
                    P.waits("V", exps[-1:])
                    mk = []
                    for (g, j0, j1, pb) in groupinfo:
                        for j in range(j0, j1):
                            loc = (j - j0) * 512
                            if j <= 3:
                                b0 = j * 128
                                mk.append(P.op("V", lambda e, s, pb=pb, loc=loc, b0=b0: e.tensor_mul(
                                    pb[:, loc + b0:loc + b0 + 128], pb[:, loc + b0:loc + b0 + 128], tlow[:, :])))
                            elif j >= 8:
                                b0 = (j - 8) * 128
                                mk.append(P.op("V", lambda e, s, pb=pb, loc=loc, b0=b0: e.tensor_mul(
                                    pb[:, loc + b0:loc + b0 + 128], pb[:, loc + b0:loc + b0 + 128], thig[:, :])))
                    P.waits("T", mk[-1:] + exps[-1:])
                    if prev_n5 is not None:
                        P.wait("T", prev_n5)
                    first = True
                    for (g, j0, j1, pb) in groupinfo:
                        for j in range(j0, j1):
                            kc = w * 4 + j
                            slot12 = kc % 12
                            loc = (j - j0) * 512
                            if j <= 3:
                                qlo, qhi = 0, (j + 1) * 128
                            elif j >= 8:
                                qlo, qhi = (j - 8) * 128, 512
                            else:
                                qlo, qhi = 0, 512
                            P.op("T", lambda e, s, slot12=slot12, h=h, pb=pb, loc=loc,
                                 qlo=qlo, qhi=qhi, first=first: e.matmul(
                                psC[0:65, qlo:qhi],
                                vS[:, slot12 * 1088 + h * 68:slot12 * 1088 + h * 68 + 65],
                                pb[:, loc + qlo:loc + qhi],
                                start=first, stop=False))
                            first = False
                    pvlast = P.op("T", lambda e, s, h=h, w=w: e.matmul(
                        psC[0:65, 0:512],
                        pads[:, h * 65:h * 65 + 65],
                        padp[:, :],
                        start=False, stop=True))
                    P.waits("V", [pvlast])
                    n1 = P.op("V", lambda e, s: e.tensor_copy(scrA[64:65, :], psC[64:65, 0:512]))
                    VG()
                    P.op("V", lambda e, s: e.reciprocal(scrA[96:97, :], scrA[64:65, :]))
                    VG()
                    n2 = P.op("V", lambda e, s: e.tensor_copy(scrB[64:65, 0:512], scrA[96:97, :]))
                    psn = psA if pg == 0 else psB
                    P.waits("T", [n2])
                    n3 = P.op("T", lambda e, s, psn=psn: e.matmul(
                        psn[0:64, 0:512], ones[64:65, 0:64], scrB[64:65, 0:512], start=True, stop=True))
                    P.waits("V", [n3])
                    n4 = P.op("V", lambda e, s, psn=psn: e.tensor_copy(rbsb[:, :], psn[0:64, 0:512]))
                    VG()
                    n5 = P.op("V", lambda e, s, kh=kh, ro=ro: e.scalar_tensor_tensor(
                        atw[kh][ro:ro + 64, :], psC[0:64, 0:512], 1.0, rbsb[:, :], ALU.mult, ALU.mult))
                    prev_n5 = ("V", P.cnt["V"])
                    lastex[pg] = n4
                snap3 = P.snap()
                P.barrier_from("Y", snap3)
                for k in range(8):
                    P.dma("Y", lambda e, s, k=k, w=w: e.dma_start(
                        out=attnTs[k * 128:(k + 1) * 128, w * 512:(w + 1) * 512], in_=atw[k][:, :]))
                stage_done[w] = P.snap()
                for eng in ("T", "V", "S", "G"):
                    P.barrier_from(eng, stage_done[w])

            produce(0)
            produce(1)
            produce(2)
            attend(0)
            produce(3)
            attend(1)
            produce(4)
            attend(2)
            produce(5)
            attend(3)

            fP = P.snap()

            def endy(e, s, fP=fP):
                e.wait_ge(s["dY"], fP["dY"])
            P.ops.append(("Y", endy))
            P.qstream["Y"].append("w")

            def endg(e, s, fP=fP):
                e.wait_ge(s["dG"], fP["dG"])
            P.ops.append(("G", endg))
            P.qstream["G"].append("w")

            P.emit(blk0, sems)

        P2 = Plan(P.cnt)
        P2.spacers = P.spacers
        es2 = contextlib.ExitStack()
        with es2:
            CC2 = es2.enter_context
            W1_ = CC2(nc.sbuf_tensor("W1", [128, 8 * HFF], BF16))
            W2_ = CC2(nc.sbuf_tensor("W2", [128, 32 * E], BF16))
            WO_ = CC2(nc.sbuf_tensor("WO", [128, 8 * E], BF16))
            b1s = CC2(nc.sbuf_tensor("b1s", [128, 32], F32))
            bsum = CC2(nc.sbuf_tensor("bsum", [1, E], BF16))
            hT_ = CC2(nc.sbuf_tensor("hT", [128, 32 * SPAN], BF16))
            xnsp_ = CC2(nc.sbuf_tensor("xnsp", [128, 8 * SPAN], BF16))
            atsp_ = CC2(nc.sbuf_tensor("atsp", [128, 8 * SPAN], BF16))
            xsp_ = CC2(nc.sbuf_tensor("xsp", [128, 2 * E], F32))
            ost_ = CC2(nc.sbuf_tensor("ost", [128, 4 * 512], F32))
            aT4 = CC2(nc.sbuf_tensor("aT4", [4, SPAN], BF16))
            aTr = CC2(nc.sbuf_tensor("aTr", [1, 4 * SPAN], BF16))
            cS = CC2(nc.sbuf_tensor("cS", [16, SPAN], BF16))
            blk1 = CC2(nc.Block())
            W1 = [W1_[:, k * HFF:(k + 1) * HFF] for k in range(8)]
            W2 = [W2_[:, k * E:(k + 1) * E] for k in range(32)]
            WO = [WO_[:, k * E:(k + 1) * E] for k in range(8)]
            hT = [hT_[:, k * SPAN:(k + 1) * SPAN] for k in range(32)]
            xnsp = [xnsp_[:, k * SPAN:(k + 1) * SPAN] for k in range(8)]
            atsp = [atsp_[:, k * SPAN:(k + 1) * SPAN] for k in range(8)]
            xsp = [xsp_[:, i * E:(i + 1) * E] for i in range(2)]
            bosr = xsp_[0:1, 0:E]
            b2sr = xsp_[0:1, E:2 * E]
            ost = [ost_[:, i * 512:(i + 1) * 512] for i in range(4)]

            pb0 = P2.snap()
            for eng in ("T", "V", "S", "G", "Y"):
                P2.barrier_from(eng, pb0)

            for k in range(8):
                P2.dma("G", lambda e, s, k=k: e.dma_start(out=W1[k], in_=w1T[k * 128:(k + 1) * 128, :]))
            for k in range(32):
                P2.dma("G", lambda e, s, k=k: e.dma_start(out=W2[k], in_=w2T[k * 128:(k + 1) * 128, :]))
            for k in range(8):
                P2.dma("G", lambda e, s, k=k: e.dma_start(out=WO[k], in_=woT[k * 128:(k + 1) * 128, :]))
            l1 = P2.dma("Y", lambda e, s: e.dma_start(out=b1s[:, :], in_=b1c))
            l2 = P2.dma("Y", lambda e, s: e.dma_start(out=bosr, in_=bor))
            l3 = P2.dma("Y", lambda e, s: e.dma_start(out=b2sr, in_=b2r))
            P2.waits("V", [l2, l3])
            P2.op("V", lambda e, s: e.tensor_add(bsum[0:1, :], bosr[0:1, :], b2sr[0:1, :]))
            wsnap = P2.snap()
            for eng in ("T", "S", "V", "Y"):
                P2.barrier_from(eng, wsnap)

            for sp in range(NSPAN):
                sb2 = 0
                lds = []
                for k in range(8):
                    lds.append(P2.dma("Y", lambda e, s, k=k, sp=sp, sb2=sb2: e.dma_start(
                        out=xnsp[k][:, :],
                        in_=xnTs[k * 128:(k + 1) * 128, sp * SPAN:(sp + 1) * SPAN])))
                    lds.append(P2.dma("Y", lambda e, s, k=k, sp=sp, sb2=sb2: e.dma_start(
                        out=atsp[k][:, :],
                        in_=attnTs[k * 128:(k + 1) * 128, sp * SPAN:(sp + 1) * SPAN])))
                for tt in range(2):
                    lds.append(P2.dma("Y", lambda e, s, tt=tt, sp=sp, sb2=sb2: e.dma_start(
                        out=xsp[tt][:, :],
                        in_=xh[512 + sp * SPAN + tt * 128: 512 + sp * SPAN + (tt + 1) * 128, :])))
                P2.waits("T", lds)
                am = []
                for k in range(8):
                    am.append(P2.op("T", lambda e, s, k=k, sb2=sb2: e.matmul(
                        psC[0:4, 0:SPAN], eoT[k][:, 1:8:2], xnsp[k][:, :],
                        start=(k == 0), stop=(k == 7))))
                P2.waits("S", am[-1:])
                ae = P2.op("S", lambda e, s: e.activation(aT4[:, :], psC[0:4, 0:SPAN], AF.Identity,
                                                          bias=0.0, scale=sclb[0:4, 0:1]))
                P2.wait("Y", ae)
                acp = []
                for i in range(4):
                    acp.append(P2.dma("Y", lambda e, s, i=i: e.dma_start(
                        out=aTr[0:1, i * SPAN:(i + 1) * SPAN], in_=aT4[i:i + 1, :])))
                P2.waits("T", acp)
                hdone = []
                hbanks = [psA[:, 0:SPAN], psA[:, 512:512 + SPAN], psA[:, 1024:1024 + SPAN],
                          psB[:, 0:SPAN], psB[:, 512:512 + SPAN]]
                for hc in range(32):
                    ph = hbanks[hc % 5]
                    if hc >= 5 and hc % 5 == 0:
                        P2.wait("T", hdone[hc - 1])
                    mm = []
                    for k in range(8):
                        mm.append(P2.op("T", lambda e, s, hc=hc, k=k, ph=ph: e.matmul(
                            ph, W1[k][:, hc * 128:(hc + 1) * 128],
                            xnsp[k][:, :],
                            start=(k == 0), stop=False)))
                    i = hc // 8
                    mm.append(P2.op("T", lambda e, s, hc=hc, i=i, ph=ph: e.matmul(
                        ph, EVN[0:1, i * E + (hc % 8) * 128: i * E + (hc % 8 + 1) * 128],
                        aTr[0:1, i * SPAN:(i + 1) * SPAN],
                        start=False, stop=True)))
                    P2.waits("S", mm[-1:])
                    hdone.append(P2.op("S", lambda e, s, hc=hc, ph=ph: e.activation(
                        hT[hc][:, :], ph, AF.Gelu, bias=b1s[:, hc:hc + 1], scale=1.0)))
                P2.waits("T", hdone[-1:])
                cm = []
                for hk in range(32):
                    cm.append(P2.op("T", lambda e, s, hk=hk: e.matmul(
                        psC[0:16, 0:SPAN], ODP[:, hk * 16:(hk + 1) * 16], hT[hk][:, :],
                        start=(hk == 0), stop=(hk == 31))))
                P2.waits("S", cm[-1:])
                ce = P2.op("S", lambda e, s: e.activation(cS[:, :], psC[0:16, 0:SPAN], AF.Identity,
                                                          bias=0.0, scale=sclb[0:16, 0:1]))
                P2.wait("T", ce)
                for tt in range(2):
                    vos = []
                    for oc2 in range(2):
                        po = psB[:, 1024:1536] if (tt * 2 + oc2) % 2 == 0 else psC[:, 0:512]
                        ob = ost[tt * 2 + oc2]
                        mm = []
                        for hk in range(32):
                            mm.append(P2.op("T", lambda e, s, hk=hk, tt=tt, oc2=oc2, po=po: e.matmul(
                                po, hT[hk][:, tt * 128:(tt + 1) * 128],
                                W2[hk][:, oc2 * 512:(oc2 + 1) * 512],
                                start=(hk == 0), stop=False)))
                        for k in range(8):
                            mm.append(P2.op("T", lambda e, s, k=k, tt=tt, oc2=oc2, po=po, sb2=sb2: e.matmul(
                                po, atsp[k][:, tt * 128:(tt + 1) * 128],
                                WO[k][:, oc2 * 512:(oc2 + 1) * 512],
                                start=False, stop=False)))
                        mm.append(P2.op("T", lambda e, s, tt=tt, oc2=oc2, po=po: e.matmul(
                            po, cS[:, tt * 128:(tt + 1) * 128],
                            E2s[:, oc2 * 512:(oc2 + 1) * 512],
                            start=False, stop=False)))
                        mm.append(P2.op("T", lambda e, s, oc2=oc2, po=po: e.matmul(
                            po, ones[0:1, 0:128], bsum[0:1, oc2 * 512:(oc2 + 1) * 512],
                            start=False, stop=True)))
                        P2.waits("V", mm[-1:])
                        vo = P2.op("V", lambda e, s, tt=tt, oc2=oc2, po=po, ob=ob, sb2=sb2: e.scalar_tensor_tensor(
                            ob[:, :], po, 1.0,
                            xsp[tt][:, oc2 * 512:(oc2 + 1) * 512],
                            ALU.mult, ALU.add))
                        vos.append(vo)
                        P2.wait("T", vo)
                        if oc2 == 1:
                            P2.waits("Y", vos)
                            st = P2.dma("Y", lambda e, s, tt=tt, sp=sp: e.dma_start(
                                out=out[sp * SPAN + tt * 128: sp * SPAN + (tt + 1) * 128, 0:E],
                                in_=ost_[:, tt * 1024:(tt + 1) * 1024]))
                            P2.wait("V", st)
                spsnap = P2.snap()
                for eng in ("T", "V", "S"):
                    P2.barrier_from(eng, spsnap)

            final = P2.snap()

            def fin(e, s, final=final):
                e.wait_ge(s["dY"], final["dY"])
            P2.ops.append(("Y", fin))

            P2.emit(blk1, sems)

    return nc


def kernel(L0_attn_x, L0_attn_wq, L0_attn_bq, L0_attn_wk, L0_attn_bk,
           L0_attn_wv, L0_attn_bv, L0_attn_wo, L0_attn_bo,
           L0_ffn_W1, L0_ffn_b1, L0_ffn_W2, L0_ffn_b2, L0_ffn_scale):
    x = np.ascontiguousarray(np.asarray(L0_attn_x, dtype=np.float32))
    wq = np.asarray(L0_attn_wq, np.float32)
    wk = np.asarray(L0_attn_wk, np.float32)
    wv = np.asarray(L0_attn_wv, np.float32)
    wo = np.asarray(L0_attn_wo, np.float32)
    W1 = np.asarray(L0_ffn_W1, np.float32)
    W2 = np.asarray(L0_ffn_W2, np.float32)
    bq = np.asarray(L0_attn_bq, np.float32)
    bk = np.asarray(L0_attn_bk, np.float32)
    bv = np.asarray(L0_attn_bv, np.float32)
    bo = np.asarray(L0_attn_bo, np.float32)
    b1 = np.asarray(L0_ffn_b1, np.float32)
    b2 = np.asarray(L0_ffn_b2, np.float32)
    scale = np.float32(np.asarray(L0_ffn_scale))

    shared = {
        "wqT": np.ascontiguousarray(wq.T), "wkT": np.ascontiguousarray(wk.T),
        "wvT": np.ascontiguousarray(wv.T), "woT": np.ascontiguousarray(wo.T),
        "w1T": np.ascontiguousarray(W1.T), "w2T": np.ascontiguousarray(W2.T),
        "bqc": np.ascontiguousarray(bq.reshape(8, 128).T),
        "bkc": np.ascontiguousarray(bk.reshape(8, 128).T),
        "bvr": bv.reshape(1, E).copy(), "bor": bo.reshape(1, E).copy(),
        "b2r": b2.reshape(1, E).copy(),
        "b1c": np.ascontiguousarray(b1.reshape(32, 128).T),
        "sclc": np.broadcast_to(scale, (16, 1)).copy(),
    }
    in_maps = []
    for core in range(8):
        b, wi = core // 4, core % 4
        t0 = wi * TOWN
        xhc = np.zeros((TLOC, E), np.float32)
        if wi > 0:
            xhc[0:512] = x[b, t0 - 512:t0]
        xhc[512:512 + TOWN] = x[b, t0:t0 + TOWN]
        if wi < 3:
            xhc[512 + TOWN:TLOC] = x[b, t0 + TOWN:t0 + TOWN + 512]
        lam = np.ones(NCH, np.float32)
        if wi == 0:
            lam[0:4] = 0
        if wi == 3:
            lam[20:24] = 0
        mu = np.zeros(NOWN, np.float32)
        if wi == 0:
            mu[0] = 1
        if wi == 3:
            mu[3] = 1
        psel = np.zeros((16, 16 * 65), np.float32)
        for h in range(H):
            psel[h, h * 65:h * 65 + 64] = -1.0
            psel[h, h * 65 + 64] = 1.0
        lnm = np.full((16, NOWN), -1e30, np.float32)
        for w in range(NOWN):
            if mu[w] > 0:
                lnm[:, w] = 0.0
        in_maps.append(dict(shared,
                            xh=xhc,
                            x16=np.ascontiguousarray(x[b, 1:17]),
                            lamt=np.ascontiguousarray(np.broadcast_to(lam, (128, NCH))),
                            padsel=psel, lnmu=lnm))

    nc = build_program()
    res = run_bass_kernel_spmd(nc, in_maps, core_ids=list(range(8)))
    outp = np.zeros((2, 8192, E), np.float32)
    for core in range(8):
        b, wi = core // 4, core % 4
        outp[b, wi * TOWN:(wi + 1) * TOWN] = res.results[core]["out"]
    return outp


# version 21
# speedup vs baseline: 1.0845x; 1.0025x over previous
import math
import numpy as np
import concourse.bass as bass
import concourse.mybir as mybir
from concourse.bass_utils import run_bass_kernel_spmd

F32 = mybir.dt.float32
BF16 = mybir.dt.bfloat16
AF = mybir.ActivationFunctionType
ALU = mybir.AluOpType
AX = mybir.AxisListType

E = 1024
H = 16
WS = 512
NOWN = 4
NLW = 6
TLOC = NLW * WS
NCH = TLOC // 128
TOWN = NOWN * WS
LN512 = float(math.log(512.0))
C2 = 30.0
HFF = 4096
SPAN = 256
NSPAN = TOWN // SPAN
GROUPS = [(0, 3), (3, 6), (6, 9), (9, 12)]


class Plan:
    def __init__(self, cnt=None):
        self.ops = []
        self.cnt = dict(cnt) if cnt else {"T": 0, "V": 0, "S": 0, "G": 0, "dY": 0, "dG": 0}
        self.base = dict(self.cnt)
        self.cover = {}
        self.spacers = {}
        self.qstream = {"Y": [], "G": []}
        self.defer = []

    SCALE = 256

    def op(self, eng, fn):
        self.cnt[eng] += self.SCALE
        tgt = self.cnt[eng]

        def run(e, sems, fn=fn, eng=eng):
            fn(e, sems).then_inc(sems[eng], self.SCALE)
        self.ops.append((eng, run))
        return (eng, tgt)

    def dma(self, eng, fn):
        sem = {"Y": "dY", "G": "dG"}[eng]
        self.cnt[sem] += 16
        tgt = self.cnt[sem]

        def run(e, sems, fn=fn, sem=sem):
            fn(e, sems).then_inc(sems[sem], 16)
        self.qstream[eng].append(16)
        self.ops.append((eng, run))
        return (sem, tgt)

    FLUSH_K = 1

    def _spacer(self, sem):
        self.cnt[sem] += self.SCALE

        def run(e, sems, sem=sem):
            e.drain()
            e.sem_inc(sems[sem], self.SCALE)
        self.ops.append((sem, run))

    def wait(self, eng, dep):
        sem, val = dep
        if val <= 0:
            return
        if sem in getattr(self, "spacers", {}) and sem != eng:
            cu, sv = self.cover.get(sem, (0, 0))
            if val > cu:
                covered = self.cnt[sem]
                for _ in range(self.FLUSH_K):
                    self._spacer(sem)
                self.cover[sem] = (covered, self.cnt[sem])
                cu, sv = self.cover[sem]
            val = sv

        if sem in ("dY", "dG"):
            cell = [val]
            self.defer.append((sem, cell))

            def run(e, sems, sem=sem, cell=cell):
                e.wait_ge(sems[sem], cell[0])
        else:
            def run(e, sems, sem=sem, val=val):
                e.wait_ge(sems[sem], val)
        if eng in ("Y", "G"):
            self.qstream[eng].append("w")
        self.ops.append((eng, run))

    def waits(self, eng, deps):
        best = {}
        for (sem, val) in deps:
            if val > 0:
                best[sem] = max(best.get(sem, 0), val)
        for sem, val in best.items():
            self.wait(eng, (sem, val))

    def snap(self):
        return dict(self.cnt)

    def barrier_from(self, eng, snap):
        self.waits(eng, [(k, v) for k, v in snap.items() if k != {"Y": "dY", "G": "dG"}.get(eng, eng)])

    def finalize(self):
        for q, sem in (("Y", "dY"), ("G", "dG")):
            cum = self.base[sem]
            bs = [cum]
            for it in self.qstream[q]:
                if it == "w":
                    bs.append(cum)
                else:
                    cum += it
            bs.append(cum)
            bs = sorted(set(bs))
            for s2, cell in self.defer:
                if s2 != sem:
                    continue
                v = cell[0]
                cell[0] = next((b for b in bs if b >= v), bs[-1])
        self.defer = []

    def emit(self, blk, sems):
        self.finalize()

        def mk(name):
            def body(e):
                for eng, fn in self.ops:
                    if eng == name:
                        fn(e, sems)
            return body
        blk.tensor(mk("T"))
        blk.vector(mk("V"))
        blk.scalar(mk("S"))
        blk.gpsimd(mk("G"))
        blk.sync(mk("Y"))


def build_program():
    nc = bass.Bass("TRN2", target_bir_lowering=False)

    xh = nc.dram_tensor("xh", [TLOC, E], F32, kind="ExternalInput").ap()
    x16 = nc.dram_tensor("x16", [16, E], F32, kind="ExternalInput").ap()
    wqT = nc.dram_tensor("wqT", [E, E], F32, kind="ExternalInput").ap()
    wkT = nc.dram_tensor("wkT", [E, E], F32, kind="ExternalInput").ap()
    wvT = nc.dram_tensor("wvT", [E, E], F32, kind="ExternalInput").ap()
    woT = nc.dram_tensor("woT", [E, E], F32, kind="ExternalInput").ap()
    w1T = nc.dram_tensor("w1T", [E, HFF], F32, kind="ExternalInput").ap()
    w2T = nc.dram_tensor("w2T", [HFF, E], F32, kind="ExternalInput").ap()
    bqc = nc.dram_tensor("bqc", [128, 8], F32, kind="ExternalInput").ap()
    bkc = nc.dram_tensor("bkc", [128, 8], F32, kind="ExternalInput").ap()
    bvr = nc.dram_tensor("bvr", [1, E], F32, kind="ExternalInput").ap()
    b1c = nc.dram_tensor("b1c", [128, 32], F32, kind="ExternalInput").ap()
    bor = nc.dram_tensor("bor", [1, E], F32, kind="ExternalInput").ap()
    b2r = nc.dram_tensor("b2r", [1, E], F32, kind="ExternalInput").ap()
    sclc = nc.dram_tensor("sclc", [16, 1], F32, kind="ExternalInput").ap()
    lamt = nc.dram_tensor("lamt", [128, NCH], F32, kind="ExternalInput").ap()
    padsel = nc.dram_tensor("padsel", [16, 16 * 65], F32, kind="ExternalInput").ap()
    lnmu = nc.dram_tensor("lnmu", [16, NOWN], F32, kind="ExternalInput").ap()
    out = nc.dram_tensor("out", [TOWN, E], F32, kind="ExternalOutput").ap()
    xnTs = nc.dram_tensor("xnTs", [E, TOWN], BF16, kind="Internal").ap()
    attnTs = nc.dram_tensor("attnTs", [E, TOWN], BF16, kind="Internal").ap()

    import contextlib
    es0 = contextlib.ExitStack()
    with es0:
        CC0 = es0.enter_context
        semT = CC0(nc.semaphore("semT"))
        semV = CC0(nc.semaphore("semV"))
        semS = CC0(nc.semaphore("semS"))
        semG = CC0(nc.semaphore("semG"))
        semdY = CC0(nc.semaphore("semdY"))
        semdG = CC0(nc.semaphore("semdG"))
        psA = CC0(nc.psum_tensor("psA", [128, 1536], F32))
        psB = CC0(nc.psum_tensor("psB", [128, 1536], F32))
        psC = CC0(nc.psum_tensor("psC", [128, 512], F32))
        psT = CC0(nc.psum_tensor("psT", [128, 1024], BF16))
        ones = CC0(nc.sbuf_tensor("ones", [128, 128], BF16))
        ident = CC0(nc.sbuf_tensor("ident", [128, 128], BF16))
        tlow = CC0(nc.sbuf_tensor("tlow", [128, 128], BF16))
        thig = CC0(nc.sbuf_tensor("thig", [128, 128], BF16))
        iotaf = CC0(nc.sbuf_tensor("iotaf", [128, 128], F32))
        iotap = CC0(nc.sbuf_tensor("iotap", [128, 128], F32))
        xn16 = CC0(nc.sbuf_tensor("xn16", [16, E], BF16))
        eoT_ = CC0(nc.sbuf_tensor("eoT", [128, 8 * 16], BF16))
        E2s = CC0(nc.sbuf_tensor("E2s", [16, E], BF16))
        EVN = CC0(nc.sbuf_tensor("EVN", [1, 4 * E], BF16))
        ODP = CC0(nc.sbuf_tensor("ODP", [128, 32 * 16], BF16))
        sclb = CC0(nc.sbuf_tensor("sclb", [16, 1], F32))
        lamb = CC0(nc.sbuf_tensor("lamb", [128, NCH], F32))
        lnmuS = CC0(nc.sbuf_tensor("lnmuS", [16, NOWN], F32))
        smll = CC0(nc.sbuf_tensor("smll", [128, 64], F32))
        sm2 = CC0(nc.sbuf_tensor("sm2", [128, 64], F32))
        junkS = CC0(nc.sbuf_tensor("junkS", [128, 2], F32))
        junkG = CC0(nc.sbuf_tensor("junkG", [128, 2], F32))
        eoT = [eoT_[:, k * 16:(k + 1) * 16] for k in range(8)]
        sems = dict(T=semT, V=semV, S=semS, G=semG, dY=semdY, dG=semdG)

        def sl(i):
            return smll[0:16, i:i + 1]

        P = Plan()
        P.spacers = {
            "V": lambda e: e.tensor_copy(smll[:, 62:63], smll[:, 63:64]),
            "S": lambda e: e.copy(junkS[:, 0:1], junkS[:, 1:2]),
            "G": lambda e: e.memset(junkG[:, 0:1], 0.0),
        }

        def VG():
            P.op("V", lambda e, s: e.tensor_copy(smll[:, 62:63], smll[:, 63:64]))

        es1 = contextlib.ExitStack()
        with es1:
            CC1 = es1.enter_context
            wq_ = CC1(nc.sbuf_tensor("wq", [128, 8 * E], BF16))
            wk_ = CC1(nc.sbuf_tensor("wk", [128, 8 * E], BF16))
            wv_ = CC1(nc.sbuf_tensor("wv", [128, 8 * E], BF16))
            bqs = CC1(nc.sbuf_tensor("bqs", [128, 8], F32))
            bks = CC1(nc.sbuf_tensor("bks", [128, 8], F32))
            bvs = CC1(nc.sbuf_tensor("bvs", [1, E], BF16))
            pads = CC1(nc.sbuf_tensor("pads", [16, 16 * 65], BF16))
            xt_ = CC1(nc.sbuf_tensor("xt", [128, 4 * E], F32))
            xsq_ = CC1(nc.sbuf_tensor("xsq", [128, 2 * E], F32))
            xntm_ = CC1(nc.sbuf_tensor("xntm", [128, 4 * E], BF16))
            xnT_ = CC1(nc.sbuf_tensor("xnT", [128, 8 * 2 * WS], BF16))
            kt_ = CC1(nc.sbuf_tensor("kt", [128, 8 * 3 * WS], BF16))
            vS = CC1(nc.sbuf_tensor("vS", [128, 12 * 1088], BF16))
            qt_ = CC1(nc.sbuf_tensor("qt", [128, 8 * WS], BF16))
            probs_ = CC1(nc.sbuf_tensor("probs", [128, 4 * 1536], BF16))
            rs = CC1(nc.sbuf_tensor("rs", [128, 64], F32))
            rsb = CC1(nc.sbuf_tensor("rsb", [128, 64], BF16))
            atw_ = CC1(nc.sbuf_tensor("atw", [128, 8 * WS], BF16))
            mkr = CC1(nc.sbuf_tensor("mkr", [16, 8], F32))
            scrA = CC1(nc.sbuf_tensor("scrA", [128, 512], F32))
            scrB = CC1(nc.sbuf_tensor("scrB", [128, 512], BF16))
            rbsb = CC1(nc.sbuf_tensor("rbsb", [64, 512], F32))
            Sg = CC1(nc.sbuf_tensor("Sg", [1, 64], F32))
            Sg16 = CC1(nc.sbuf_tensor("Sg16", [1, 16], BF16))
            Sg16f = CC1(nc.sbuf_tensor("Sg16f", [1, 16], F32))
            padp = CC1(nc.sbuf_tensor("padp", [16, 512], BF16))
            nmrow = CC1(nc.sbuf_tensor("nmrow", [1, 16], BF16))
            bcA = CC1(nc.sbuf_tensor("bcA", [128, 16], F32))
            bcB = CC1(nc.sbuf_tensor("bcB", [128, 16], F32))
            bcC = CC1(nc.sbuf_tensor("bcC", [128, 16], F32))
            Jsl = CC1(nc.sbuf_tensor("Jsl", [128, 31], BF16))
            blk0 = CC1(nc.Block())
            wq = [wq_[:, k * E:(k + 1) * E] for k in range(8)]
            wk = [wk_[:, k * E:(k + 1) * E] for k in range(8)]
            wv = [wv_[:, k * E:(k + 1) * E] for k in range(8)]
            xt = [xt_[:, i * E:(i + 1) * E] for i in range(4)]
            x16f = xt_[0:16, 0:E]
            x16sq = xsq_[0:16, 0:E]
            xsq = [xsq_[:, i * E:(i + 1) * E] for i in range(2)]
            xntm = [xntm_[:, i * E:(i + 1) * E] for i in range(4)]
            xnT = [xnT_[:, k * 2 * WS:(k + 1) * 2 * WS] for k in range(8)]
            kt = [kt_[:, k * 3 * WS:(k + 1) * 3 * WS] for k in range(8)]
            qt = [qt_[:, k * WS:(k + 1) * WS] for k in range(8)]
            sqb = [xntm_[:, k * WS:(k + 1) * WS] for k in range(8)]
            probs = [probs_[:, i * 1536:(i + 1) * 1536] for i in range(4)]
            r1scr = probs_[:, 0:1536]
            atw = [atw_[:, k * WS:(k + 1) * WS] for k in range(8)]

            for k in range(8):
                P.dma("G", lambda e, s, k=k: e.dma_start(out=wq[k], in_=wqT[k * 128:(k + 1) * 128, :]))
                P.dma("G", lambda e, s, k=k: e.dma_start(out=wk[k], in_=wkT[k * 128:(k + 1) * 128, :]))
                P.dma("G", lambda e, s, k=k: e.dma_start(out=wv[k], in_=wvT[k * 128:(k + 1) * 128, :]))
            P.dma("G", lambda e, s: e.dma_start(out=bvs[:, :], in_=bvr))
            P.dma("G", lambda e, s: e.dma_start(out=pads[:, :], in_=padsel))
            P.dma("Y", lambda e, s: e.dma_start(out=bqs[:, :], in_=bqc))
            P.dma("Y", lambda e, s: e.dma_start(out=bks[:, :], in_=bkc))
            P.dma("Y", lambda e, s: e.dma_start(out=sclb[:, :], in_=sclc))
            P.dma("Y", lambda e, s: e.dma_start(out=lamb[:, :], in_=lamt))
            x16ld = P.dma("Y", lambda e, s: e.dma_start(out=x16f, in_=x16))
            P.dma("Y", lambda e, s: e.dma_start(out=lnmuS[:, :], in_=lnmu))

            g1 = P.op("G", lambda e, s: e.iota(iotaf[:, :], [[1, 128]], channel_multiplier=0,
                                               allow_small_or_imprecise_dtypes=True))
            g2 = P.op("G", lambda e, s: e.iota(iotap[:, :], [[0, 128]], channel_multiplier=1,
                                               allow_small_or_imprecise_dtypes=True))
            g3 = P.op("G", lambda e, s: e.memset(Jsl[:, :], 0.0))
            g4 = P.op("G", lambda e, s: e.memset(ODP[:, :], 0.0))
            P.waits("V", [g1, g2, g3])
            c1 = P.op("V", lambda e, s: e.tensor_tensor(ident[:, :], iotaf[:, :], iotap[:, :], ALU.is_equal))
            P.op("V", lambda e, s: e.tensor_tensor(tlow[:, :], iotaf[:, :], iotap[:, :], ALU.is_le))
            P.op("V", lambda e, s: e.tensor_tensor(thig[:, :], iotaf[:, :], iotap[:, :], ALU.is_ge))
            P.op("V", lambda e, s: e.memset(Jsl[0:64, 15:16], 1.0))
            P.op("V", lambda e, s: e.memset(Jsl[64:128, 16:17], 1.0))
            P.op("V", lambda e, s: e.memset(ones[:, :], 1.0))

            P.wait("G", x16ld)
            gx = P.op("G", lambda e, s: e.tensor_mul(x16sq[:, :], x16f[:, :], x16f[:, :]))
            P.wait("V", x16ld)
            P.op("V", lambda e, s: e.reduce_sum(sm2[0:16, 0:1], x16f[:, :], axis=AX.X))
            P.wait("V", gx)
            P.op("V", lambda e, s: e.reduce_sum(sm2[0:16, 1:2], x16sq[:, :], axis=AX.X))
            P.op("V", lambda e, s: e.tensor_scalar_mul(sm2[0:16, 2:3], sm2[0:16, 0:1], 1.0 / E))
            P.op("V", lambda e, s: e.tensor_scalar_mul(sm2[0:16, 3:4], sm2[0:16, 1:2], 1.0 / E))
            P.op("V", lambda e, s: e.tensor_mul(sm2[0:16, 4:5], sm2[0:16, 2:3], sm2[0:16, 2:3]))
            VG()
            P.op("V", lambda e, s: e.tensor_sub(sm2[0:16, 5:6], sm2[0:16, 3:4], sm2[0:16, 4:5]))
            VG()
            v7 = P.op("V", lambda e, s: e.tensor_scalar_add(sm2[0:16, 6:7], sm2[0:16, 5:6], 1e-5))
            P.wait("S", v7)
            s1 = P.op("S", lambda e, s: e.sqrt(sm2[0:16, 7:8], sm2[0:16, 6:7]))
            P.wait("V", s1)
            P.op("V", lambda e, s: e.reciprocal(sm2[0:16, 8:9], sm2[0:16, 7:8]))
            VG()
            P.op("V", lambda e, s: e.tensor_mul(sm2[0:16, 9:10], sm2[0:16, 2:3], sm2[0:16, 8:9]))
            VG()
            v10 = P.op("V", lambda e, s: e.tensor_scalar_mul(sm2[0:16, 10:11], sm2[0:16, 9:10], -1.0))
            P.wait("S", v10)
            sx = P.op("S", lambda e, s: e.activation(xn16[:, :], x16f[:, :], AF.Identity,
                                                     bias=sm2[0:16, 10:11], scale=sm2[0:16, 8:9]))
            P.waits("T", [sx, c1])
            trl = []
            for k in range(8):
                trl.append(P.op("T", lambda e, s, k=k: e.matmul(
                    psT[:, k * 128:k * 128 + 16], xn16[:, k * 128:(k + 1) * 128], ident[0:16, 0:16],
                    start=(k == 0), stop=(k == 7), is_transpose=True)))
            P.waits("V", trl[-1:])
            ev = []
            for k in range(8):
                ev.append(P.op("V", lambda e, s, k=k: e.tensor_copy(eoT[k][:, :], psT[:, k * 128:k * 128 + 16])))
            ez = P.op("V", lambda e, s: e.memset(E2s[:, :], 0.0))
            P.waits("Y", [sx, ez])
            for i in range(4):
                P.dma("Y", lambda e, s, i=i: e.dma_start(out=EVN[0:1, i * E:(i + 1) * E], in_=xn16[2 * i:2 * i + 1, :]))
            for m in range(16):
                i, jpp = m // 4, m % 4
                P.dma("Y", lambda e, s, i=i, jpp=jpp, m=m: e.dma_start(
                    out=E2s[m:m + 1, i * 256:(i + 1) * 256],
                    in_=xn16[8 + 2 * i:9 + 2 * i, jpp:jpp + E - 4 + 1:4],
                    allow_slow_non_contiguous=True))
            P.waits("V", [g4] + ev)
            for hk in range(32):
                k, jpp = hk % 8, hk // 8
                P.op("V", lambda e, s, hk=hk, k=k, jpp=jpp: e.tensor_copy(
                    bass.AP(ODP, hk * 16 + jpp, [[32 * 16, 128], [4, 4], [1, 1]]),
                    eoT[k][:, 9:16:2]))

            snap = P.snap()
            for eng in ("T", "V", "S", "G", "Y"):
                P.barrier_from(eng, snap)

            stage_done = {}
            sqred = {}
            prod_done = {}

            def produce(lw):
                slot2 = lw % 2
                slot3 = lw % 3
                if lw - 1 in prod_done:
                    P.barrier_from("Y", prod_done[lw - 1])
                xload = []
                for tc in range(4):
                    xload.append(P.dma("Y", lambda e, s, lw=lw, tc=tc: e.dma_start(
                        out=xt[tc], in_=xh[lw * 512 + tc * 128: lw * 512 + (tc + 1) * 128, :])))
                for tc in range(4):
                    xb = xt[tc]
                    sq = xsq[tc % 2]
                    P.wait("G", xload[tc])
                    if tc >= 2:
                        P.wait("G", sqred[tc - 2])
                    gq = P.op("G", lambda e, s, xb=xb, sq=sq: e.tensor_mul(sq, xb, xb))
                    P.wait("V", xload[tc])
                    P.op("V", lambda e, s, xb=xb, tc=tc: e.reduce_sum(sm2[:, 16 + tc:17 + tc], xb, axis=AX.X))
                    P.wait("V", gq)
                    sqred[tc] = P.op("V", lambda e, s, sq=sq, tc=tc: e.reduce_sum(sm2[:, 20 + tc:21 + tc], sq, axis=AX.X))
                P.op("V", lambda e, s: e.tensor_scalar_mul(sm2[:, 24:28], sm2[:, 16:20], 1.0 / E))
                P.op("V", lambda e, s: e.tensor_scalar_mul(sm2[:, 28:32], sm2[:, 20:24], 1.0 / E))
                P.op("V", lambda e, s: e.tensor_mul(sm2[:, 32:36], sm2[:, 24:28], sm2[:, 24:28]))
                VG()
                P.op("V", lambda e, s: e.tensor_sub(sm2[:, 36:40], sm2[:, 28:32], sm2[:, 32:36]))
                VG()
                a7 = P.op("V", lambda e, s: e.tensor_scalar_add(sm2[:, 40:44], sm2[:, 36:40], 1e-5))
                P.wait("S", a7)
                b1_ = P.op("S", lambda e, s: e.sqrt(sm2[:, 44:48], sm2[:, 40:44]))
                P.wait("V", b1_)
                P.op("V", lambda e, s: e.reciprocal(sm2[:, 48:52], sm2[:, 44:48]))
                VG()
                P.op("V", lambda e, s: e.tensor_mul(sm2[:, 52:56], sm2[:, 24:28], sm2[:, 48:52]))
                VG()
                P.op("V", lambda e, s: e.tensor_scalar_mul(sm2[:, 56:60], sm2[:, 52:56], -1.0))
                evacs = []
                for tc in range(4):
                    xb = xt[tc]
                    VG()
                    nrm = P.op("V", lambda e, s, xb=xb, tc=tc: e.tensor_scalar(
                        xntm[tc], xb, sm2[:, 56 + tc:57 + tc], sm2[:, 48 + tc:49 + tc],
                        ALU.add, ALU.mult))
                    P.wait("T", nrm)
                    trl2 = []
                    for k in range(8):
                        trl2.append(P.op("T", lambda e, s, tc=tc, k=k: e.matmul(
                            psT[:, k * 128:(k + 1) * 128], xntm[tc][:, k * 128:(k + 1) * 128], ident[:, :],
                            start=(k == 0), stop=(k == 7), is_transpose=True)))
                    P.waits("V", trl2[-1:])
                    for k in range(8):
                        col = slot2 * 512 + tc * 128
                        evacs.append(P.op("V", lambda e, s, k=k, col=col: e.tensor_copy(
                            xnT[k][:, col:col + 128], psT[:, k * 128:(k + 1) * 128])))
                if 1 <= lw <= 4:
                    P.waits("Y", evacs[-8:])
                    for k in range(8):
                        P.dma("Y", lambda e, s, k=k, lw=lw, slot2=slot2: e.dma_start(
                            out=xnTs[k * 128:(k + 1) * 128, (lw - 1) * 512:lw * 512],
                            in_=xnT[k][:, slot2 * 512:slot2 * 512 + 512]))
                P.waits("T", evacs)
                banks6m = [(psA, 0), (psA, 512), (psA, 1024), (psB, 0), (psB, 512), (psB, 1024)]
                banks6 = [h_[:, o_:o_ + 512] for (h_, o_) in banks6m]
                kdone = []
                for oc in range(8):
                    if oc >= 6:
                        P.wait("T", kdone[oc - 6])
                    half = banks6[oc % 6]
                    mm = []
                    for k in range(8):
                        mm.append(P.op("T", lambda e, s, oc=oc, k=k, half=half, slot2=slot2: e.matmul(
                            half, wk[k][:, oc * 128:(oc + 1) * 128],
                            xnT[k][:, slot2 * 512:slot2 * 512 + 512],
                            start=(k == 0), stop=(k == 7))))
                    P.waits("V", mm[-1:])
                    kdone.append(P.op("V", lambda e, s, oc=oc, half=half, slot3=slot3: e.tensor_scalar(
                        kt[oc][:, slot3 * 512:slot3 * 512 + 512], half,
                        bks[:, oc:oc + 1], None, ALU.add)))
                vprev = []
                for tc in range(4):
                    kc = lw * 4 + tc
                    slot12 = kc % 12
                    for oc2 in range(2):
                        idx = tc * 2 + oc2
                        if idx >= 6:
                            P.wait("T", vprev[idx - 6])
                        else:
                            P.wait("T", kdone[idx + 6 if idx < 2 else idx])
                        half = banks6[idx % 6]
                        hh, ho = banks6m[idx % 6]
                        mm = []
                        for k in range(8):
                            mm.append(P.op("T", lambda e, s, tc=tc, oc2=oc2, k=k, half=half, slot2=slot2: e.matmul(
                                half,
                                xnT[k][:, slot2 * 512 + tc * 128: slot2 * 512 + (tc + 1) * 128],
                                wv[k][:, oc2 * 512:(oc2 + 1) * 512],
                                start=(k == 0), stop=False)))
                        mm.append(P.op("T", lambda e, s, oc2=oc2, half=half: e.matmul(
                            half, ones[0:1, 0:128], bvs[0:1, oc2 * 512:(oc2 + 1) * 512],
                            start=False, stop=True)))
                        P.waits("V", mm[-1:])
                        base = slot12 * 1088 + oc2 * 8 * 68
                        vprev.append(P.op("V", lambda e, s, base=base, kc=kc, hh=hh, ho=ho: e.tensor_scalar(
                            bass.AP(vS, base, [[12 * 1088, 128], [68, 8], [1, 64]]),
                            bass.AP(hh, ho, [[1536, 128], [64, 8], [1, 64]]),
                            lamb[:, kc:kc + 1], None, ALU.mult)))
                    P.op("V", lambda e, s, slot12=slot12, kc=kc: e.tensor_scalar(
                        bass.AP(vS, slot12 * 1088 + 64, [[12 * 1088, 128], [68, 16], [1, 1]]),
                        bass.AP(ones, 0, [[128, 128], [1, 16], [1, 1]]),
                        lamb[:, kc:kc + 1], None, ALU.mult))
                P.waits("G", kdone + vprev[-1:])
                sqd = []
                for k in range(8):
                    sqd.append(P.op("G", lambda e, s, k=k, slot3=slot3: e.tensor_mul(
                        sqb[k][:, :], kt[k][:, slot3 * 512:slot3 * 512 + 512],
                        kt[k][:, slot3 * 512:slot3 * 512 + 512])))
                P.waits("T", sqd + vprev)
                nm = []
                for k in range(8):
                    nm.append(P.op("T", lambda e, s, k=k: e.matmul(
                        psC[0:16, 0:512], Jsl[:, 15 - 2 * k:31 - 2 * k], sqb[k][:, :],
                        start=(k == 0), stop=(k == 7))))
                P.waits("V", nm[-1:])
                P.op("V", lambda e, s, lw=lw: e.reduce_max(mkr[:, lw:lw + 1], psC[0:16, 0:512], axis=AX.X))
                prod_done[lw] = P.snap()
                for eng in ("T", "V", "S", "G"):
                    P.barrier_from(eng, prod_done[lw])

            def attend(w):
                lw = w + 1
                slot2 = lw % 2
                qdone = []
                banks6 = [psA[:, 0:512], psA[:, 512:1024], psA[:, 1024:1536],
                          psB[:, 0:512], psB[:, 512:1024], psB[:, 1024:1536]]
                for oc in range(8):
                    if oc >= 6:
                        P.wait("T", qdone[oc - 6])
                    half = banks6[oc % 6]
                    mm = []
                    for k in range(8):
                        mm.append(P.op("T", lambda e, s, oc=oc, k=k, half=half, slot2=slot2: e.matmul(
                            half, wq[k][:, oc * 128:(oc + 1) * 128],
                            xnT[k][:, slot2 * 512:slot2 * 512 + 512],
                            start=(k == 0), stop=(k == 7))))
                    P.waits("V", mm[-1:])
                    qdone.append(P.op("V", lambda e, s, oc=oc, half=half: e.tensor_scalar(
                        qt[oc][:, :], half, bqs[:, oc:oc + 1], None, ALU.add)))
                P.waits("G", qdone)
                sqd = []
                for k in range(8):
                    sqd.append(P.op("G", lambda e, s, k=k: e.tensor_mul(sqb[k][:, :], qt[k][:, :], qt[k][:, :])))
                P.waits("T", sqd + qdone)
                nm = []
                for k in range(8):
                    nm.append(P.op("T", lambda e, s, k=k: e.matmul(
                        psC[0:16, 0:512], Jsl[:, 15 - 2 * k:31 - 2 * k], sqb[k][:, :],
                        start=(k == 0), stop=(k == 7))))
                P.waits("V", nm[-1:])
                vq = P.op("V", lambda e, s: e.reduce_max(sl(0), psC[0:16, 0:512], axis=AX.X))
                sg = []
                for k in range(8):
                    sg.append(P.op("T", lambda e, s, k=k: e.matmul(
                        psB[0:16, 0:512], Jsl[:, 15 - 2 * k:31 - 2 * k], qt[k][:, :],
                        start=(k == 0), stop=(k == 7))))
                P.waits("V", sg[-1:])
                sgv = P.op("V", lambda e, s: e.tensor_copy(scrA[0:16, :], psB[0:16, 0:512]))
                P.wait("V", vq)
                P.op("V", lambda e, s, lw=lw: e.tensor_max(sl(1), mkr[:, lw - 1:lw], mkr[:, lw:lw + 1]))
                VG()
                P.op("V", lambda e, s, lw=lw: e.tensor_max(sl(2), sl(1), mkr[:, lw + 1:lw + 2]))
                VG()
                P.op("V", lambda e, s: e.tensor_scalar_max(sl(3), sl(2), 64.0))
                VG()
                vb = P.op("V", lambda e, s: e.tensor_mul(sl(4), sl(0), sl(3)))
                P.wait("S", vb)
                sb = P.op("S", lambda e, s: e.sqrt(sl(5), sl(4)))
                P.wait("V", sb)
                P.op("V", lambda e, s: e.reciprocal(sl(6), sl(5)))
                VG()
                P.op("V", lambda e, s: e.tensor_scalar_mul(sl(7), sl(6), 40.0))
                VG()
                P.op("V", lambda e, s: e.tensor_scalar_min(sl(8), sl(7), 1.0))
                VG()
                P.op("V", lambda e, s: e.tensor_copy(scrB[0:16, 0:1], sl(8)))
                VG()
                P.op("V", lambda e, s: e.tensor_copy(sl(9), scrB[0:16, 0:1]))
                VG()
                P.op("V", lambda e, s: e.reciprocal(sl(10), sl(9)))
                P.op("V", lambda e, s: e.tensor_scalar_mul(sl(22), sl(9), -1.0))
                P.op("V", lambda e, s: e.tensor_mul(sl(11), sl(5), sl(9)))
                VG()
                P.op("V", lambda e, s: e.tensor_scalar_mul(sl(11), sl(11), -1.0))
                P.op("V", lambda e, s: e.tensor_scalar(sl(12), sl(5), -1.0, LN512, ALU.mult, ALU.add))
                VG()
                P.op("V", lambda e, s: e.tensor_mul(sl(12), sl(12), sl(9)))
                VG()
                r11 = P.op("V", lambda e, s: e.tensor_copy(scrB[0:16, 1:2], sl(11)))
                P.waits("T", [r11])
                t1 = P.op("T", lambda e, s: e.matmul(psT[0:1, 0:16], scrB[0:16, 0:1], ident[0:16, 0:16],
                                                     start=True, stop=True, is_transpose=True))
                P.waits("V", [t1])
                u1 = P.op("V", lambda e, s: e.tensor_copy(nmrow[0:1, 0:16], psT[0:1, 0:16]))
                P.waits("T", [u1])
                t3 = P.op("T", lambda e, s: e.matmul(psC[:, 0:16], ones[0:1, 0:128], nmrow[0:1, 0:16],
                                                     start=True, stop=True))
                P.waits("V", [t3])
                u3 = P.op("V", lambda e, s: e.tensor_copy(bcB[:, :], psC[:, 0:16]))
                P.waits("T", [u3])
                t2 = P.op("T", lambda e, s: e.matmul(psT[0:1, 0:16], scrB[0:16, 1:2], ident[0:16, 0:16],
                                                     start=True, stop=True, is_transpose=True))
                P.waits("V", [t2])
                u2 = P.op("V", lambda e, s: e.tensor_copy(nmrow[0:1, 0:16], psT[0:1, 0:16]))
                P.waits("T", [u2])
                t4 = P.op("T", lambda e, s: e.matmul(psC[:, 0:16], ones[0:1, 0:128], nmrow[0:1, 0:16],
                                                     start=True, stop=True))
                P.waits("V", [t4])
                u4 = P.op("V", lambda e, s: e.tensor_copy(bcC[:, :], psC[:, 0:16]))
                snap0 = P.snap()
                for eng in ("T", "S", "V"):
                    P.barrier_from(eng, snap0)

                pe = P.op("S", lambda e, s: e.activation(
                    padp[:, :], scrA[0:16, 0:512], AF.Exp, bias=sl(12), scale=sl(22),
                    accum_out=sl(13)))

                lastex = {0: None, 1: None}
                pg = 0
                for h in range(H):
                    kh = h // 2
                    ro = (h % 2) * 64
                    for g, (j0, j1) in enumerate(GROUPS):
                        ps = psA if pg == 0 else psB
                        if lastex[pg] is not None:
                            P.wait("T", lastex[pg])
                        mm = []
                        for j in range(j0, j1):
                            kc = w * 4 + j
                            col = ((kc // 4) % 3) * 512 + (kc % 4) * 128
                            mm.append(P.op("T", lambda e, s, kh=kh, ro=ro, col=col, j=j, j0=j0, ps=ps: e.matmul(
                                ps[:, (j - j0) * 512:(j - j0 + 1) * 512],
                                kt[kh][ro:ro + 64, col:col + 128],
                                qt[kh][ro:ro + 64, :],
                                start=True, stop=True)))
                        P.waits("S", mm)
                        lastex[pg] = P.op("S", lambda e, s, ps=ps, h=h, g=g: e.activation(
                            r1scr[:, :], ps[:, :], AF.Exp, bias=bcC[:, h:h + 1], scale=bcB[:, h:h + 1],
                            accum_out=rs[:, 4 * h + g:4 * h + g + 1]))
                        pg ^= 1
                snap1 = P.snap()
                for eng in ("T", "V"):
                    P.barrier_from(eng, snap1)

                rc = P.op("V", lambda e, s: e.tensor_copy(rsb[:, :], rs[:, :]))
                P.waits("T", [rc])
                smm = []
                for h in range(H):
                    smm.append(P.op("T", lambda e, s, h=h: e.matmul(
                        psC[0:1, 4 * h:4 * h + 4], ones[:, 0:1], rsb[:, 4 * h:4 * h + 4],
                        start=(h == 0), stop=(h == 15))))
                P.waits("V", smm[-1:])
                P.op("V", lambda e, s: e.tensor_copy(Sg[0:1, :], psC[0:1, 0:64]))
                VG()
                P.op("V", lambda e, s: e.reduce_sum(
                    Sg16f[0:1, 0:16], bass.AP(Sg, 0, [[64, 1], [4, 16], [1, 4]]), axis=AX.X))
                VG()
                w2_ = P.op("V", lambda e, s: e.tensor_copy(Sg16[0:1, 0:16], Sg16f[0:1, 0:16]))
                P.waits("T", [w2_])
                w3_ = P.op("T", lambda e, s: e.matmul(psC[0:16, 64:65], Sg16[0:1, 0:16], ones[0:1, 0:1],
                                                      start=True, stop=True))
                P.waits("V", [w3_, pe])
                P.op("V", lambda e, s: e.tensor_copy(sl(14), psC[0:16, 64:65]))
                VG()
                w5_ = P.op("V", lambda e, s: e.tensor_add(sl(15), sl(14), sl(13)))
                P.wait("S", w5_)
                w6_ = P.op("S", lambda e, s: e.activation(sl(16), sl(15), AF.Ln, bias=0.0, scale=1.0))
                P.wait("V", w6_)
                P.op("V", lambda e, s: e.tensor_mul(sl(17), sl(16), sl(10)))
                VG()
                P.op("V", lambda e, s: e.tensor_add(sl(18), sl(17), sl(5)))
                VG()
                P.op("V", lambda e, s: e.tensor_scalar(sl(19), sl(18), -1.0, C2, ALU.mult, ALU.add))
                VG()
                P.op("V", lambda e, s: e.tensor_scalar_min(sl(19), sl(19), 80.0))
                VG()
                wa_ = P.op("V", lambda e, s: e.tensor_copy(scrB[0:16, 2:3], sl(19)))
                VG()
                P.op("V", lambda e, s: e.tensor_copy(sl(20), scrB[0:16, 2:3]))
                VG()
                P.op("V", lambda e, s: e.tensor_scalar_add(sl(21), sl(20), LN512))
                VG()
                wc_ = P.op("V", lambda e, s, w=w: e.tensor_add(sl(23), sl(21), lnmuS[0:16, w:w + 1]))
                P.waits("T", [wa_])
                y1_ = P.op("T", lambda e, s: e.matmul(psT[0:1, 0:16], scrB[0:16, 2:3], ident[0:16, 0:16],
                                                      start=True, stop=True, is_transpose=True))
                P.waits("V", [y1_])
                y2_ = P.op("V", lambda e, s: e.tensor_copy(nmrow[0:1, 0:16], psT[0:1, 0:16]))
                P.waits("T", [y2_])
                y3_ = P.op("T", lambda e, s: e.matmul(psC[:, 0:16], ones[0:1, 0:128], nmrow[0:1, 0:16],
                                                      start=True, stop=True))
                P.waits("V", [y3_])
                P.op("V", lambda e, s: e.tensor_copy(bcA[:, :], psC[:, 0:16]))
                P.waits("S", [wc_])
                P.op("S", lambda e, s: e.activation(padp[:, :], scrA[0:16, 0:512], AF.Exp,
                                                    bias=sl(23), scale=-1.0))
                snap2 = P.snap()
                for eng in ("T", "S", "V"):
                    P.barrier_from(eng, snap2)

                lastex = {0: None, 1: None}
                pg = 0
                prev_n5 = None
                for h in range(H):
                    kh = h // 2
                    ro = (h % 2) * 64
                    groupinfo = []
                    exps = []
                    for g, (j0, j1) in enumerate(GROUPS):
                        ps = psA if pg == 0 else psB
                        if lastex[pg] is not None:
                            P.wait("T", lastex[pg])
                        for j in range(j0, j1):
                            kc = w * 4 + j
                            col = ((kc // 4) % 3) * 512 + (kc % 4) * 128
                            if j <= 3:
                                qlo, qhi = 0, (j + 1) * 128
                            elif j >= 8:
                                qlo, qhi = (j - 8) * 128, 512
                            else:
                                qlo, qhi = 0, 512
                            P.op("T", lambda e, s, kh=kh, ro=ro, col=col, j=j, j0=j0, ps=ps,
                                 qlo=qlo, qhi=qhi: e.matmul(
                                ps[:, (j - j0) * 512 + qlo:(j - j0) * 512 + qhi],
                                kt[kh][ro:ro + 64, col:col + 128],
                                qt[kh][ro:ro + 64, qlo:qhi],
                                start=True, stop=True))
                        cntT = ("T", P.cnt["T"])
                        P.wait("S", cntT)
                        pb = probs[g]
                        ex = P.op("S", lambda e, s, ps=ps, h=h, pb=pb: e.activation(
                            pb[:, :], ps[:, :], AF.Exp, bias=bcA[:, h:h + 1], scale=1.0))
                        exps.append(ex)
                        lastex[pg] = ex
                        groupinfo.append((g, j0, j1, pb))
                        pg ^= 1
                    P.waits("V", exps[-1:])
                    mk = []
                    for (g, j0, j1, pb) in groupinfo:
                        for j in range(j0, j1):
                            loc = (j - j0) * 512
                            if j <= 3:
                                b0 = j * 128
                                mk.append(P.op("V", lambda e, s, pb=pb, loc=loc, b0=b0: e.tensor_mul(
                                    pb[:, loc + b0:loc + b0 + 128], pb[:, loc + b0:loc + b0 + 128], tlow[:, :])))
                            elif j >= 8:
                                b0 = (j - 8) * 128
                                mk.append(P.op("V", lambda e, s, pb=pb, loc=loc, b0=b0: e.tensor_mul(
                                    pb[:, loc + b0:loc + b0 + 128], pb[:, loc + b0:loc + b0 + 128], thig[:, :])))
                    P.waits("T", mk[-1:] + exps[-1:])
                    if prev_n5 is not None:
                        P.wait("T", prev_n5)
                    first = True
                    for (g, j0, j1, pb) in groupinfo:
                        for j in range(j0, j1):
                            kc = w * 4 + j
                            slot12 = kc % 12
                            loc = (j - j0) * 512
                            if j <= 3:
                                qlo, qhi = 0, (j + 1) * 128
                            elif j >= 8:
                                qlo, qhi = (j - 8) * 128, 512
                            else:
                                qlo, qhi = 0, 512
                            P.op("T", lambda e, s, slot12=slot12, h=h, pb=pb, loc=loc,
                                 qlo=qlo, qhi=qhi, first=first: e.matmul(
                                psC[0:65, qlo:qhi],
                                vS[:, slot12 * 1088 + h * 68:slot12 * 1088 + h * 68 + 65],
                                pb[:, loc + qlo:loc + qhi],
                                start=first, stop=False))
                            first = False
                    pvlast = P.op("T", lambda e, s, h=h, w=w: e.matmul(
                        psC[0:65, 0:512],
                        pads[:, h * 65:h * 65 + 65],
                        padp[:, :],
                        start=False, stop=True))
                    P.waits("V", [pvlast])
                    n1 = P.op("V", lambda e, s: e.tensor_copy(scrA[64:65, :], psC[64:65, 0:512]))
                    VG()
                    P.op("V", lambda e, s: e.reciprocal(scrA[96:97, :], scrA[64:65, :]))
                    VG()
                    n2 = P.op("V", lambda e, s: e.tensor_copy(scrB[64:65, 0:512], scrA[96:97, :]))
                    psn = psA if pg == 0 else psB
                    P.waits("T", [n2])
                    n3 = P.op("T", lambda e, s, psn=psn: e.matmul(
                        psn[0:64, 0:512], ones[64:65, 0:64], scrB[64:65, 0:512], start=True, stop=True))
                    P.waits("V", [n3])
                    n4 = P.op("V", lambda e, s, psn=psn: e.tensor_copy(rbsb[:, :], psn[0:64, 0:512]))
                    VG()
                    n5 = P.op("V", lambda e, s, kh=kh, ro=ro: e.scalar_tensor_tensor(
                        atw[kh][ro:ro + 64, :], psC[0:64, 0:512], 1.0, rbsb[:, :], ALU.mult, ALU.mult))
                    prev_n5 = ("V", P.cnt["V"])
                    lastex[pg] = n4
                snap3 = P.snap()
                P.barrier_from("Y", snap3)
                for k in range(8):
                    P.dma("Y", lambda e, s, k=k, w=w: e.dma_start(
                        out=attnTs[k * 128:(k + 1) * 128, w * 512:(w + 1) * 512], in_=atw[k][:, :]))
                stage_done[w] = P.snap()
                for eng in ("T", "V", "S", "G"):
                    P.barrier_from(eng, stage_done[w])

            produce(0)
            produce(1)
            produce(2)
            attend(0)
            produce(3)
            attend(1)
            produce(4)
            attend(2)
            produce(5)
            attend(3)

            fP = P.snap()

            def endy(e, s, fP=fP):
                e.wait_ge(s["dY"], fP["dY"])
            P.ops.append(("Y", endy))
            P.qstream["Y"].append("w")

            def endg(e, s, fP=fP):
                e.wait_ge(s["dG"], fP["dG"])
            P.ops.append(("G", endg))
            P.qstream["G"].append("w")

            P.emit(blk0, sems)

        P2 = Plan(P.cnt)
        P2.spacers = P.spacers
        es2 = contextlib.ExitStack()
        with es2:
            CC2 = es2.enter_context
            W1_ = CC2(nc.sbuf_tensor("W1", [128, 8 * HFF], BF16))
            W2_ = CC2(nc.sbuf_tensor("W2", [128, 32 * E], BF16))
            WO_ = CC2(nc.sbuf_tensor("WO", [128, 8 * E], BF16))
            b1s = CC2(nc.sbuf_tensor("b1s", [128, 32], F32))
            bsum = CC2(nc.sbuf_tensor("bsum", [1, E], BF16))
            hT_ = CC2(nc.sbuf_tensor("hT", [128, 32 * SPAN], BF16))
            xnsp_ = CC2(nc.sbuf_tensor("xnsp", [128, 8 * SPAN], BF16))
            atsp_ = CC2(nc.sbuf_tensor("atsp", [128, 8 * SPAN], BF16))
            xsp_ = CC2(nc.sbuf_tensor("xsp", [128, 2 * E], F32))
            ost_ = CC2(nc.sbuf_tensor("ost", [128, 2 * 512], F32))
            aT4 = CC2(nc.sbuf_tensor("aT4", [4, SPAN], BF16))
            aTr = CC2(nc.sbuf_tensor("aTr", [1, 4 * SPAN], BF16))
            cS = CC2(nc.sbuf_tensor("cS", [16, SPAN], BF16))
            blk1 = CC2(nc.Block())
            W1 = [W1_[:, k * HFF:(k + 1) * HFF] for k in range(8)]
            W2 = [W2_[:, k * E:(k + 1) * E] for k in range(32)]
            WO = [WO_[:, k * E:(k + 1) * E] for k in range(8)]
            hT = [hT_[:, k * SPAN:(k + 1) * SPAN] for k in range(32)]
            xnsp = [xnsp_[:, k * SPAN:(k + 1) * SPAN] for k in range(8)]
            atsp = [atsp_[:, k * SPAN:(k + 1) * SPAN] for k in range(8)]
            xsp = [xsp_[:, i * E:(i + 1) * E] for i in range(2)]
            bosr = xsp_[0:1, 0:E]
            b2sr = xsp_[0:1, E:2 * E]
            ost = [ost_[:, i * 512:(i + 1) * 512] for i in range(2)]

            pb0 = P2.snap()
            for eng in ("T", "V", "S", "G", "Y"):
                P2.barrier_from(eng, pb0)

            for k in range(8):
                P2.dma("G", lambda e, s, k=k: e.dma_start(out=W1[k], in_=w1T[k * 128:(k + 1) * 128, :]))
            for k in range(32):
                P2.dma("G", lambda e, s, k=k: e.dma_start(out=W2[k], in_=w2T[k * 128:(k + 1) * 128, :]))
            for k in range(8):
                P2.dma("G", lambda e, s, k=k: e.dma_start(out=WO[k], in_=woT[k * 128:(k + 1) * 128, :]))
            l1 = P2.dma("Y", lambda e, s: e.dma_start(out=b1s[:, :], in_=b1c))
            l2 = P2.dma("Y", lambda e, s: e.dma_start(out=bosr, in_=bor))
            l3 = P2.dma("Y", lambda e, s: e.dma_start(out=b2sr, in_=b2r))
            P2.waits("V", [l2, l3])
            P2.op("V", lambda e, s: e.tensor_add(bsum[0:1, :], bosr[0:1, :], b2sr[0:1, :]))
            wsnap = P2.snap()
            for eng in ("T", "S", "V", "Y"):
                P2.barrier_from(eng, wsnap)

            for sp in range(NSPAN):
                sb2 = 0
                lds = []
                for k in range(8):
                    lds.append(P2.dma("Y", lambda e, s, k=k, sp=sp, sb2=sb2: e.dma_start(
                        out=xnsp[k][:, :],
                        in_=xnTs[k * 128:(k + 1) * 128, sp * SPAN:(sp + 1) * SPAN])))
                    lds.append(P2.dma("Y", lambda e, s, k=k, sp=sp, sb2=sb2: e.dma_start(
                        out=atsp[k][:, :],
                        in_=attnTs[k * 128:(k + 1) * 128, sp * SPAN:(sp + 1) * SPAN])))
                for tt in range(2):
                    lds.append(P2.dma("Y", lambda e, s, tt=tt, sp=sp, sb2=sb2: e.dma_start(
                        out=xsp[tt][:, :],
                        in_=xh[512 + sp * SPAN + tt * 128: 512 + sp * SPAN + (tt + 1) * 128, :])))
                P2.waits("T", lds)
                am = []
                for k in range(8):
                    am.append(P2.op("T", lambda e, s, k=k, sb2=sb2: e.matmul(
                        psC[0:4, 0:SPAN], eoT[k][:, 1:8:2], xnsp[k][:, :],
                        start=(k == 0), stop=(k == 7))))
                P2.waits("S", am[-1:])
                ae = P2.op("S", lambda e, s: e.activation(aT4[:, :], psC[0:4, 0:SPAN], AF.Identity,
                                                          bias=0.0, scale=sclb[0:4, 0:1]))
                P2.wait("Y", ae)
                acp = []
                for i in range(4):
                    acp.append(P2.dma("Y", lambda e, s, i=i: e.dma_start(
                        out=aTr[0:1, i * SPAN:(i + 1) * SPAN], in_=aT4[i:i + 1, :])))
                P2.waits("T", acp)
                hdone = []
                hbanks = [psA[:, 0:SPAN], psA[:, 512:512 + SPAN], psA[:, 1024:1024 + SPAN],
                          psB[:, 0:SPAN], psB[:, 512:512 + SPAN]]
                for hc in range(32):
                    ph = hbanks[hc % 5]
                    if hc >= 5 and hc % 5 == 0:
                        P2.wait("T", hdone[hc - 1])
                    mm = []
                    for k in range(8):
                        mm.append(P2.op("T", lambda e, s, hc=hc, k=k, ph=ph: e.matmul(
                            ph, W1[k][:, hc * 128:(hc + 1) * 128],
                            xnsp[k][:, :],
                            start=(k == 0), stop=False)))
                    i = hc // 8
                    mm.append(P2.op("T", lambda e, s, hc=hc, i=i, ph=ph: e.matmul(
                        ph, EVN[0:1, i * E + (hc % 8) * 128: i * E + (hc % 8 + 1) * 128],
                        aTr[0:1, i * SPAN:(i + 1) * SPAN],
                        start=False, stop=True)))
                    P2.waits("S", mm[-1:])
                    hdone.append(P2.op("S", lambda e, s, hc=hc, ph=ph: e.activation(
                        hT[hc][:, :], ph, AF.Gelu, bias=b1s[:, hc:hc + 1], scale=1.0)))
                P2.waits("T", hdone[-1:])
                cm = []
                for hk in range(32):
                    cm.append(P2.op("T", lambda e, s, hk=hk: e.matmul(
                        psC[0:16, 0:SPAN], ODP[:, hk * 16:(hk + 1) * 16], hT[hk][:, :],
                        start=(hk == 0), stop=(hk == 31))))
                P2.waits("S", cm[-1:])
                ce = P2.op("S", lambda e, s: e.activation(cS[:, :], psC[0:16, 0:SPAN], AF.Identity,
                                                          bias=0.0, scale=sclb[0:16, 0:1]))
                P2.wait("T", ce)
                vohist = []
                for tt in range(2):
                    for oc2 in range(2):
                        po = psB[:, 1024:1536] if (tt * 2 + oc2) % 2 == 0 else psC[:, 0:512]
                        ob = ost[(tt * 2 + oc2) % 2]
                        if len(vohist) >= 2:
                            P2.wait("T", vohist[-2])
                        mm = []
                        for hk in range(32):
                            mm.append(P2.op("T", lambda e, s, hk=hk, tt=tt, oc2=oc2, po=po: e.matmul(
                                po, hT[hk][:, tt * 128:(tt + 1) * 128],
                                W2[hk][:, oc2 * 512:(oc2 + 1) * 512],
                                start=(hk == 0), stop=False)))
                        for k in range(8):
                            mm.append(P2.op("T", lambda e, s, k=k, tt=tt, oc2=oc2, po=po, sb2=sb2: e.matmul(
                                po, atsp[k][:, tt * 128:(tt + 1) * 128],
                                WO[k][:, oc2 * 512:(oc2 + 1) * 512],
                                start=False, stop=False)))
                        mm.append(P2.op("T", lambda e, s, tt=tt, oc2=oc2, po=po: e.matmul(
                            po, cS[:, tt * 128:(tt + 1) * 128],
                            E2s[:, oc2 * 512:(oc2 + 1) * 512],
                            start=False, stop=False)))
                        mm.append(P2.op("T", lambda e, s, oc2=oc2, po=po: e.matmul(
                            po, ones[0:1, 0:128], bsum[0:1, oc2 * 512:(oc2 + 1) * 512],
                            start=False, stop=True)))
                        P2.waits("V", mm[-1:])
                        vo = P2.op("V", lambda e, s, tt=tt, oc2=oc2, po=po, ob=ob, sb2=sb2: e.scalar_tensor_tensor(
                            ob[:, :], po, 1.0,
                            xsp[tt][:, oc2 * 512:(oc2 + 1) * 512],
                            ALU.mult, ALU.add))
                        P2.wait("Y", vo)
                        st = P2.dma("Y", lambda e, s, tt=tt, oc2=oc2, sp=sp, ob=ob: e.dma_start(
                            out=out[sp * SPAN + tt * 128: sp * SPAN + (tt + 1) * 128,
                                    oc2 * 512:(oc2 + 1) * 512],
                            in_=ob[:, :]))
                        vohist.append(vo)
                        P2.wait("V", st)
                spsnap = P2.snap()
                for eng in ("T", "V", "S"):
                    P2.barrier_from(eng, spsnap)

            final = P2.snap()

            def fin(e, s, final=final):
                e.wait_ge(s["dY"], final["dY"])
            P2.ops.append(("Y", fin))

            P2.emit(blk1, sems)

    return nc


def kernel(L0_attn_x, L0_attn_wq, L0_attn_bq, L0_attn_wk, L0_attn_bk,
           L0_attn_wv, L0_attn_bv, L0_attn_wo, L0_attn_bo,
           L0_ffn_W1, L0_ffn_b1, L0_ffn_W2, L0_ffn_b2, L0_ffn_scale):
    x = np.ascontiguousarray(np.asarray(L0_attn_x, dtype=np.float32))
    wq = np.asarray(L0_attn_wq, np.float32)
    wk = np.asarray(L0_attn_wk, np.float32)
    wv = np.asarray(L0_attn_wv, np.float32)
    wo = np.asarray(L0_attn_wo, np.float32)
    W1 = np.asarray(L0_ffn_W1, np.float32)
    W2 = np.asarray(L0_ffn_W2, np.float32)
    bq = np.asarray(L0_attn_bq, np.float32)
    bk = np.asarray(L0_attn_bk, np.float32)
    bv = np.asarray(L0_attn_bv, np.float32)
    bo = np.asarray(L0_attn_bo, np.float32)
    b1 = np.asarray(L0_ffn_b1, np.float32)
    b2 = np.asarray(L0_ffn_b2, np.float32)
    scale = np.float32(np.asarray(L0_ffn_scale))

    shared = {
        "wqT": np.ascontiguousarray(wq.T), "wkT": np.ascontiguousarray(wk.T),
        "wvT": np.ascontiguousarray(wv.T), "woT": np.ascontiguousarray(wo.T),
        "w1T": np.ascontiguousarray(W1.T), "w2T": np.ascontiguousarray(W2.T),
        "bqc": np.ascontiguousarray(bq.reshape(8, 128).T),
        "bkc": np.ascontiguousarray(bk.reshape(8, 128).T),
        "bvr": bv.reshape(1, E).copy(), "bor": bo.reshape(1, E).copy(),
        "b2r": b2.reshape(1, E).copy(),
        "b1c": np.ascontiguousarray(b1.reshape(32, 128).T),
        "sclc": np.broadcast_to(scale, (16, 1)).copy(),
    }
    in_maps = []
    for core in range(8):
        b, wi = core // 4, core % 4
        t0 = wi * TOWN
        xhc = np.zeros((TLOC, E), np.float32)
        if wi > 0:
            xhc[0:512] = x[b, t0 - 512:t0]
        xhc[512:512 + TOWN] = x[b, t0:t0 + TOWN]
        if wi < 3:
            xhc[512 + TOWN:TLOC] = x[b, t0 + TOWN:t0 + TOWN + 512]
        lam = np.ones(NCH, np.float32)
        if wi == 0:
            lam[0:4] = 0
        if wi == 3:
            lam[20:24] = 0
        mu = np.zeros(NOWN, np.float32)
        if wi == 0:
            mu[0] = 1
        if wi == 3:
            mu[3] = 1
        psel = np.zeros((16, 16 * 65), np.float32)
        for h in range(H):
            psel[h, h * 65:h * 65 + 64] = -1.0
            psel[h, h * 65 + 64] = 1.0
        lnm = np.full((16, NOWN), -1e30, np.float32)
        for w in range(NOWN):
            if mu[w] > 0:
                lnm[:, w] = 0.0
        in_maps.append(dict(shared,
                            xh=xhc,
                            x16=np.ascontiguousarray(x[b, 1:17]),
                            lamt=np.ascontiguousarray(np.broadcast_to(lam, (128, NCH))),
                            padsel=psel, lnmu=lnm))

    nc = build_program()
    res = run_bass_kernel_spmd(nc, in_maps, core_ids=list(range(8)))
    outp = np.zeros((2, 8192, E), np.float32)
    for core in range(8):
        b, wi = core // 4, core % 4
        outp[b, wi * TOWN:(wi + 1) * TOWN] = res.results[core]["out"]
    return outp
